# version 12
# baseline (speedup 1.0000x reference)
import numpy as np
from contextlib import ExitStack

import concourse.bass as bass
import concourse.mybir as mybir
from concourse.bass_core_rust import IndirectOffsetOnAxis
from concourse.bass_utils import run_bass_kernel_spmd

F32 = mybir.dt.float32
BF16 = mybir.dt.bfloat16
U32 = mybir.dt.uint32
AF = mybir.ActivationFunctionType
ALU = mybir.AluOpType
AX = mybir.AxisListType

D = 1024
T_OWN = 2048
NT = T_OWN // 128
T_PRE = 6144
NPT = T_PRE // 128
CAP = 256
NE = 16
FF = 2048
H_ROWS = NE * CAP + T_OWN
Y_ROWS = NE * CAP + 1
ZROW = NE * CAP
LN_EPS = 1e-5

PBF_N = 29696
PF_N = 7168


class Sem:
    def __init__(self, h):
        self.h = h
        self.n = 0
    def inc(self, k):
        self.n += k
        return self.n


class Prog:
    def __init__(self):
        self.ops = []
        self.strict = {}

    def emit(self, engine, fn, waits=(), inc=None, relaxed=False):
        strict = self.strict.get(engine)
        w = list(waits)
        if strict is not None and not relaxed:
            w.append((strict, strict.n))
            if inc is None:
                inc = (strict, 1)
        w = tuple(w)
        if inc is not None:
            inc[0].inc(inc[1])
        def run(eng, w=w, fn=fn, inc=inc):
            for s, v in w:
                eng.wait_ge(s.h, v)
            ins = fn(eng)
            if inc is not None:
                ins.then_inc(inc[0].h, inc[1])
            return ins
        self.ops.append((engine, run))

    def wait(self, engine, sem, val=None):
        v = sem.n if val is None else val
        self.ops.append((engine, lambda eng, s=sem, v=v: eng.wait_ge(s.h, v)))

    def replay(self, engine, eng):
        for e, fn in self.ops:
            if e == engine:
                fn(eng)


def build_program(debug=0):
    nc = bass.Bass("TRN2", target_bir_lowering=False, debug=False)

    def di(name, shape, dt=F32):
        return nc.dram_tensor(name, shape, dt, kind="ExternalInput").ap()

    x_own = di("x_own", [T_OWN, D])
    x_pre = di("x_pre", [T_PRE, D])
    wk_t = di("wk_t", [D, D])
    wv_t = di("wv_t", [D, D])
    wo_t = di("wo_t", [D, D])
    wg_t = di("wg_t", [D, FF])
    wu_t = di("wu_t", [D, FF])
    wd_t = di("wd_t", [FF, D])
    wr_t = di("wr_t", [D, NE])
    gup = di("gup", [NE, D, 2 * FF])
    dwn = di("dwn", [NE, FF, D])
    bk_r = di("bk_r", [1, D])
    bv_r = di("bv_r", [1, D])
    bo_r = di("bo_r", [1, D])
    lnb_fm = di("lnb_fm", [128, 8])
    lng_fm = di("lng_fm", [128, 8])
    ident = di("ident", [128, 128])
    u_incl = di("u_incl", [128, 128])
    u_strict = di("u_strict", [128, 128])
    ones_col = di("ones_col", [1, 128])
    ones128 = di("ones128", [128, 1])
    iota_big = di("iota_big", [128, NE])
    iota_pln = di("iota_pln", [128, NE])
    rsq_pos = di("rsq_pos", [128, NT])
    mask_pre = di("mask_pre", [128, NPT])
    garb = di("garb", [128, NT])

    out = nc.dram_tensor("out", [T_OWN, D], F32, kind="ExternalOutput").ap()

    dbg = {}
    if debug:
        def do(name, shape, dt=F32):
            dbg[name] = nc.dram_tensor(name, shape, dt, kind="ExternalOutput").ap()
        do("d_xnfm", [128, 8 * T_OWN], BF16)
        do("d_logits", [128, NE * NT])
        do("d_slots", [128, NT])
        do("d_sval", [128, NT])
        do("d_carry16", [1, NE])
        do("d_carryD", [1, D])
        do("d_carryE", [1, D])
        do("d_hdram", [H_ROWS, D], BF16)
        do("d_ydram", [Y_ROWS, D])
        do("d_K", [128, D])
        do("d_kn", [128, D])
        do("d_bound", [128, D])
        do("d_mem", [128, D])
        do("d_retr", [128, D])
        do("d_accC", [128, NT * D])
        do("d_accE", [128, NT * D])
        do("d_rowsC", [128, D])
        do("d_row", [1, D])
        do("d_hfm", [128, 2048], BF16)
        do("d_fact", [128, 1024], BF16)
        do("d_gate", [128, 256])

    h_dram = nc.dram_tensor("h_dram", [H_ROWS, D], BF16, kind="Internal").ap()
    y_dram = nc.dram_tensor("y_dram", [Y_ROWS, D], F32, kind="Internal").ap()

    es = ExitStack()
    sb = lambda name, shape, dt=F32: es.enter_context(nc.sbuf_tensor(name, shape, dt))
    ps = lambda name, shape, dt=F32: es.enter_context(nc.psum_tensor(name, shape, dt))

    pr = Prog()
    with es:
        s_x = Sem(es.enter_context(nc.semaphore("s_x")))
        s_g = Sem(es.enter_context(nc.semaphore("s_g")))
        s_st = Sem(es.enter_context(nc.semaphore("s_st")))
        c_v = Sem(es.enter_context(nc.semaphore("c_v")))
        c_s = Sem(es.enter_context(nc.semaphore("c_s")))
        c_t = Sem(es.enter_context(nc.semaphore("c_t")))
        c_p = Sem(es.enter_context(nc.semaphore("c_p")))
        pr.strict["vector"] = c_v
        pr.strict["scalar"] = c_s
        pr.strict["gpsimd"] = c_p

        acc = sb("acc", [128, NT * D])
        xn_fm = sb("xn_fm", [128, 8 * T_OWN], BF16)
        logit_all = sb("logit_all", [128, NE * NT])
        sval = sb("sval", [128, NT])
        slotg = sb("slotg", [128, NT], U32)
        carry16 = sb("carry16", [1, NE])
        rowsA = sb("rowsA", [128, D])
        rowsB = sb("rowsB", [128, D])
        carry = rowsA[0:1, :]
        attn_row = rowsB[0:1, :]
        k_ident = sb("k_ident", [128, 128])
        k_ident_bf = sb("k_ident_bf", [128, 128], BF16)
        k_ui = sb("k_ui", [128, 128])
        k_us = sb("k_us", [128, 128])
        k_onesc3 = sb("k_onesc3", [128, 128])
        k_onesb3 = sb("k_onesb3", [128, 128], BF16)
        rowsC = sb("rowsC", [128, D], BF16)
        k_onesc = k_onesc3[0:1, :]
        k_ones128 = sb("k_ones128", [128, 1])
        k_iob = sb("k_iob", [128, NE])
        k_iop = sb("k_iop", [128, NE])
        k_rsq = sb("k_rsq", [128, NT])
        k_mask = sb("k_mask", [128, NPT])
        k_garb = sb("k_garb", [128, NT])
        k_lnb = sb("k_lnb", [128, 8])
        k_lng = sb("k_lng", [128, 8])
        wr_sb = sb("wr_sb", [128, 8 * NE])
        pbf = sb("pbf_pool", [128, PBF_N], BF16)
        pf = sb("pf_pool", [128, PF_N])
        def bfs(off, n):
            return pbf[:, off:off + n]
        def f32s(off, n):
            return pf[:, off:off + n]
        s1c = sb("s1c", [128, 1]); nmc = sb("nmc", [128, 1])
        vsc = sb("vsc", [128, 1]); sdc = sb("sdc", [128, 1]); rrc = sb("rrc", [128, 1])
        mxc = sb("mxc", [128, 1]); idxc = sb("idxc", [128, 1])
        seqc = sb("seqc", [128, NE]); t16c = sb("t16c", [128, NE]); ohc = sb("ohc", [128, NE])
        poshc = sb("poshc", [128, NE]); ptc = sb("ptc", [128, 1]); gec = sb("gec", [128, 1])
        slf = sb("slf", [128, 1]); slsf = sb("slsf", [128, 1]); slgf = sb("slgf", [128, 1])
        slz = sb("slz", [128, 1]); s_u32 = sb("s_u32", [128, 1], U32)
        qv0 = sb("qv0", [128, 256]); qv1 = sb("qv1", [128, 256])
        qp0 = sb("qp0", [128, 256]); qp1 = sb("qp1", [128, 256])
        ssq, snr, src_ = qv0, qv1, qp0

        pb = [ps(f"pb{k}", [128, 512], F32) for k in range(7)]
        pbank_bf = es.enter_context(nc.psum_tensor("pbank_bf", [128, 1024], BF16))

        def ld(dst, src):
            pr.emit("sync", lambda e, d=dst, s=src: e.dma_start(out=d, in_=s), inc=(s_x, 16))
        for dst, src in [(k_ident[:, :], ident), (k_ui[:, :], u_incl), (k_us[:, :], u_strict),
                         (k_onesc3[0:1, :], ones_col), (k_onesc3[32:33, :], ones_col),
                         (k_onesc3[64:65, :], ones_col), (k_ones128[:, :], ones128),
                         (k_iob[:, :], iota_big), (k_iop[:, :], iota_pln), (k_rsq[:, :], rsq_pos),
                         (k_mask[:, :], mask_pre), (k_garb[:, :], garb),
                         (rowsA[32:33, :], bk_r), (rowsA[64:65, :], bv_r), (rowsB[32:33, :], bo_r),
                         (k_lnb[:, :], lnb_fm), (k_lng[:, :], lng_fm)]:
            ld(dst, src)
        ld(wr_sb[:, :].rearrange("p (c e) -> p c e", c=8), wr_t.rearrange("(c p) e -> p c e", p=128))
        pr.emit("gpsimd", lambda e: e.dma_start(out=k_ident_bf[:, :], in_=ident), inc=(s_g, 16))
        pr.emit("gpsimd", lambda e: e.dma_start(out=k_onesb3[0:1, :], in_=ones_col), inc=(s_g, 16))
        pr.emit("gpsimd", lambda e: e.dma_start(out=k_onesb3[32:33, :], in_=ones_col), inc=(s_g, 16))
        pr.emit("gpsimd", lambda e: e.dma_start(out=k_onesb3[64:65, :], in_=ones_col), inc=(s_g, 16))
        pr.emit("gpsimd", lambda e: e.dma_start(out=rowsC[32:33, :], in_=bk_r), inc=(s_g, 16))
        pr.emit("gpsimd", lambda e: e.dma_start(out=rowsC[64:65, :], in_=bv_r), inc=(s_g, 16))
        const_x = s_x.n
        const_g = s_g.n

        pr.emit("vector", lambda e: e.memset(rowsA[0:1, :], 0.0))
        pr.emit("vector", lambda e: e.memset(carry16[:, :], 0.0))
        pr.emit("vector", lambda e: e.memset(rowsB[64:65, :], 0.0))
        v_zero = c_v.n
        pr.emit("sync", lambda e: e.dma_start(out=y_dram[ZROW:ZROW + 1, :], in_=rowsB[64:65, :]),
                waits=[(c_v, v_zero)], inc=(s_st, 16))

        FO_XS0, FO_XS1, FO_XN32, FO_LNSCR, FO_FM32 = 0, 1024, 2048, 3072, 4096
        FO_KS, FO_VS, FO_KN, FO_BD, FO_MS = 4096, 5120, 6144, 1024, 5120
        FO_XP0, FO_XP1, FO_XNP = 0, 1024, 2048
        FO_RT, FO_ZSC, FO_Z = 1024, 0, 2048
        FO_SG = 3072
        FO_YSB = 4096
        FO_YTOK, FO_OT = 6144, 2048
        BO_XNTOK, BO_HS = 0, NT * D
        BO_WG0, BO_WG1, BO_WU0, BO_WU1 = 0, 4096, 8192, 12288
        BO_WD0, BO_WD1, BO_ACT = 16384, 20480, 24576
        BO_WK, BO_WV, BO_WOG = 0, 8192, 16384
        BO_XNPFM, BO_ZB, BO_ZFM = 24576, 24576, 24576
        BO_GUP0, BO_GUP1, BO_DWN0, BO_DWN1 = 0, 8192, 16384, 20480
        BO_HTOK, BO_HFM, BO_FACT = 24576, 26624, 28672

        def emit_ln(xin, xout, xw):
            pr.emit("vector", lambda e: e.reduce_sum(s1c[:, :], xin, axis=AX.X),
                    waits=([(s_x, xw)] if xw else []))
            pr.emit("vector", lambda e: e.tensor_scalar(nmc[:, :], s1c[:, :], -1.0 / D, None, op0=ALU.mult))
            nm_done = c_v.n
            pr.emit("scalar", lambda e: e.activation(f32s(FO_LNSCR, D), xin, AF.Square,
                                                     bias=nmc[:, :], scale=1.0,
                                                     accum_out=vsc[:, :]),
                    waits=[(c_v, nm_done)])
            vs_done = c_s.n
            pr.emit("vector", lambda e: e.tensor_scalar(vsc[:, :], vsc[:, :], 1.0 / D, LN_EPS,
                                                        op0=ALU.mult, op1=ALU.add),
                    waits=[(c_s, vs_done)])
            ve_done = c_v.n
            pr.emit("scalar", lambda e: e.activation(sdc[:, :], vsc[:, :], AF.Sqrt),
                    waits=[(c_v, ve_done)])
            sd_done = c_s.n
            pr.emit("vector", lambda e: e.reciprocal(rrc[:, :], sdc[:, :]),
                    waits=[(c_s, sd_done)])
            pr.emit("vector", lambda e: e.tensor_scalar(xout, xin, nmc[:, :], rrc[:, :],
                                                        op0=ALU.add, op1=ALU.mult))
            return c_v.n

        for i in range(NT):
            xst = f32s(FO_XS0 if i % 2 == 0 else FO_XS1, D)
            if i >= 2:
                pr.wait("sync", c_v)
                pr.wait("sync", c_s)
            pr.emit("sync", lambda e, d=xst, s=x_own[i * 128:(i + 1) * 128, :]:
                    e.dma_start(out=d, in_=s), inc=(s_x, 16))
            xw = s_x.n
            emit_ln(xst, f32s(FO_XN32, D), xw)
            pr.emit("vector", lambda e, i=i, xst=xst: e.tensor_tensor(
                        acc[:, i * D:(i + 1) * D], f32s(FO_XN32, D), xst, ALU.add))
            pr.emit("vector", lambda e, i=i: e.tensor_copy(
                        bfs(BO_XNTOK + i * D, D), f32s(FO_XN32, D)))
            xn_done = c_v.n
            pr.wait("tensor", c_v, xn_done)
            def tp_all(e, i=i):
                for j in range(8):
                    e.transpose(pbank_bf[:, j * 128:(j + 1) * 128],
                                bfs(BO_XNTOK + i * D + j * 128, 128),
                                k_ident_bf[:, :])
                for j in range(8):
                    ins = e.transpose(pb[4 + j // 4][:, (j % 4) * 128:(j % 4 + 1) * 128],
                                      f32s(FO_XN32 + j * 128, 128), k_ident[:, :])
                return ins
            pr.emit("tensor", tp_all, waits=[(s_g, const_g)], inc=(c_t, 1))
            tp_done = c_t.n
            pr.emit("vector", lambda e, i=i: e.tensor_copy(
                        xn_fm[:, :].rearrange("p (c t) -> p c t", c=8)[:, :, i * 128:(i + 1) * 128],
                        pbank_bf[:, :].rearrange("p (c t) -> p c t", c=8)[:, :, :]),
                    waits=[(c_t, tp_done)])
            pr.emit("vector", lambda e: e.tensor_copy(f32s(FO_FM32, 512), pb[4][:, :]))
            pr.emit("vector", lambda e: e.tensor_copy(f32s(FO_FM32 + 512, 512), pb[5][:, :]))
            ev_done = c_v.n
            def router(e, i=i):
                for j in range(8):
                    ins = e.matmul(pb[6][:, 0:NE], f32s(FO_FM32 + j * 128, 128),
                                   wr_sb[:, j * NE:(j + 1) * NE],
                                   start=(j == 0), stop=(j == 7))
                return ins
            pr.emit("tensor", router, waits=[(c_v, ev_done), (s_x, const_x)], inc=(c_t, 1))
            lg_done = c_t.n
            pr.emit("vector", lambda e, i=i: e.tensor_copy(
                        logit_all[:, i * NE:(i + 1) * NE], pb[6][:, 0:NE]),
                    waits=[(c_t, lg_done)])

        pr.wait("vector", s_x, const_x)
        for i in range(NT):
            lg = logit_all[:, i * NE:(i + 1) * NE]
            pr.emit("vector", lambda e, lg=lg: e.reduce_max(mxc[:, :], lg, axis=AX.X))
            pr.emit("vector", lambda e, lg=lg: e.tensor_single_scalar(seqc[:, :], lg, mxc[:, :], ALU.is_equal))
            pr.emit("vector", lambda e: e.scalar_tensor_tensor(t16c[:, :], seqc[:, :], -1e4,
                                                               k_iob[:, :], op0=ALU.mult, op1=ALU.add))
            pr.emit("vector", lambda e: e.tensor_reduce(idxc[:, :], t16c[:, :], axis=AX.X, op=ALU.min))
            pr.emit("vector", lambda e: e.tensor_single_scalar(ohc[:, :], k_iop[:, :], idxc[:, :], ALU.is_equal))
            oh_done = c_v.n
            mx_done = c_v.n
            pr.emit("scalar", lambda e, i=i: e.activation(sval[:, i:i + 1], mxc[:, :], AF.Sigmoid),
                    waits=[(c_v, mx_done)])
            s_done = c_s.n
            def posmm(e, i=i):
                e.matmul(pb[0][:, 0:NE], k_us[:, :], ohc[:, :], start=True, stop=False)
                return e.matmul(pb[0][:, 0:NE], k_onesc3[0:1, :], carry16[:, :], start=False, stop=True)
            pr.emit("tensor", posmm, waits=[(c_v, oh_done)], inc=(c_t, 1))
            pos_done = c_t.n
            def c16mm(e, i=i):
                return e.matmul(pb[1][0:1, 0:NE], k_ones128[:, :], ohc[:, :], start=True, stop=True)
            pr.emit("tensor", c16mm, inc=(c_t, 1))
            c16_done = c_t.n
            pr.emit("vector", lambda e: e.tensor_add(carry16[:, :], carry16[:, :], pb[1][0:1, 0:NE]),
                    waits=[(c_t, c16_done)])
            pr.emit("vector", lambda e: e.tensor_tensor(poshc[:, :], pb[0][:, 0:NE], ohc[:, :], ALU.mult),
                    waits=[(c_t, pos_done)])
            pr.emit("vector", lambda e: e.reduce_sum(ptc[:, :], poshc[:, :], axis=AX.X))
            pr.emit("vector", lambda e: e.scalar_tensor_tensor(
                        slf[:, :], idxc[:, :], float(CAP), ptc[:, :], op0=ALU.mult, op1=ALU.add))
            pr.emit("vector", lambda e: e.tensor_scalar(gec[:, :], ptc[:, :], float(CAP), None, op0=ALU.is_ge))
            pr.emit("vector", lambda e, i=i: e.tensor_sub(slz[:, :], k_garb[:, i:i + 1], slf[:, :]))
            pr.emit("vector", lambda e: e.tensor_mul(slz[:, :], slz[:, :], gec[:, :]))
            pr.emit("vector", lambda e: e.tensor_add(slsf[:, :], slf[:, :], slz[:, :]))
            pr.emit("vector", lambda e: e.tensor_scalar(slz[:, :], slf[:, :], -1.0, float(ZROW),
                                                        op0=ALU.mult, op1=ALU.add))
            pr.emit("vector", lambda e: e.tensor_mul(slz[:, :], slz[:, :], gec[:, :]))
            pr.emit("vector", lambda e: e.tensor_add(slgf[:, :], slf[:, :], slz[:, :]))
            pr.emit("vector", lambda e, i=i: e.tensor_copy(slotg[:, i:i + 1], slgf[:, :]))
            pr.emit("vector", lambda e: e.tensor_copy(s_u32[:, :], slsf[:, :]))
            pr.emit("vector", lambda e, i=i: e.tensor_single_scalar(
                        bfs(BO_HS, D), bfs(BO_XNTOK + i * D, D), sval[:, i:i + 1], ALU.mult),
                    waits=[(c_s, s_done)])
            hs_done = c_v.n
            pr.emit("gpsimd", lambda e: e.indirect_dma_start(
                        out=h_dram, out_offset=IndirectOffsetOnAxis(s_u32[:, 0:1], 0),
                        in_=bfs(BO_HS, D), in_offset=None),
                    waits=[(c_v, hs_done)], inc=(s_g, 16))
            pr.wait("vector", s_g)
        b_done_v = c_v.n

        wg_off = [BO_WG0, BO_WG1]
        wu_off = [BO_WU0, BO_WU1]
        wd_off = [BO_WD0, BO_WD1]
        grp_done_t = {}
        for g in range(4):
            bi = g % 2
            if bi in grp_done_t:
                pr.wait("gpsimd", c_t, grp_done_t[bi])
            pr.emit("gpsimd", lambda e, g=g, bi=bi: e.dma_start(
                        out=bfs(wg_off[bi], 4096).rearrange("p (c f) -> p c f", c=8),
                        in_=wg_t[:, g * 512:(g + 1) * 512].rearrange("(c p) f -> p c f", p=128)),
                    inc=(s_g, 16))
            pr.emit("gpsimd", lambda e, g=g, bi=bi: e.dma_start(
                        out=bfs(wu_off[bi], 4096).rearrange("p (c f) -> p c f", c=8),
                        in_=wu_t[:, g * 512:(g + 1) * 512].rearrange("(c p) f -> p c f", p=128)),
                    inc=(s_g, 16))
            pr.emit("gpsimd", lambda e, g=g, bi=bi: e.dma_start(
                        out=bfs(wd_off[bi], 4096).rearrange("p (c h) -> p c h", c=4),
                        in_=wd_t[g * 512:(g + 1) * 512, :].rearrange("(c p) h -> p c h", p=128)),
                    inc=(s_g, 16))
            gw = s_g.n
            for tb in range(8):
                for ft in range(4):
                    def gatemm(e, bi=bi, tb=tb, ft=ft):
                        for j in range(8):
                            ins = e.matmul(pb[4][:, 0:256],
                                           bfs(wg_off[bi] + j * 512 + ft * 128, 128),
                                           xn_fm[:, j * T_OWN + tb * 256: j * T_OWN + (tb + 1) * 256],
                                           start=(j == 0), stop=(j == 7))
                        return ins
                    w = [(s_g, gw)]
                    if g == 0 and tb == 0 and ft == 0:
                        w.append((c_v, b_done_v))
                    pr.emit("tensor", gatemm, waits=w, inc=(c_t, 1))
                    gmm = c_t.n
                    def upmm(e, bi=bi, tb=tb, ft=ft):
                        for j in range(8):
                            ins = e.matmul(pb[5][:, 0:256],
                                           bfs(wu_off[bi] + j * 512 + ft * 128, 128),
                                           xn_fm[:, j * T_OWN + tb * 256: j * T_OWN + (tb + 1) * 256],
                                           start=(j == 0), stop=(j == 7))
                        return ins
                    pr.emit("tensor", upmm, inc=(c_t, 1))
                    umm = c_t.n
                    pr.emit("scalar", lambda e: e.activation(f32s(FO_SG, 256), pb[4][:, 0:256], AF.Silu),
                            waits=[(c_t, gmm)])
                    sgd = c_s.n
                    pr.emit("vector", lambda e, ft=ft: e.tensor_tensor(
                                bfs(BO_ACT + ft * 256, 256), f32s(FO_SG, 256), pb[5][:, 0:256], ALU.mult),
                            waits=[(c_t, umm), (c_s, sgd)])
                    act_done = c_v.n
                    def downmm(e, bi=bi, ft=ft):
                        for tt in range(2):
                            for nh in range(2):
                                ins = e.matmul(pb[tt * 2 + nh][:, :],
                                               bfs(BO_ACT + ft * 256 + tt * 128, 128),
                                               bfs(wd_off[bi] + ft * 1024 + nh * 512, 512),
                                               start=(ft == 0), stop=(ft == 3))
                        return ins
                    pr.emit("tensor", downmm, waits=[(c_v, act_done)], inc=(c_t, 1))
                dn = c_t.n
                grp_done_t[bi] = dn
                for tt in range(2):
                    ti = tb * 2 + tt
                    for nh in range(2):
                        pr.emit("vector", lambda e, ti=ti, tt=tt, nh=nh: e.tensor_add(
                                    acc[:, ti * D + nh * 512: ti * D + (nh + 1) * 512],
                                    acc[:, ti * D + nh * 512: ti * D + (nh + 1) * 512],
                                    pb[tt * 2 + nh][:, :]),
                                waits=[(c_t, dn)])
                yadd = c_v.n
                pr.wait("tensor", c_v, yadd)

        if debug:
            pr.wait("sync", c_v, c_v.n)
            pr.emit("sync", lambda e: e.dma_start(out=dbg["d_accC"], in_=acc[:, :]), inc=(s_st, 16))

        pr.wait("gpsimd", c_t, c_t.n)
        pr.emit("gpsimd", lambda e: e.dma_start(
                    out=bfs(BO_WK, 8192).rearrange("p (c n) -> p c n", c=8),
                    in_=wk_t.rearrange("(c p) n -> p c n", p=128)), inc=(s_g, 16))
        pr.emit("gpsimd", lambda e: e.dma_start(
                    out=bfs(BO_WV, 8192).rearrange("p (c n) -> p c n", c=8),
                    in_=wv_t.rearrange("(c p) n -> p c n", p=128)), inc=(s_g, 16))
        kv_w = s_g.n
        pr.wait("sync", c_v, c_v.n)
        for j in range(8):
            xo = FO_XP0 if j % 2 == 0 else FO_XP1
            if j >= 2:
                pr.wait("sync", c_t)
            pr.emit("sync", lambda e, j=j, xo=xo: e.dma_start(
                        out=f32s(xo, D), in_=wo_t[j * 128:(j + 1) * 128, :]), inc=(s_x, 16))
            ww = s_x.n
            pr.emit("vector", lambda e, j=j, xo=xo: e.tensor_single_scalar(
                        bfs(BO_WOG + j * 1024, D), f32s(xo, D), k_lng[:, j:j + 1], ALU.mult),
                    waits=[(s_x, ww)])
            wsc = c_v.n
            def rowmm2(e, j=j, xo=xo):
                e.matmul(pb[6][0:1, :], k_lnb[:, j:j + 1], f32s(xo, 512),
                         start=(j == 0), stop=(j == 7))
                return e.matmul(pb[2][0:1, :], k_lnb[:, j:j + 1], f32s(xo + 512, 512),
                                start=(j == 0), stop=(j == 7))
            pr.emit("tensor", rowmm2, waits=[(c_v, wsc)], inc=(c_t, 1))
        row_done = c_t.n
        pr.emit("vector", lambda e: e.tensor_add(rowsB[0:1, 0:512], rowsB[32:33, 0:512], pb[6][0:1, :]),
                waits=[(c_t, row_done)])
        pr.emit("vector", lambda e: e.tensor_add(rowsB[0:1, 512:1024], rowsB[32:33, 512:1024], pb[2][0:1, :]))
        pr.emit("vector", lambda e: e.tensor_copy(rowsC[0:1, :], rowsB[0:1, :]))
        if debug:
            pr.wait("sync", c_v, c_v.n)
            pr.emit("sync", lambda e: e.dma_start(out=dbg["d_row"], in_=rowsB[0:1, :]), inc=(s_st, 16))

        QTAB = {
            0: [(1, 0, 0), (-1, 1, 1), (-1, 2, 2), (-1, 3, 3)],
            1: [(1, 0, 1), (1, 1, 0), (1, 2, 3), (-1, 3, 2)],
            2: [(1, 0, 2), (-1, 1, 3), (1, 2, 0), (1, 3, 1)],
            3: [(1, 0, 3), (1, 1, 2), (-1, 2, 1), (1, 3, 0)],
        }

        def quat_emit(out_off, a_off, b_off, conj_b, pool_comps, waits_v=(), waits_p=()):
            wv = list(waits_v)
            wp = list(waits_p)
            comps = sorted(range(4), key=lambda c: c in pool_comps)
            last_of = {("gpsimd" if c in pool_comps else "vector"): c for c in comps}
            for comp in range(4):
                eng = "gpsimd" if comp in pool_comps else "vector"
                t0, t1 = (qp0, qp1) if eng == "gpsimd" else (qv0, qv1)
                terms = QTAB[comp]
                if conj_b:
                    terms = [(s * (-1 if b != 0 else 1), a, b) for (s, a, b) in terms]
                terms = sorted(terms, key=lambda t: -t[0])
                ov = pf[:, out_off + comp:out_off + D:4]
                w = wp if eng == "gpsimd" else wv
                for n_, (s, a, b) in enumerate(terms):
                    av = pf[:, a_off + a:a_off + D:4]
                    bv = pf[:, b_off + b:b_off + D:4]
                    is_last = (last_of[eng] == comp and n_ == 3)
                    if n_ == 0:
                        pr.emit(eng, lambda e, av=av, bv=bv, t0=t0: e.tensor_mul(t0[:, :], av, bv),
                                waits=tuple(w), relaxed=True)
                        if eng == "gpsimd":
                            wp = []
                        else:
                            wv = []
                    else:
                        pr.emit(eng, lambda e, av=av, bv=bv, t1=t1: e.tensor_mul(t1[:, :], av, bv),
                                relaxed=True)
                        dst = ov if n_ == 3 else t0[:, :]
                        op = ALU.add if s > 0 else ALU.subtract
                        pr.emit(eng, lambda e, dst=dst, t0=t0, t1=t1, op=op:
                                e.tensor_tensor(dst, t0[:, :], t1[:, :], op),
                                relaxed=not is_last)
            return c_v.n, c_p.n

        FO_LN2 = 3072

        st_state = {"cons_v": 0, "cons_p": 0, "evac_s": c_s.n, "tot_add_v": 0,
                    "pbank_v": c_v.n, "dump_st": 0}

        def emit_kv_mm(fm_fn, extra_wait):
            def kmm(e):
                for j in range(8):
                    for nh in range(2):
                        e.matmul(pb[nh][:, :], fm_fn(j),
                                 bfs(BO_WK + j * 1024 + nh * 512, 512),
                                 start=(j == 0), stop=False)
                for nh in range(2):
                    ins = e.matmul(pb[nh][:, :], k_onesb3[32:33, :],
                                   rowsC[32:33, nh * 512:(nh + 1) * 512],
                                   start=False, stop=True)
                return ins
            pr.emit("tensor", kmm,
                    waits=[(s_g, kv_w), (c_s, st_state["evac_s"])] + list(extra_wait),
                    inc=(c_t, 1))
            def vmm(e):
                for j in range(8):
                    for nh in range(2):
                        e.matmul(pb[2 + nh][:, :], fm_fn(j),
                                 bfs(BO_WV + j * 1024 + nh * 512, 512),
                                 start=(j == 0), stop=False)
                for nh in range(2):
                    ins = e.matmul(pb[2 + nh][:, :], k_onesb3[64:65, :],
                                   rowsC[64:65, nh * 512:(nh + 1) * 512],
                                   start=False, stop=True)
                return ins
            pr.emit("tensor", vmm, inc=(c_t, 1))
            return c_t.n

        def emit_kv_evac(v_done):
            w = [(c_t, v_done), (c_v, st_state["cons_v"]), (c_p, st_state["cons_p"])]
            if debug and st_state["dump_st"]:
                w.append((s_st, st_state["dump_st"]))
                st_state["dump_st"] = 0
            pr.emit("scalar", lambda e: e.copy(f32s(FO_KS, 512), pb[0][:, :]), waits=w)
            pr.emit("scalar", lambda e: e.copy(f32s(FO_KS + 512, 512), pb[1][:, :]))
            pr.emit("scalar", lambda e: e.copy(f32s(FO_VS, 512), pb[2][:, :]))
            pr.emit("scalar", lambda e: e.copy(f32s(FO_VS + 512, 512), pb[3][:, :]))
            st_state["evac_s"] = c_s.n

        def emit_knorm_bound(parity):
            pr.emit("scalar", lambda e: e.activation(f32s(FO_LN2, D), f32s(FO_KS, D), AF.Square))
            sq_done = c_s.n
            sqv = lambda c: pf[:, FO_LN2 + c:FO_LN2 + D:4]
            pr.emit("vector", lambda e: e.tensor_tensor(ssq[:, :], sqv(0), sqv(1), ALU.add),
                    waits=[(c_s, sq_done)])
            pr.emit("vector", lambda e: e.tensor_tensor(snr[:, :], sqv(2), sqv(3), ALU.add))
            pr.emit("vector", lambda e: e.tensor_tensor(ssq[:, :], ssq[:, :], snr[:, :], ALU.add))
            ss_done = c_v.n
            pr.emit("scalar", lambda e: e.activation(snr[:, :], ssq[:, :], AF.Sqrt), waits=[(c_v, ss_done)])
            nrm_done = c_s.n
            pr.emit("vector", lambda e: e.reciprocal(src_[:, :], snr[:, :]), waits=[(c_s, nrm_done)])
            pr.emit("vector", lambda e: e.tensor_scalar(src_[:, :], src_[:, :], 1e12, None, op0=ALU.min))
            rcp_done = c_v.n
            for c in range(4):
                eng = "vector" if c < 2 else "gpsimd"
                w = [(c_v, rcp_done)] if eng == "gpsimd" else []
                pr.emit(eng, lambda e, c=c: e.tensor_tensor(
                            pf[:, FO_KN + c:FO_KN + D:4], pf[:, FO_KS + c:FO_KS + D:4],
                            src_[:, :], ALU.mult), waits=w)
            kn_v, kn_p = c_v.n, c_p.n
            pool_comps = (1, 2, 3) if parity == 0 else (2, 3)
            return quat_emit(FO_BD, FO_VS, FO_KN, False, pool_comps,
                             waits_v=[(c_v, kn_v), (c_p, kn_p)],
                             waits_p=[(c_v, kn_v), (c_p, kn_p)])

        def emit_tot_carry(lhs_col):
            for nh in range(2):
                def totmm(e, nh=nh, lhs_col=lhs_col):
                    return e.matmul(pb[6][0:1, :], lhs_col,
                                    f32s(FO_BD + nh * 512, 512), start=True, stop=True)
                w = [(c_v, st_state["cons_v"]), (c_p, st_state["cons_p"]),
                     (c_v, st_state["tot_add_v"])] if nh == 0 else [(c_v, c_v.n)]
                pr.emit("tensor", totmm, waits=w, inc=(c_t, 1))
                td = c_t.n
                pr.emit("vector", lambda e, nh=nh: e.tensor_add(
                            rowsA[0:1, nh * 512:(nh + 1) * 512],
                            rowsA[0:1, nh * 512:(nh + 1) * 512], pb[6][0:1, :]),
                        waits=[(c_t, td)])
            st_state["tot_add_v"] = c_v.n

        def emit_tail_D(j8):
            bd_v, bd_p = emit_knorm_bound(j8 & 1)
            st_state["cons_v"], st_state["cons_p"] = bd_v, bd_p
            emit_tot_carry(k_mask[:, j8:j8 + 1])

        def emit_tail_E(i):
            bd_v, bd_p = emit_knorm_bound(i & 1)
            st_state["cons_v"], st_state["cons_p"] = bd_v, bd_p
            def memmm(e, i=i):
                for nh in range(2):
                    e.matmul(pb[4 + nh][:, :], k_ui[:, :], f32s(FO_BD + nh * 512, 512),
                             start=True, stop=False)
                    ins = e.matmul(pb[4 + nh][:, :], k_onesc3[0:1, :],
                                   rowsA[0:1, nh * 512:(nh + 1) * 512],
                                   start=False, stop=True)
                return ins
            pr.emit("tensor", memmm, waits=[(c_v, bd_v), (c_p, bd_p)], inc=(c_t, 1))
            mem_done = c_t.n
            pr.emit("vector", lambda e, i=i: e.tensor_single_scalar(
                        f32s(FO_MS, 512), pb[4][:, :], k_rsq[:, i:i + 1], ALU.mult),
                    waits=[(c_t, mem_done)])
            pr.emit("vector", lambda e, i=i: e.tensor_single_scalar(
                        f32s(FO_MS + 512, 512), pb[5][:, :], k_rsq[:, i:i + 1], ALU.mult))
            ms_done = c_v.n
            emit_tot_carry(k_ones128[:, :])
            if debug and i == 0:
                pr.wait("sync", c_v, c_v.n)
                pr.wait("sync", c_p, c_p.n)
                pr.emit("sync", lambda e: e.dma_start(out=dbg["d_K"], in_=f32s(FO_KS, D)), inc=(s_st, 16))
                pr.emit("sync", lambda e: e.dma_start(out=dbg["d_kn"], in_=f32s(FO_KN, D)), inc=(s_st, 16))
                pr.emit("sync", lambda e: e.dma_start(out=dbg["d_bound"], in_=f32s(FO_BD, D)), inc=(s_st, 16))
                pr.emit("sync", lambda e: e.dma_start(out=dbg["d_mem"], in_=f32s(FO_MS, D)), inc=(s_st, 16))
                st_state["dump_st"] = s_st.n
            rt_v, rt_p = quat_emit(FO_ZSC, FO_MS, FO_KN, True, (1, 2, 3) if (i & 1) else (2, 3),
                                   waits_v=[(c_v, ms_done)], waits_p=[(c_v, ms_done)])
            pr.wait("vector", c_p, rt_p)
            pr.wait("vector", c_v, rt_v)
            if debug and i == 0:
                pr.wait("sync", c_v, c_v.n)
                pr.emit("sync", lambda e: e.dma_start(out=dbg["d_retr"], in_=f32s(FO_ZSC, D)), inc=(s_st, 16))
                st_state["dump_st"] = s_st.n
            emit_ln(f32s(FO_ZSC, D), bfs(BO_ZB, D), 0)
            zb_done = c_v.n
            pr.wait("tensor", c_v, zb_done)
            def tpz(e):
                for j in range(8):
                    ins = e.transpose(pbank_bf[:, j * 128:(j + 1) * 128],
                                      bfs(BO_ZB + j * 128, 128), k_ident_bf[:, :])
                return ins
            pr.emit("tensor", tpz, inc=(c_t, 1))
            tzd = c_t.n
            pr.emit("vector", lambda e: e.tensor_copy(bfs(BO_ZFM, D), pbank_bf[:, :]),
                    waits=[(c_t, tzd)])
            zfm_done = c_v.n
            st_state["pbank_v"] = zfm_done
            def womm(e, i=i):
                for j in range(8):
                    for nh in range(2):
                        e.matmul(pb[4 + nh][:, :], bfs(BO_ZFM + j * 128, 128),
                                 bfs(BO_WOG + j * 1024 + nh * 512, 512),
                                 start=(j == 0), stop=False)
                for nh in range(2):
                    ins = e.matmul(pb[4 + nh][:, :], k_onesb3[0:1, :],
                                   rowsC[0:1, nh * 512:(nh + 1) * 512],
                                   start=False, stop=True)
                return ins
            pr.emit("tensor", womm, waits=[(c_v, zfm_done)], inc=(c_t, 1))
            wo_done = c_t.n
            for nh in range(2):
                pr.emit("vector", lambda e, i=i, nh=nh: e.tensor_add(
                            acc[:, i * D + nh * 512: i * D + (nh + 1) * 512],
                            acc[:, i * D + nh * 512: i * D + (nh + 1) * 512],
                            pb[4 + nh][:, :]),
                        waits=[(c_t, wo_done)])

        ln_done_hist = {}
        pending = None
        for step in range(NPT + NT):
            if step < NPT:
                j8 = step
                xo = FO_XP0
                if j8 >= 1:
                    pr.wait("sync", c_v, ln_done_hist[j8 - 1])
                else:
                    pr.wait("sync", c_t, row_done)
                pr.emit("sync", lambda e, j8=j8, xo=xo: e.dma_start(
                            out=f32s(xo, D), in_=x_pre[j8 * 128:(j8 + 1) * 128, :]), inc=(s_x, 16))
                xw = s_x.n
                ln_done_hist[j8] = emit_ln(f32s(xo, D), bfs(BO_XNPFM, D), xw)
                xnb_done = c_v.n
                pr.wait("tensor", c_v, max(xnb_done, st_state["pbank_v"]))
                def tpp(e):
                    for j in range(8):
                        ins = e.transpose(pbank_bf[:, j * 128:(j + 1) * 128],
                                          bfs(BO_XNPFM + j * 128, 128), k_ident_bf[:, :])
                    return ins
                pr.emit("tensor", tpp, inc=(c_t, 1))
                tpd = c_t.n
                pr.emit("vector", lambda e: e.tensor_copy(bfs(BO_XNPFM, D), pbank_bf[:, :]),
                        waits=[(c_t, tpd)])
                fme = c_v.n
                st_state["pbank_v"] = fme
                v_done = emit_kv_mm(lambda j: bfs(BO_XNPFM + j * 128, 128), [(c_v, fme)])
            else:
                i = step - NPT
                v_done = emit_kv_mm(
                    lambda j, i=i: xn_fm[:, j * T_OWN + i * 128: j * T_OWN + (i + 1) * 128], [])
            if pending is not None:
                (emit_tail_D if pending[0] == "D" else emit_tail_E)(pending[1])
                if debug and step == NPT:
                    pr.wait("sync", c_v, c_v.n)
                    pr.emit("sync", lambda e: e.dma_start(out=dbg["d_carryD"], in_=rowsA[0:1, :]), inc=(s_st, 16))
                    st_state["dump_st"] = s_st.n
            emit_kv_evac(v_done)
            pending = ("D", j8) if step < NPT else ("E", step - NPT)
        (emit_tail_D if pending[0] == "D" else emit_tail_E)(pending[1])
        if debug:
            pr.wait("sync", c_v, c_v.n)
            pr.emit("sync", lambda e: e.dma_start(out=dbg["d_carryE"], in_=rowsA[0:1, :]), inc=(s_st, 16))
            pr.emit("sync", lambda e: e.dma_start(out=dbg["d_accE"], in_=acc[:, :]), inc=(s_st, 16))
            prc = sb("prc", [128, D])
            pr.emit("vector", lambda e: e.tensor_copy(prc[:, :], rowsC[:, :]))
            pr.wait("sync", c_v, c_v.n)
            pr.emit("sync", lambda e: e.dma_start(out=dbg["d_rowsC"], in_=prc[:, :]), inc=(s_st, 16))

        nonlocal_dn = [0, 0]
        dn_sg = [0, 0]
        dwn_off = [BO_DWN0, BO_DWN1]
        def emit_down(bi, g, ft, mult_v):
            def downmm(e, bi=bi, g=g, ft=ft):
                for tt in range(2):
                    for nh in range(2):
                        ins = e.matmul(pb[tt * 2 + nh][:, :],
                                       bfs(BO_FACT + ft * 256 + tt * 128, 128),
                                       bfs(dwn_off[bi] + ft * 1024 + nh * 512, 512),
                                       start=(g == 0 and ft == 0), stop=(g == 3 and ft == 3))
                return ins
            pr.emit("tensor", downmm, waits=[(c_v, mult_v), (s_g, dn_sg[bi])], inc=(c_t, 1))
            nonlocal_dn[bi] = c_t.n

        gup_off = [BO_GUP0, BO_GUP1]
        pr.wait("gpsimd", c_t, c_t.n)
        gu_done = {}
        ystore_st = 0
        hload_w = None
        for e_i in range(NE):
            if e_i == 0:
                pr.emit("sync", lambda e: e.dma_start(
                            out=bfs(BO_HTOK, 2048).rearrange("p (tt d) -> p tt d", tt=2),
                            in_=h_dram[0:CAP, :].rearrange("(tt p) d -> p tt d", p=128)),
                        waits=[(s_g, s_g.n), (c_t, c_t.n)], inc=(s_x, 16))
                hload_w = s_x.n
            hw = hload_w
            for tt in range(2):
                def tph(e, tt=tt):
                    for j in range(8):
                        ins = e.transpose(pbank_bf[:, j * 128:(j + 1) * 128],
                                          bfs(BO_HTOK + tt * 1024 + j * 128, 128), k_ident_bf[:, :])
                    return ins
                pr.emit("tensor", tph, waits=[(s_x, hw), (c_v, c_v.n)], inc=(c_t, 1))
                tphd = c_t.n
                pr.emit("vector", lambda e, tt=tt: e.tensor_copy(
                            bfs(BO_HFM, 2048).rearrange("p (c t) -> p c t", c=8)[:, :, tt * 128:(tt + 1) * 128],
                            pbank_bf[:, :].rearrange("p (c t) -> p c t", c=8)[:, :, :]),
                        waits=[(c_t, tphd)])
            hfm_done = c_v.n
            st_state["pbank_v"] = hfm_done
            if e_i + 1 < NE:
                pr.emit("sync", lambda e, e_i=e_i: e.dma_start(
                            out=bfs(BO_HTOK, 2048).rearrange("p (tt d) -> p tt d", tt=2),
                            in_=h_dram[(e_i + 1) * CAP:(e_i + 2) * CAP, :].rearrange("(tt p) d -> p tt d", p=128)),
                        waits=[(c_t, tphd)], inc=(s_x, 16))
                hload_w = s_x.n
            mult_hist = {}
            for g in range(4):
                bi = g % 2
                wgu = [(c_t, gu_done[bi])] if bi in gu_done else []
                pr.emit("gpsimd", lambda e, e_i=e_i, g=g, bi=bi: e.dma_start(
                            out=bfs(gup_off[bi], 8192).rearrange("p (c f) -> p c f", c=8)[:, :, 0:512],
                            in_=gup[e_i, :, g * 512:(g + 1) * 512].rearrange("(c p) f -> p c f", p=128)),
                        waits=wgu, inc=(s_g, 16))
                pr.emit("gpsimd", lambda e, e_i=e_i, g=g, bi=bi: e.dma_start(
                            out=bfs(gup_off[bi], 8192).rearrange("p (c f) -> p c f", c=8)[:, :, 512:1024],
                            in_=gup[e_i, :, FF + g * 512: FF + (g + 1) * 512].rearrange("(c p) f -> p c f", p=128)),
                        inc=(s_g, 16))
                pr.emit("gpsimd", lambda e, e_i=e_i, g=g, bi=bi: e.dma_start(
                            out=bfs(dwn_off[bi], 4096).rearrange("p (c h) -> p c h", c=4),
                            in_=dwn[e_i, g * 512:(g + 1) * 512, :].rearrange("(c p) h -> p c h", p=128)),
                        waits=[(c_t, nonlocal_dn[bi])], inc=(s_g, 16))
                dn_sg[bi] = s_g.n
                gw = s_g.n
                for ft in range(4):
                    gbank = pb[4] if ft % 2 == 0 else pb[6]
                    def gatemm(e, bi=bi, ft=ft, gbank=gbank):
                        for j in range(8):
                            ins = e.matmul(gbank[:, 0:256],
                                           bfs(gup_off[bi] + j * 1024 + ft * 128, 128),
                                           bfs(BO_HFM + j * 256, 256),
                                           start=(j == 0), stop=(j == 7))
                        return ins
                    w = [(s_g, gw)]
                    if g == 0 and ft == 0:
                        w.append((c_v, hfm_done))
                    pr.emit("tensor", gatemm, waits=w, inc=(c_t, 1))
                    gmm = c_t.n
                    wup = ([(c_v, mult_hist[ft - 1])] if ft > 0 else
                           ([(c_v, mult_hist[3])] if g > 0 else [(c_v, hfm_done)]))
                    def upmm(e, bi=bi, ft=ft):
                        for j in range(8):
                            ins = e.matmul(pb[5][:, 0:256],
                                           bfs(gup_off[bi] + j * 1024 + 512 + ft * 128, 128),
                                           bfs(BO_HFM + j * 256, 256),
                                           start=(j == 0), stop=(j == 7))
                        return ins
                    pr.emit("tensor", upmm, waits=wup, inc=(c_t, 1))
                    umm = c_t.n
                    if ft > 0:
                        emit_down(bi, g, ft - 1, mult_hist[ft - 1])
                    pr.emit("scalar", lambda e, gbank=gbank: e.activation(
                                f32s(FO_SG, 256), gbank[:, 0:256], AF.Silu),
                            waits=[(c_t, gmm)])
                    sgd = c_s.n
                    pr.emit("vector", lambda e, ft=ft: e.tensor_tensor(
                                bfs(BO_FACT + ft * 256, 256), f32s(FO_SG, 256), pb[5][:, 0:256], ALU.mult),
                            waits=[(c_t, umm), (c_s, sgd), (c_t, max(nonlocal_dn))])
                    mult_hist[ft] = c_v.n
                emit_down(bi, g, 3, mult_hist[3])
                gu_done[bi] = c_t.n
            ydn = max(nonlocal_dn)
            for tt in range(2):
                for nh in range(2):
                    w = [(c_t, ydn)]
                    if tt == 0 and nh == 0 and ystore_st:
                        w.append((s_st, ystore_st))
                    pr.emit("vector", lambda e, tt=tt, nh=nh: e.tensor_copy(
                                f32s(FO_YSB + tt * 1024 + nh * 512, 512), pb[tt * 2 + nh][:, :]),
                            waits=w)
            yev = c_v.n
            pr.emit("sync", lambda e, e_i=e_i: e.dma_start(
                        out=y_dram[e_i * CAP:(e_i + 1) * CAP, :].rearrange("(tt p) d -> p tt d", p=128),
                        in_=f32s(FO_YSB, 2048).rearrange("p (tt d) -> p tt d", tt=2)),
                    waits=[(c_v, yev)], inc=(s_st, 16))
            ystore_st = s_st.n
        if debug:
            pr.wait("sync", s_st, s_st.n)
            pr.emit("sync", lambda e: e.dma_start(out=dbg["d_ydram"], in_=y_dram), inc=(s_st, 16))

        y_all_st = s_st.n
        for i in range(NT):
            pr.emit("gpsimd", lambda e, i=i: e.indirect_dma_start(
                        out=f32s(FO_YTOK, D), out_offset=None,
                        in_=y_dram, in_offset=IndirectOffsetOnAxis(slotg[:, i:i + 1], 0)),
                    waits=[(s_st, y_all_st)], inc=(s_g, 16))
            gy = s_g.n
            pr.emit("vector", lambda e, i=i: e.tensor_add(
                        f32s(FO_OT, D), acc[:, i * D:(i + 1) * D], f32s(FO_YTOK, D)),
                    waits=[(s_g, gy)])
            ot_done = c_v.n
            pr.emit("sync", lambda e, i=i: e.dma_start(
                        out=out[i * 128:(i + 1) * 128, :], in_=f32s(FO_OT, D)),
                    waits=[(c_v, ot_done)], inc=(s_st, 16))
            pr.wait("gpsimd", s_st)
            pr.wait("vector", s_st)

        if debug:
            pr.wait("sync", c_v, c_v.n)
            pr.wait("sync", s_g, s_g.n)
            pr.emit("sync", lambda e: e.dma_start(out=dbg["d_xnfm"], in_=xn_fm[:, :]), inc=(s_st, 16))
            pr.emit("sync", lambda e: e.dma_start(out=dbg["d_logits"], in_=logit_all[:, :]), inc=(s_st, 16))
            pr.emit("sync", lambda e: e.dma_start(out=dbg["d_sval"], in_=sval[:, :]), inc=(s_st, 16))
            pr.emit("sync", lambda e: e.dma_start(out=dbg["d_carry16"], in_=carry16[:, :]), inc=(s_st, 16))
            pr.emit("sync", lambda e: e.dma_start(out=dbg["d_hdram"], in_=h_dram), inc=(s_st, 16))
            sg32 = sb("sg32", [128, NT])
            pr.emit("vector", lambda e: e.tensor_copy(sg32[:, :], slotg[:, :]))
            pr.wait("sync", c_v, c_v.n)
            pr.emit("sync", lambda e: e.dma_start(out=dbg["d_slots"], in_=sg32[:, :]), inc=(s_st, 16))

        pr.wait("sync", s_st, s_st.n)

        blk = es.enter_context(nc.Block())

        @blk.sync
        def _(eng):
            pr.replay("sync", eng)

        @blk.gpsimd
        def _(eng):
            pr.replay("gpsimd", eng)

        @blk.tensor
        def _(eng):
            pr.replay("tensor", eng)

        @blk.vector
        def _(eng):
            pr.replay("vector", eng)

        @blk.scalar
        def _(eng):
            pr.replay("scalar", eng)

    return nc



def stage_inputs(L0_attn_x, L0_attn_Wk, L0_attn_bk, L0_attn_Wv, L0_attn_bv,
                 L0_attn_ln_g, L0_attn_ln_b, L0_attn_Wo, L0_attn_bo,
                 L0_ffn_w_router, L0_ffn_gate_up_proj, L0_ffn_down_proj,
                 L0_ffn_w_gate, L0_ffn_w_up, L0_ffn_w_down):
    f32 = np.float32
    x = np.ascontiguousarray(np.asarray(L0_attn_x, dtype=f32).reshape(2, 8192, D))
    shared = {
        "wk_t": np.ascontiguousarray(np.asarray(L0_attn_Wk, f32).T),
        "wv_t": np.ascontiguousarray(np.asarray(L0_attn_Wv, f32).T),
        "wo_t": np.ascontiguousarray(np.asarray(L0_attn_Wo, f32).T),
        "wg_t": np.ascontiguousarray(np.asarray(L0_ffn_w_gate, f32).T),
        "wu_t": np.ascontiguousarray(np.asarray(L0_ffn_w_up, f32).T),
        "wd_t": np.ascontiguousarray(np.asarray(L0_ffn_w_down, f32).T),
        "wr_t": np.ascontiguousarray(np.asarray(L0_ffn_w_router, f32).T),
        "gup": np.ascontiguousarray(np.asarray(L0_ffn_gate_up_proj, f32)),
        "dwn": np.ascontiguousarray(np.asarray(L0_ffn_down_proj, f32)),
        "bk_r": np.asarray(L0_attn_bk, f32).reshape(1, D),
        "bv_r": np.asarray(L0_attn_bv, f32).reshape(1, D),
        "bo_r": np.asarray(L0_attn_bo, f32).reshape(1, D),
        "lnb_fm": np.ascontiguousarray(np.asarray(L0_attn_ln_b, f32).reshape(8, 128).T),
        "lng_fm": np.ascontiguousarray(np.asarray(L0_attn_ln_g, f32).reshape(8, 128).T),
        "ident": np.eye(128, dtype=f32),
        "u_incl": np.triu(np.ones((128, 128), f32)),
        "u_strict": np.triu(np.ones((128, 128), f32), 1),
        "ones_col": np.ones((1, 128), f32),
        "ones128": np.ones((128, 1), f32),
        "iota_big": np.tile(np.arange(NE, dtype=f32)[None, :] + 1e4, (128, 1)),
        "iota_pln": np.tile(np.arange(NE, dtype=f32)[None, :], (128, 1)),
    }
    in_maps = []
    for c in range(8):
        b, q = c // 4, c % 4
        own = np.ascontiguousarray(x[b, q * T_OWN:(q + 1) * T_OWN])
        pre = np.zeros((T_PRE, D), f32)
        pre[: q * T_OWN] = x[b, : q * T_OWN]
        mask = np.zeros((128, NPT), f32)
        mask[:, : q * NT] = 1.0
        gpos = (q * T_OWN + np.arange(T_OWN, dtype=np.float64) + 1.0)
        rsq = (1.0 / np.sqrt(gpos)).astype(f32).reshape(NT, 128).T
        garbv = (float(ZROW) + np.arange(T_OWN, dtype=f32)).reshape(NT, 128).T
        m = dict(shared)
        m.update({
            "x_own": own,
            "x_pre": pre,
            "mask_pre": np.ascontiguousarray(mask),
            "rsq_pos": np.ascontiguousarray(rsq),
            "garb": np.ascontiguousarray(garbv),
        })
        in_maps.append(m)
    return in_maps


_NC_CACHE = {}


def run(debug=0, **inputs):
    in_maps = stage_inputs(**inputs)
    if debug not in _NC_CACHE:
        _NC_CACHE[debug] = build_program(debug=debug)
    return run_bass_kernel_spmd(_NC_CACHE[debug], in_maps, core_ids=list(range(8)))


def kernel(**inputs):
    res = run(debug=0, **inputs)
    outs = [np.asarray(r["out"]) for r in res.results]
    full = np.empty((2, 8192, D), np.float32)
    for c in range(8):
        b, q = c // 4, c % 4
        full[b, q * T_OWN:(q + 1) * T_OWN] = outs[c]
    return full


# version 13
# speedup vs baseline: 1.1490x; 1.1490x over previous
import numpy as np
from contextlib import ExitStack

import concourse.bass as bass
import concourse.mybir as mybir
from concourse.bass_core_rust import IndirectOffsetOnAxis
from concourse.bass_utils import run_bass_kernel_spmd

F32 = mybir.dt.float32
BF16 = mybir.dt.bfloat16
U32 = mybir.dt.uint32
AF = mybir.ActivationFunctionType
ALU = mybir.AluOpType
AX = mybir.AxisListType

D = 1024
T_OWN = 2048
NT = T_OWN // 128
T_PRE = 6144
NPT = T_PRE // 128
CAP = 256
NE = 16
FF = 2048
H_ROWS = NE * CAP + T_OWN
Y_ROWS = NE * CAP + 1
ZROW = NE * CAP
LN_EPS = 1e-5

PBF_N = 29696
PF_N = 7168


class Sem:
    def __init__(self, h):
        self.h = h
        self.n = 0
    def inc(self, k):
        self.n += k
        return self.n


class Prog:
    def __init__(self):
        self.ops = []
        self.strict = {}

    def emit(self, engine, fn, waits=(), inc=None, relaxed=False):
        strict = self.strict.get(engine)
        w = list(waits)
        if strict is not None and not relaxed:
            w.append((strict, strict.n))
            if inc is None:
                inc = (strict, 1)
        w = tuple(w)
        if inc is not None:
            inc[0].inc(inc[1])
        def run(eng, w=w, fn=fn, inc=inc):
            for s, v in w:
                eng.wait_ge(s.h, v)
            ins = fn(eng)
            if inc is not None:
                ins.then_inc(inc[0].h, inc[1])
            return ins
        self.ops.append((engine, run))

    def wait(self, engine, sem, val=None):
        v = sem.n if val is None else val
        self.ops.append((engine, lambda eng, s=sem, v=v: eng.wait_ge(s.h, v)))

    def replay(self, engine, eng):
        for e, fn in self.ops:
            if e == engine:
                fn(eng)


def build_program(debug=0):
    nc = bass.Bass("TRN2", target_bir_lowering=False, debug=False)

    def di(name, shape, dt=F32):
        return nc.dram_tensor(name, shape, dt, kind="ExternalInput").ap()

    x_own = di("x_own", [T_OWN, D])
    x_pre = di("x_pre", [T_PRE, D])
    wk_t = di("wk_t", [D, D])
    wv_t = di("wv_t", [D, D])
    wo_t = di("wo_t", [D, D])
    wg_t = di("wg_t", [D, FF])
    wu_t = di("wu_t", [D, FF])
    wd_t = di("wd_t", [FF, D])
    wr_t = di("wr_t", [D, NE])
    gup = di("gup", [NE, D, 2 * FF])
    dwn = di("dwn", [NE, FF, D])
    bk_r = di("bk_r", [1, D])
    bv_r = di("bv_r", [1, D])
    bo_r = di("bo_r", [1, D])
    lnb_fm = di("lnb_fm", [128, 8])
    lng_fm = di("lng_fm", [128, 8])
    ident = di("ident", [128, 128])
    u_incl = di("u_incl", [128, 128])
    u_strict = di("u_strict", [128, 128])
    ones_col = di("ones_col", [1, 128])
    ones128 = di("ones128", [128, 1])
    iota_big = di("iota_big", [128, NE])
    iota_pln = di("iota_pln", [128, NE])
    rsq_pos = di("rsq_pos", [128, NT])
    mask_pre = di("mask_pre", [128, NPT])
    garb = di("garb", [128, NT])

    out = nc.dram_tensor("out", [T_OWN, D], F32, kind="ExternalOutput").ap()

    dbg = {}
    if debug:
        def do(name, shape, dt=F32):
            dbg[name] = nc.dram_tensor(name, shape, dt, kind="ExternalOutput").ap()
        do("d_xnfm", [128, 8 * T_OWN], BF16)
        do("d_logits", [128, NE * NT])
        do("d_slots", [128, NT])
        do("d_sval", [128, NT])
        do("d_carry16", [1, NE])
        do("d_carryD", [1, D])
        do("d_carryE", [1, D])
        do("d_hdram", [H_ROWS, D], BF16)
        do("d_ydram", [Y_ROWS, D])
        do("d_K", [128, D])
        do("d_kn", [128, D])
        do("d_bound", [128, D])
        do("d_mem", [128, D])
        do("d_retr", [128, D])
        do("d_accC", [128, NT * D])
        do("d_accE", [128, NT * D])
        do("d_rowsC", [128, D])
        do("d_row", [1, D])
        do("d_hfm", [128, 2048], BF16)
        do("d_fact", [128, 1024], BF16)
        do("d_gate", [128, 256])

    h_dram = nc.dram_tensor("h_dram", [H_ROWS, D], BF16, kind="Internal").ap()
    y_dram = nc.dram_tensor("y_dram", [Y_ROWS, D], F32, kind="Internal").ap()

    es = ExitStack()
    sb = lambda name, shape, dt=F32: es.enter_context(nc.sbuf_tensor(name, shape, dt))
    ps = lambda name, shape, dt=F32: es.enter_context(nc.psum_tensor(name, shape, dt))

    pr = Prog()
    with es:
        s_x = Sem(es.enter_context(nc.semaphore("s_x")))
        s_g = Sem(es.enter_context(nc.semaphore("s_g")))
        s_st = Sem(es.enter_context(nc.semaphore("s_st")))
        c_v = Sem(es.enter_context(nc.semaphore("c_v")))
        c_s = Sem(es.enter_context(nc.semaphore("c_s")))
        c_t = Sem(es.enter_context(nc.semaphore("c_t")))
        c_p = Sem(es.enter_context(nc.semaphore("c_p")))
        pr.strict["vector"] = c_v
        pr.strict["scalar"] = c_s
        pr.strict["gpsimd"] = c_p

        acc = sb("acc", [128, NT * D])
        xn_fm = sb("xn_fm", [128, 8 * T_OWN], BF16)
        logit_all = sb("logit_all", [128, NE * NT])
        sval = sb("sval", [128, NT])
        slotg = sb("slotg", [128, NT], U32)
        carry16 = sb("carry16", [1, NE])
        rowsA = sb("rowsA", [128, D])
        rowsB = sb("rowsB", [128, D])
        carry = rowsA[0:1, :]
        attn_row = rowsB[0:1, :]
        k_ident = sb("k_ident", [128, 128])
        k_ident_bf = sb("k_ident_bf", [128, 128], BF16)
        k_ui = sb("k_ui", [128, 128])
        k_us = sb("k_us", [128, 128])
        k_onesc3 = sb("k_onesc3", [128, 128])
        k_onesb3 = sb("k_onesb3", [128, 128], BF16)
        rowsC = sb("rowsC", [128, D], BF16)
        k_onesc = k_onesc3[0:1, :]
        k_ones128 = sb("k_ones128", [128, 1])
        k_iob = sb("k_iob", [128, NE])
        k_iop = sb("k_iop", [128, NE])
        k_rsq = sb("k_rsq", [128, NT])
        k_mask = sb("k_mask", [128, NPT])
        k_garb = sb("k_garb", [128, NT])
        k_lnb = sb("k_lnb", [128, 8])
        k_lng = sb("k_lng", [128, 8])
        wr_sb = sb("wr_sb", [128, 8 * NE])
        pbf = sb("pbf_pool", [128, PBF_N], BF16)
        pf = sb("pf_pool", [128, PF_N])
        def bfs(off, n):
            return pbf[:, off:off + n]
        def f32s(off, n):
            return pf[:, off:off + n]
        s1c = sb("s1c", [128, 1]); nmc = sb("nmc", [128, 1])
        vsc = sb("vsc", [128, 1]); sdc = sb("sdc", [128, 1]); rrc = sb("rrc", [128, 1])
        mxc = sb("mxc", [128, 1]); idxc = sb("idxc", [128, 1])
        seqc = sb("seqc", [128, NE]); t16c = sb("t16c", [128, NE]); ohc = sb("ohc", [128, NE])
        poshc = sb("poshc", [128, NE]); ptc = sb("ptc", [128, 1]); gec = sb("gec", [128, 1])
        slf = sb("slf", [128, 1]); slsf = sb("slsf", [128, 1]); slgf = sb("slgf", [128, 1])
        slz = sb("slz", [128, 1]); s_u32 = sb("s_u32", [128, 1], U32)
        qv0 = sb("qv0", [128, 256]); qv1 = sb("qv1", [128, 256])
        qp0 = sb("qp0", [128, 256]); qp1 = sb("qp1", [128, 256])
        ssq, snr, src_ = qv0, qv1, qp0

        pb = [ps(f"pb{k}", [128, 512], F32) for k in range(7)]
        pbank_bf = es.enter_context(nc.psum_tensor("pbank_bf", [128, 1024], BF16))

        def ld(dst, src):
            pr.emit("sync", lambda e, d=dst, s=src: e.dma_start(out=d, in_=s), inc=(s_x, 16))
        for dst, src in [(k_ident[:, :], ident), (k_ui[:, :], u_incl), (k_us[:, :], u_strict),
                         (k_onesc3[0:1, :], ones_col), (k_onesc3[32:33, :], ones_col),
                         (k_onesc3[64:65, :], ones_col), (k_ones128[:, :], ones128),
                         (k_iob[:, :], iota_big), (k_iop[:, :], iota_pln), (k_rsq[:, :], rsq_pos),
                         (k_mask[:, :], mask_pre), (k_garb[:, :], garb),
                         (rowsA[32:33, :], bk_r), (rowsA[64:65, :], bv_r), (rowsB[32:33, :], bo_r),
                         (k_lnb[:, :], lnb_fm), (k_lng[:, :], lng_fm)]:
            ld(dst, src)
        ld(wr_sb[:, :].rearrange("p (c e) -> p c e", c=8), wr_t.rearrange("(c p) e -> p c e", p=128))
        pr.emit("gpsimd", lambda e: e.dma_start(out=k_ident_bf[:, :], in_=ident), inc=(s_g, 16))
        pr.emit("gpsimd", lambda e: e.dma_start(out=k_onesb3[0:1, :], in_=ones_col), inc=(s_g, 16))
        pr.emit("gpsimd", lambda e: e.dma_start(out=k_onesb3[32:33, :], in_=ones_col), inc=(s_g, 16))
        pr.emit("gpsimd", lambda e: e.dma_start(out=k_onesb3[64:65, :], in_=ones_col), inc=(s_g, 16))
        pr.emit("gpsimd", lambda e: e.dma_start(out=rowsC[32:33, :], in_=bk_r), inc=(s_g, 16))
        pr.emit("gpsimd", lambda e: e.dma_start(out=rowsC[64:65, :], in_=bv_r), inc=(s_g, 16))
        const_x = s_x.n
        const_g = s_g.n

        pr.emit("vector", lambda e: e.memset(rowsA[0:1, :], 0.0))
        pr.emit("vector", lambda e: e.memset(carry16[:, :], 0.0))
        pr.emit("vector", lambda e: e.memset(rowsB[64:65, :], 0.0))
        v_zero = c_v.n
        pr.emit("sync", lambda e: e.dma_start(out=y_dram[ZROW:ZROW + 1, :], in_=rowsB[64:65, :]),
                waits=[(c_v, v_zero)], inc=(s_st, 16))

        FO_XS0, FO_XS1, FO_XN32, FO_LNSCR, FO_FM32 = 0, 1024, 2048, 3072, 4096
        FO_KS, FO_VS, FO_KN, FO_BD, FO_MS = 4096, 5120, 6144, 1024, 5120
        FO_XP0, FO_XP1, FO_XNP = 0, 1024, 2048
        FO_RT, FO_ZSC, FO_Z = 1024, 0, 2048
        FO_SG = 3072
        FO_YSB = 4096
        FO_YTOK, FO_OT = 6144, 2048
        BO_XNTOK, BO_HS = 0, NT * D
        BO_WG0, BO_WG1, BO_WU0, BO_WU1 = 0, 4096, 8192, 12288
        BO_WD0, BO_WD1, BO_ACT = 16384, 20480, 24576
        BO_WK, BO_WV, BO_WOG = 0, 8192, 16384
        BO_XNPFM, BO_ZB, BO_ZFM = 24576, 24576, 24576
        BO_GUP0, BO_GUP1, BO_DWN0, BO_DWN1 = 0, 8192, 16384, 20480
        BO_HTOK, BO_HFM, BO_FACT = 24576, 26624, 28672

        def emit_ln(xin, xout, xw):
            pr.emit("vector", lambda e: e.reduce_sum(s1c[:, :], xin, axis=AX.X),
                    waits=([(s_x, xw)] if xw else []))
            pr.emit("vector", lambda e: e.tensor_scalar(nmc[:, :], s1c[:, :], -1.0 / D, None, op0=ALU.mult))
            nm_done = c_v.n
            pr.emit("scalar", lambda e: e.activation(f32s(FO_LNSCR, D), xin, AF.Square,
                                                     bias=nmc[:, :], scale=1.0,
                                                     accum_out=vsc[:, :]),
                    waits=[(c_v, nm_done)])
            vs_done = c_s.n
            pr.emit("vector", lambda e: e.tensor_scalar(vsc[:, :], vsc[:, :], 1.0 / D, LN_EPS,
                                                        op0=ALU.mult, op1=ALU.add),
                    waits=[(c_s, vs_done)])
            ve_done = c_v.n
            pr.emit("scalar", lambda e: e.activation(sdc[:, :], vsc[:, :], AF.Sqrt),
                    waits=[(c_v, ve_done)])
            sd_done = c_s.n
            pr.emit("vector", lambda e: e.reciprocal(rrc[:, :], sdc[:, :]),
                    waits=[(c_s, sd_done)])
            pr.emit("vector", lambda e: e.tensor_scalar(xout, xin, nmc[:, :], rrc[:, :],
                                                        op0=ALU.add, op1=ALU.mult))
            return c_v.n

        for i in range(NT):
            xst = f32s(FO_XS0 if i % 2 == 0 else FO_XS1, D)
            if i >= 2:
                pr.wait("sync", c_v)
                pr.wait("sync", c_s)
            pr.emit("sync", lambda e, d=xst, s=x_own[i * 128:(i + 1) * 128, :]:
                    e.dma_start(out=d, in_=s), inc=(s_x, 16))
            xw = s_x.n
            emit_ln(xst, f32s(FO_XN32, D), xw)
            pr.emit("vector", lambda e, i=i, xst=xst: e.tensor_tensor(
                        acc[:, i * D:(i + 1) * D], f32s(FO_XN32, D), xst, ALU.add))
            pr.emit("vector", lambda e, i=i: e.tensor_copy(
                        bfs(BO_XNTOK + i * D, D), f32s(FO_XN32, D)))
            xn_done = c_v.n
            pr.wait("tensor", c_v, xn_done)
            def tp_all(e, i=i):
                for j in range(8):
                    e.transpose(pbank_bf[:, j * 128:(j + 1) * 128],
                                bfs(BO_XNTOK + i * D + j * 128, 128),
                                k_ident_bf[:, :])
                for j in range(8):
                    ins = e.transpose(pb[4 + j // 4][:, (j % 4) * 128:(j % 4 + 1) * 128],
                                      f32s(FO_XN32 + j * 128, 128), k_ident[:, :])
                return ins
            pr.emit("tensor", tp_all, waits=[(s_g, const_g)], inc=(c_t, 1))
            tp_done = c_t.n
            pr.emit("vector", lambda e, i=i: e.tensor_copy(
                        xn_fm[:, :].rearrange("p (c t) -> p c t", c=8)[:, :, i * 128:(i + 1) * 128],
                        pbank_bf[:, :].rearrange("p (c t) -> p c t", c=8)[:, :, :]),
                    waits=[(c_t, tp_done)])
            pr.emit("vector", lambda e: e.tensor_copy(f32s(FO_FM32, 512), pb[4][:, :]))
            pr.emit("vector", lambda e: e.tensor_copy(f32s(FO_FM32 + 512, 512), pb[5][:, :]))
            ev_done = c_v.n
            def router(e, i=i):
                for j in range(8):
                    ins = e.matmul(pb[6][:, 0:NE], f32s(FO_FM32 + j * 128, 128),
                                   wr_sb[:, j * NE:(j + 1) * NE],
                                   start=(j == 0), stop=(j == 7))
                return ins
            pr.emit("tensor", router, waits=[(c_v, ev_done), (s_x, const_x)], inc=(c_t, 1))
            lg_done = c_t.n
            pr.emit("vector", lambda e, i=i: e.tensor_copy(
                        logit_all[:, i * NE:(i + 1) * NE], pb[6][:, 0:NE]),
                    waits=[(c_t, lg_done)])

        pr.wait("vector", s_x, const_x)
        for i in range(NT):
            lg = logit_all[:, i * NE:(i + 1) * NE]
            pr.emit("vector", lambda e, lg=lg: e.reduce_max(mxc[:, :], lg, axis=AX.X))
            pr.emit("vector", lambda e, lg=lg: e.tensor_single_scalar(seqc[:, :], lg, mxc[:, :], ALU.is_equal))
            pr.emit("vector", lambda e: e.scalar_tensor_tensor(t16c[:, :], seqc[:, :], -1e4,
                                                               k_iob[:, :], op0=ALU.mult, op1=ALU.add))
            pr.emit("vector", lambda e: e.tensor_reduce(idxc[:, :], t16c[:, :], axis=AX.X, op=ALU.min))
            pr.emit("vector", lambda e: e.tensor_single_scalar(ohc[:, :], k_iop[:, :], idxc[:, :], ALU.is_equal))
            oh_done = c_v.n
            mx_done = c_v.n
            pr.emit("scalar", lambda e, i=i: e.activation(sval[:, i:i + 1], mxc[:, :], AF.Sigmoid),
                    waits=[(c_v, mx_done)])
            s_done = c_s.n
            def posmm(e, i=i):
                e.matmul(pb[0][:, 0:NE], k_us[:, :], ohc[:, :], start=True, stop=False)
                return e.matmul(pb[0][:, 0:NE], k_onesc3[0:1, :], carry16[:, :], start=False, stop=True)
            pr.emit("tensor", posmm, waits=[(c_v, oh_done)], inc=(c_t, 1))
            pos_done = c_t.n
            def c16mm(e, i=i):
                return e.matmul(pb[1][0:1, 0:NE], k_ones128[:, :], ohc[:, :], start=True, stop=True)
            pr.emit("tensor", c16mm, inc=(c_t, 1))
            c16_done = c_t.n
            pr.emit("vector", lambda e: e.tensor_add(carry16[:, :], carry16[:, :], pb[1][0:1, 0:NE]),
                    waits=[(c_t, c16_done)])
            pr.emit("vector", lambda e: e.tensor_tensor(poshc[:, :], pb[0][:, 0:NE], ohc[:, :], ALU.mult),
                    waits=[(c_t, pos_done)])
            pr.emit("vector", lambda e: e.reduce_sum(ptc[:, :], poshc[:, :], axis=AX.X))
            pr.emit("vector", lambda e: e.scalar_tensor_tensor(
                        slf[:, :], idxc[:, :], float(CAP), ptc[:, :], op0=ALU.mult, op1=ALU.add))
            pr.emit("vector", lambda e: e.tensor_scalar(gec[:, :], ptc[:, :], float(CAP), None, op0=ALU.is_ge))
            pr.emit("vector", lambda e, i=i: e.tensor_sub(slz[:, :], k_garb[:, i:i + 1], slf[:, :]))
            pr.emit("vector", lambda e: e.tensor_mul(slz[:, :], slz[:, :], gec[:, :]))
            pr.emit("vector", lambda e: e.tensor_add(slsf[:, :], slf[:, :], slz[:, :]))
            pr.emit("vector", lambda e: e.tensor_scalar(slz[:, :], slf[:, :], -1.0, float(ZROW),
                                                        op0=ALU.mult, op1=ALU.add))
            pr.emit("vector", lambda e: e.tensor_mul(slz[:, :], slz[:, :], gec[:, :]))
            pr.emit("vector", lambda e: e.tensor_add(slgf[:, :], slf[:, :], slz[:, :]))
            pr.emit("vector", lambda e, i=i: e.tensor_copy(slotg[:, i:i + 1], slgf[:, :]))
            pr.emit("vector", lambda e: e.tensor_copy(s_u32[:, :], slsf[:, :]))
            pr.emit("vector", lambda e, i=i: e.tensor_single_scalar(
                        bfs(BO_HS, D), bfs(BO_XNTOK + i * D, D), sval[:, i:i + 1], ALU.mult),
                    waits=[(c_s, s_done)])
            hs_done = c_v.n
            pr.emit("gpsimd", lambda e: e.indirect_dma_start(
                        out=h_dram, out_offset=IndirectOffsetOnAxis(s_u32[:, 0:1], 0),
                        in_=bfs(BO_HS, D), in_offset=None),
                    waits=[(c_v, hs_done)], inc=(s_g, 16))
            pr.wait("vector", s_g)
        b_done_v = c_v.n

        wg_off = [BO_WG0, BO_WG1]
        wu_off = [BO_WU0, BO_WU1]
        wd_off = [BO_WD0, BO_WD1]
        grp_done_t = {}
        for g in range(4):
            bi = g % 2
            if bi in grp_done_t:
                pr.wait("gpsimd", c_t, grp_done_t[bi])
            pr.emit("gpsimd", lambda e, g=g, bi=bi: e.dma_start(
                        out=bfs(wg_off[bi], 4096).rearrange("p (c f) -> p c f", c=8),
                        in_=wg_t[:, g * 512:(g + 1) * 512].rearrange("(c p) f -> p c f", p=128)),
                    inc=(s_g, 16))
            pr.emit("gpsimd", lambda e, g=g, bi=bi: e.dma_start(
                        out=bfs(wu_off[bi], 4096).rearrange("p (c f) -> p c f", c=8),
                        in_=wu_t[:, g * 512:(g + 1) * 512].rearrange("(c p) f -> p c f", p=128)),
                    inc=(s_g, 16))
            pr.emit("gpsimd", lambda e, g=g, bi=bi: e.dma_start(
                        out=bfs(wd_off[bi], 4096).rearrange("p (c h) -> p c h", c=4),
                        in_=wd_t[g * 512:(g + 1) * 512, :].rearrange("(c p) h -> p c h", p=128)),
                    inc=(s_g, 16))
            gw = s_g.n
            for tb in range(8):
                for ft in range(4):
                    def gatemm(e, bi=bi, tb=tb, ft=ft):
                        for j in range(8):
                            ins = e.matmul(pb[4][:, 0:256],
                                           bfs(wg_off[bi] + j * 512 + ft * 128, 128),
                                           xn_fm[:, j * T_OWN + tb * 256: j * T_OWN + (tb + 1) * 256],
                                           start=(j == 0), stop=(j == 7))
                        return ins
                    w = [(s_g, gw)]
                    if g == 0 and tb == 0 and ft == 0:
                        w.append((c_v, b_done_v))
                    pr.emit("tensor", gatemm, waits=w, inc=(c_t, 1))
                    gmm = c_t.n
                    def upmm(e, bi=bi, tb=tb, ft=ft):
                        for j in range(8):
                            ins = e.matmul(pb[5][:, 0:256],
                                           bfs(wu_off[bi] + j * 512 + ft * 128, 128),
                                           xn_fm[:, j * T_OWN + tb * 256: j * T_OWN + (tb + 1) * 256],
                                           start=(j == 0), stop=(j == 7))
                        return ins
                    pr.emit("tensor", upmm, inc=(c_t, 1))
                    umm = c_t.n
                    pr.emit("scalar", lambda e: e.activation(f32s(FO_SG, 256), pb[4][:, 0:256], AF.Silu),
                            waits=[(c_t, gmm)])
                    sgd = c_s.n
                    pr.emit("vector", lambda e, ft=ft: e.tensor_tensor(
                                bfs(BO_ACT + ft * 256, 256), f32s(FO_SG, 256), pb[5][:, 0:256], ALU.mult),
                            waits=[(c_t, umm), (c_s, sgd)])
                    act_done = c_v.n
                    def downmm(e, bi=bi, ft=ft):
                        for tt in range(2):
                            for nh in range(2):
                                ins = e.matmul(pb[tt * 2 + nh][:, :],
                                               bfs(BO_ACT + ft * 256 + tt * 128, 128),
                                               bfs(wd_off[bi] + ft * 1024 + nh * 512, 512),
                                               start=(ft == 0), stop=(ft == 3))
                        return ins
                    pr.emit("tensor", downmm, waits=[(c_v, act_done)], inc=(c_t, 1))
                dn = c_t.n
                grp_done_t[bi] = dn
                for tt in range(2):
                    ti = tb * 2 + tt
                    for nh in range(2):
                        pr.emit("vector", lambda e, ti=ti, tt=tt, nh=nh: e.tensor_add(
                                    acc[:, ti * D + nh * 512: ti * D + (nh + 1) * 512],
                                    acc[:, ti * D + nh * 512: ti * D + (nh + 1) * 512],
                                    pb[tt * 2 + nh][:, :]),
                                waits=[(c_t, dn)])
                yadd = c_v.n
                pr.wait("tensor", c_v, yadd)

        if debug:
            pr.wait("sync", c_v, c_v.n)
            pr.emit("sync", lambda e: e.dma_start(out=dbg["d_accC"], in_=acc[:, :]), inc=(s_st, 16))

        pr.wait("gpsimd", c_t, c_t.n)
        pr.emit("gpsimd", lambda e: e.dma_start(
                    out=bfs(BO_WK, 8192).rearrange("p (c n) -> p c n", c=8),
                    in_=wk_t.rearrange("(c p) n -> p c n", p=128)), inc=(s_g, 16))
        pr.emit("gpsimd", lambda e: e.dma_start(
                    out=bfs(BO_WV, 8192).rearrange("p (c n) -> p c n", c=8),
                    in_=wv_t.rearrange("(c p) n -> p c n", p=128)), inc=(s_g, 16))
        kv_w = s_g.n
        pr.wait("sync", c_v, c_v.n)
        for j in range(8):
            xo = FO_XP0 if j % 2 == 0 else FO_XP1
            if j >= 2:
                pr.wait("sync", c_t)
            pr.emit("sync", lambda e, j=j, xo=xo: e.dma_start(
                        out=f32s(xo, D), in_=wo_t[j * 128:(j + 1) * 128, :]), inc=(s_x, 16))
            ww = s_x.n
            pr.emit("vector", lambda e, j=j, xo=xo: e.tensor_single_scalar(
                        bfs(BO_WOG + j * 1024, D), f32s(xo, D), k_lng[:, j:j + 1], ALU.mult),
                    waits=[(s_x, ww)])
            wsc = c_v.n
            def rowmm2(e, j=j, xo=xo):
                e.matmul(pb[6][0:1, :], k_lnb[:, j:j + 1], f32s(xo, 512),
                         start=(j == 0), stop=(j == 7))
                return e.matmul(pb[2][0:1, :], k_lnb[:, j:j + 1], f32s(xo + 512, 512),
                                start=(j == 0), stop=(j == 7))
            pr.emit("tensor", rowmm2, waits=[(c_v, wsc)], inc=(c_t, 1))
        row_done = c_t.n
        pr.emit("vector", lambda e: e.tensor_add(rowsB[0:1, 0:512], rowsB[32:33, 0:512], pb[6][0:1, :]),
                waits=[(c_t, row_done)])
        pr.emit("vector", lambda e: e.tensor_add(rowsB[0:1, 512:1024], rowsB[32:33, 512:1024], pb[2][0:1, :]))
        pr.emit("vector", lambda e: e.tensor_copy(rowsC[0:1, :], rowsB[0:1, :]))
        if debug:
            pr.wait("sync", c_v, c_v.n)
            pr.emit("sync", lambda e: e.dma_start(out=dbg["d_row"], in_=rowsB[0:1, :]), inc=(s_st, 16))

        QTAB = {
            0: [(1, 0, 0), (-1, 1, 1), (-1, 2, 2), (-1, 3, 3)],
            1: [(1, 0, 1), (1, 1, 0), (1, 2, 3), (-1, 3, 2)],
            2: [(1, 0, 2), (-1, 1, 3), (1, 2, 0), (1, 3, 1)],
            3: [(1, 0, 3), (1, 1, 2), (-1, 2, 1), (1, 3, 0)],
        }

        def quat_emit(out_off, a_off, b_off, conj_b, pool_comps, waits_v=(), waits_p=()):
            wv = list(waits_v)
            wp = list(waits_p)
            for comp in range(4):
                eng = "gpsimd" if comp in pool_comps else "vector"
                t0, t1 = (qp0, qp1) if eng == "gpsimd" else (qv0, qv1)
                terms = QTAB[comp]
                if conj_b:
                    terms = [(s * (-1 if b != 0 else 1), a, b) for (s, a, b) in terms]
                terms = sorted(terms, key=lambda t: -t[0])
                ov = pf[:, out_off + comp:out_off + D:4]
                w = wp if eng == "gpsimd" else wv
                for n_, (s, a, b) in enumerate(terms):
                    av = pf[:, a_off + a:a_off + D:4]
                    bv = pf[:, b_off + b:b_off + D:4]
                    if n_ == 0:
                        pr.emit(eng, lambda e, av=av, bv=bv, t0=t0: e.tensor_mul(t0[:, :], av, bv),
                                waits=tuple(w))
                        if eng == "gpsimd":
                            wp = []
                        else:
                            wv = []
                    else:
                        pr.emit(eng, lambda e, av=av, bv=bv, t1=t1: e.tensor_mul(t1[:, :], av, bv))
                        dst = ov if n_ == 3 else t0[:, :]
                        op = ALU.add if s > 0 else ALU.subtract
                        pr.emit(eng, lambda e, dst=dst, t0=t0, t1=t1, op=op:
                                e.tensor_tensor(dst, t0[:, :], t1[:, :], op))
            return c_v.n, c_p.n

        FO_LN2 = 3072

        st_state = {"cons_v": 0, "cons_p": 0, "evac_s": c_s.n, "tot_add_v": 0,
                    "pbank_v": c_v.n, "dump_st": 0}

        def emit_kv_mm(fm_fn, extra_wait):
            def kmm(e):
                for j in range(8):
                    for nh in range(2):
                        e.matmul(pb[nh][:, :], fm_fn(j),
                                 bfs(BO_WK + j * 1024 + nh * 512, 512),
                                 start=(j == 0), stop=False)
                for nh in range(2):
                    ins = e.matmul(pb[nh][:, :], k_onesb3[32:33, :],
                                   rowsC[32:33, nh * 512:(nh + 1) * 512],
                                   start=False, stop=True)
                return ins
            pr.emit("tensor", kmm,
                    waits=[(s_g, kv_w), (c_s, st_state["evac_s"])] + list(extra_wait),
                    inc=(c_t, 1))
            def vmm(e):
                for j in range(8):
                    for nh in range(2):
                        e.matmul(pb[2 + nh][:, :], fm_fn(j),
                                 bfs(BO_WV + j * 1024 + nh * 512, 512),
                                 start=(j == 0), stop=False)
                for nh in range(2):
                    ins = e.matmul(pb[2 + nh][:, :], k_onesb3[64:65, :],
                                   rowsC[64:65, nh * 512:(nh + 1) * 512],
                                   start=False, stop=True)
                return ins
            pr.emit("tensor", vmm, inc=(c_t, 1))
            return c_t.n

        def emit_kv_evac(v_done):
            w = [(c_t, v_done), (c_v, st_state["cons_v"]), (c_p, st_state["cons_p"])]
            if debug and st_state["dump_st"]:
                w.append((s_st, st_state["dump_st"]))
                st_state["dump_st"] = 0
            pr.emit("scalar", lambda e: e.copy(f32s(FO_KS, 512), pb[0][:, :]), waits=w)
            pr.emit("scalar", lambda e: e.copy(f32s(FO_KS + 512, 512), pb[1][:, :]))
            pr.emit("scalar", lambda e: e.copy(f32s(FO_VS, 512), pb[2][:, :]))
            pr.emit("scalar", lambda e: e.copy(f32s(FO_VS + 512, 512), pb[3][:, :]))
            st_state["evac_s"] = c_s.n

        def emit_knorm_bound(parity):
            pr.emit("scalar", lambda e: e.activation(f32s(FO_LN2, D), f32s(FO_KS, D), AF.Square))
            sq_done = c_s.n
            sqv = lambda c: pf[:, FO_LN2 + c:FO_LN2 + D:4]
            pr.emit("vector", lambda e: e.tensor_tensor(ssq[:, :], sqv(0), sqv(1), ALU.add),
                    waits=[(c_s, sq_done)])
            pr.emit("vector", lambda e: e.tensor_tensor(snr[:, :], sqv(2), sqv(3), ALU.add))
            pr.emit("vector", lambda e: e.tensor_tensor(ssq[:, :], ssq[:, :], snr[:, :], ALU.add))
            ss_done = c_v.n
            pr.emit("scalar", lambda e: e.activation(snr[:, :], ssq[:, :], AF.Sqrt), waits=[(c_v, ss_done)])
            nrm_done = c_s.n
            pr.emit("vector", lambda e: e.reciprocal(src_[:, :], snr[:, :]), waits=[(c_s, nrm_done)])
            pr.emit("vector", lambda e: e.tensor_scalar(src_[:, :], src_[:, :], 1e12, None, op0=ALU.min))
            rcp_done = c_v.n
            for c in range(4):
                eng = "vector" if c < 2 else "gpsimd"
                w = [(c_v, rcp_done)] if eng == "gpsimd" else []
                pr.emit(eng, lambda e, c=c: e.tensor_tensor(
                            pf[:, FO_KN + c:FO_KN + D:4], pf[:, FO_KS + c:FO_KS + D:4],
                            src_[:, :], ALU.mult), waits=w)
            kn_v, kn_p = c_v.n, c_p.n
            pool_comps = (1, 2, 3) if parity == 0 else (2, 3)
            return quat_emit(FO_BD, FO_VS, FO_KN, False, pool_comps,
                             waits_v=[(c_v, kn_v), (c_p, kn_p)],
                             waits_p=[(c_v, kn_v), (c_p, kn_p)])

        def emit_tot_carry(lhs_col):
            for nh in range(2):
                def totmm(e, nh=nh, lhs_col=lhs_col):
                    return e.matmul(pb[6][0:1, :], lhs_col,
                                    f32s(FO_BD + nh * 512, 512), start=True, stop=True)
                w = [(c_v, st_state["cons_v"]), (c_p, st_state["cons_p"]),
                     (c_v, st_state["tot_add_v"])] if nh == 0 else [(c_v, c_v.n)]
                pr.emit("tensor", totmm, waits=w, inc=(c_t, 1))
                td = c_t.n
                pr.emit("vector", lambda e, nh=nh: e.tensor_add(
                            rowsA[0:1, nh * 512:(nh + 1) * 512],
                            rowsA[0:1, nh * 512:(nh + 1) * 512], pb[6][0:1, :]),
                        waits=[(c_t, td)])
            st_state["tot_add_v"] = c_v.n

        def emit_tail_D(j8):
            bd_v, bd_p = emit_knorm_bound(j8 & 1)
            st_state["cons_v"], st_state["cons_p"] = bd_v, bd_p
            emit_tot_carry(k_mask[:, j8:j8 + 1])

        def emit_tail_E(i):
            bd_v, bd_p = emit_knorm_bound(i & 1)
            st_state["cons_v"], st_state["cons_p"] = bd_v, bd_p
            def memmm(e, i=i):
                for nh in range(2):
                    e.matmul(pb[4 + nh][:, :], k_ui[:, :], f32s(FO_BD + nh * 512, 512),
                             start=True, stop=False)
                    ins = e.matmul(pb[4 + nh][:, :], k_onesc3[0:1, :],
                                   rowsA[0:1, nh * 512:(nh + 1) * 512],
                                   start=False, stop=True)
                return ins
            pr.emit("tensor", memmm, waits=[(c_v, bd_v), (c_p, bd_p)], inc=(c_t, 1))
            mem_done = c_t.n
            pr.emit("vector", lambda e, i=i: e.tensor_single_scalar(
                        f32s(FO_MS, 512), pb[4][:, :], k_rsq[:, i:i + 1], ALU.mult),
                    waits=[(c_t, mem_done)])
            pr.emit("vector", lambda e, i=i: e.tensor_single_scalar(
                        f32s(FO_MS + 512, 512), pb[5][:, :], k_rsq[:, i:i + 1], ALU.mult))
            ms_done = c_v.n
            emit_tot_carry(k_ones128[:, :])
            if debug and i == 0:
                pr.wait("sync", c_v, c_v.n)
                pr.wait("sync", c_p, c_p.n)
                pr.emit("sync", lambda e: e.dma_start(out=dbg["d_K"], in_=f32s(FO_KS, D)), inc=(s_st, 16))
                pr.emit("sync", lambda e: e.dma_start(out=dbg["d_kn"], in_=f32s(FO_KN, D)), inc=(s_st, 16))
                pr.emit("sync", lambda e: e.dma_start(out=dbg["d_bound"], in_=f32s(FO_BD, D)), inc=(s_st, 16))
                pr.emit("sync", lambda e: e.dma_start(out=dbg["d_mem"], in_=f32s(FO_MS, D)), inc=(s_st, 16))
                st_state["dump_st"] = s_st.n
            rt_v, rt_p = quat_emit(FO_ZSC, FO_MS, FO_KN, True, (1, 2, 3) if (i & 1) else (2, 3),
                                   waits_v=[(c_v, ms_done)], waits_p=[(c_v, ms_done)])
            pr.wait("vector", c_p, rt_p)
            pr.wait("vector", c_v, rt_v)
            if debug and i == 0:
                pr.wait("sync", c_v, c_v.n)
                pr.emit("sync", lambda e: e.dma_start(out=dbg["d_retr"], in_=f32s(FO_ZSC, D)), inc=(s_st, 16))
                st_state["dump_st"] = s_st.n
            emit_ln(f32s(FO_ZSC, D), bfs(BO_ZB, D), 0)
            zb_done = c_v.n
            pr.wait("tensor", c_v, zb_done)
            def tpz(e):
                for j in range(8):
                    ins = e.transpose(pbank_bf[:, j * 128:(j + 1) * 128],
                                      bfs(BO_ZB + j * 128, 128), k_ident_bf[:, :])
                return ins
            pr.emit("tensor", tpz, inc=(c_t, 1))
            tzd = c_t.n
            pr.emit("vector", lambda e: e.tensor_copy(bfs(BO_ZFM, D), pbank_bf[:, :]),
                    waits=[(c_t, tzd)])
            zfm_done = c_v.n
            st_state["pbank_v"] = zfm_done
            def womm(e, i=i):
                for j in range(8):
                    for nh in range(2):
                        e.matmul(pb[4 + nh][:, :], bfs(BO_ZFM + j * 128, 128),
                                 bfs(BO_WOG + j * 1024 + nh * 512, 512),
                                 start=(j == 0), stop=False)
                for nh in range(2):
                    ins = e.matmul(pb[4 + nh][:, :], k_onesb3[0:1, :],
                                   rowsC[0:1, nh * 512:(nh + 1) * 512],
                                   start=False, stop=True)
                return ins
            pr.emit("tensor", womm, waits=[(c_v, zfm_done)], inc=(c_t, 1))
            wo_done = c_t.n
            for nh in range(2):
                pr.emit("vector", lambda e, i=i, nh=nh: e.tensor_add(
                            acc[:, i * D + nh * 512: i * D + (nh + 1) * 512],
                            acc[:, i * D + nh * 512: i * D + (nh + 1) * 512],
                            pb[4 + nh][:, :]),
                        waits=[(c_t, wo_done)])

        ln_done_hist = {}
        pending = None
        for step in range(NPT + NT):
            if step < NPT:
                j8 = step
                xo = FO_XP0
                if j8 >= 1:
                    pr.wait("sync", c_v, ln_done_hist[j8 - 1])
                else:
                    pr.wait("sync", c_t, row_done)
                pr.emit("sync", lambda e, j8=j8, xo=xo: e.dma_start(
                            out=f32s(xo, D), in_=x_pre[j8 * 128:(j8 + 1) * 128, :]), inc=(s_x, 16))
                xw = s_x.n
                ln_done_hist[j8] = emit_ln(f32s(xo, D), bfs(BO_XNPFM, D), xw)
                xnb_done = c_v.n
                pr.wait("tensor", c_v, max(xnb_done, st_state["pbank_v"]))
                def tpp(e):
                    for j in range(8):
                        ins = e.transpose(pbank_bf[:, j * 128:(j + 1) * 128],
                                          bfs(BO_XNPFM + j * 128, 128), k_ident_bf[:, :])
                    return ins
                pr.emit("tensor", tpp, inc=(c_t, 1))
                tpd = c_t.n
                pr.emit("vector", lambda e: e.tensor_copy(bfs(BO_XNPFM, D), pbank_bf[:, :]),
                        waits=[(c_t, tpd)])
                fme = c_v.n
                st_state["pbank_v"] = fme
                v_done = emit_kv_mm(lambda j: bfs(BO_XNPFM + j * 128, 128), [(c_v, fme)])
            else:
                i = step - NPT
                v_done = emit_kv_mm(
                    lambda j, i=i: xn_fm[:, j * T_OWN + i * 128: j * T_OWN + (i + 1) * 128], [])
            if pending is not None:
                (emit_tail_D if pending[0] == "D" else emit_tail_E)(pending[1])
                if debug and step == NPT:
                    pr.wait("sync", c_v, c_v.n)
                    pr.emit("sync", lambda e: e.dma_start(out=dbg["d_carryD"], in_=rowsA[0:1, :]), inc=(s_st, 16))
                    st_state["dump_st"] = s_st.n
            emit_kv_evac(v_done)
            pending = ("D", j8) if step < NPT else ("E", step - NPT)
        (emit_tail_D if pending[0] == "D" else emit_tail_E)(pending[1])
        if debug:
            pr.wait("sync", c_v, c_v.n)
            pr.emit("sync", lambda e: e.dma_start(out=dbg["d_carryE"], in_=rowsA[0:1, :]), inc=(s_st, 16))
            pr.emit("sync", lambda e: e.dma_start(out=dbg["d_accE"], in_=acc[:, :]), inc=(s_st, 16))
            prc = sb("prc", [128, D])
            pr.emit("vector", lambda e: e.tensor_copy(prc[:, :], rowsC[:, :]))
            pr.wait("sync", c_v, c_v.n)
            pr.emit("sync", lambda e: e.dma_start(out=dbg["d_rowsC"], in_=prc[:, :]), inc=(s_st, 16))

        nonlocal_dn = [0, 0]
        dn_sg = [0, 0]
        dwn_off = [BO_DWN0, BO_DWN1]
        def emit_down(bi, g, ft, mult_v):
            def downmm(e, bi=bi, g=g, ft=ft):
                for tt in range(2):
                    for nh in range(2):
                        ins = e.matmul(pb[tt * 2 + nh][:, :],
                                       bfs(BO_FACT + ft * 256 + tt * 128, 128),
                                       bfs(dwn_off[bi] + ft * 1024 + nh * 512, 512),
                                       start=(g == 0 and ft == 0), stop=(g == 3 and ft == 3))
                return ins
            pr.emit("tensor", downmm, waits=[(c_v, mult_v), (s_g, dn_sg[bi])], inc=(c_t, 1))
            nonlocal_dn[bi] = c_t.n

        gup_off = [BO_GUP0, BO_GUP1]
        pr.wait("gpsimd", c_t, c_t.n)
        gu_done = {}
        ystore_st = 0
        hload_w = None
        for e_i in range(NE):
            if e_i == 0:
                pr.emit("sync", lambda e: e.dma_start(
                            out=bfs(BO_HTOK, 2048).rearrange("p (tt d) -> p tt d", tt=2),
                            in_=h_dram[0:CAP, :].rearrange("(tt p) d -> p tt d", p=128)),
                        waits=[(s_g, s_g.n), (c_t, c_t.n)], inc=(s_x, 16))
                hload_w = s_x.n
            hw = hload_w
            for tt in range(2):
                def tph(e, tt=tt):
                    for j in range(8):
                        ins = e.transpose(pbank_bf[:, j * 128:(j + 1) * 128],
                                          bfs(BO_HTOK + tt * 1024 + j * 128, 128), k_ident_bf[:, :])
                    return ins
                pr.emit("tensor", tph, waits=[(s_x, hw), (c_v, c_v.n)], inc=(c_t, 1))
                tphd = c_t.n
                pr.emit("vector", lambda e, tt=tt: e.tensor_copy(
                            bfs(BO_HFM, 2048).rearrange("p (c t) -> p c t", c=8)[:, :, tt * 128:(tt + 1) * 128],
                            pbank_bf[:, :].rearrange("p (c t) -> p c t", c=8)[:, :, :]),
                        waits=[(c_t, tphd)])
            hfm_done = c_v.n
            st_state["pbank_v"] = hfm_done
            if e_i + 1 < NE:
                pr.emit("sync", lambda e, e_i=e_i: e.dma_start(
                            out=bfs(BO_HTOK, 2048).rearrange("p (tt d) -> p tt d", tt=2),
                            in_=h_dram[(e_i + 1) * CAP:(e_i + 2) * CAP, :].rearrange("(tt p) d -> p tt d", p=128)),
                        waits=[(c_t, tphd)], inc=(s_x, 16))
                hload_w = s_x.n
            mult_hist = {}
            for g in range(4):
                bi = g % 2
                wgu = [(c_t, gu_done[bi])] if bi in gu_done else []
                pr.emit("gpsimd", lambda e, e_i=e_i, g=g, bi=bi: e.dma_start(
                            out=bfs(gup_off[bi], 8192).rearrange("p (c f) -> p c f", c=8)[:, :, 0:512],
                            in_=gup[e_i, :, g * 512:(g + 1) * 512].rearrange("(c p) f -> p c f", p=128)),
                        waits=wgu, inc=(s_g, 16))
                pr.emit("gpsimd", lambda e, e_i=e_i, g=g, bi=bi: e.dma_start(
                            out=bfs(gup_off[bi], 8192).rearrange("p (c f) -> p c f", c=8)[:, :, 512:1024],
                            in_=gup[e_i, :, FF + g * 512: FF + (g + 1) * 512].rearrange("(c p) f -> p c f", p=128)),
                        inc=(s_g, 16))
                pr.emit("gpsimd", lambda e, e_i=e_i, g=g, bi=bi: e.dma_start(
                            out=bfs(dwn_off[bi], 4096).rearrange("p (c h) -> p c h", c=4),
                            in_=dwn[e_i, g * 512:(g + 1) * 512, :].rearrange("(c p) h -> p c h", p=128)),
                        waits=[(c_t, nonlocal_dn[bi])], inc=(s_g, 16))
                dn_sg[bi] = s_g.n
                gw = s_g.n
                for ft in range(4):
                    gbank = pb[4] if ft % 2 == 0 else pb[6]
                    def gatemm(e, bi=bi, ft=ft, gbank=gbank):
                        for j in range(8):
                            ins = e.matmul(gbank[:, 0:256],
                                           bfs(gup_off[bi] + j * 1024 + ft * 128, 128),
                                           bfs(BO_HFM + j * 256, 256),
                                           start=(j == 0), stop=(j == 7))
                        return ins
                    w = [(s_g, gw)]
                    if g == 0 and ft == 0:
                        w.append((c_v, hfm_done))
                    pr.emit("tensor", gatemm, waits=w, inc=(c_t, 1))
                    gmm = c_t.n
                    wup = ([(c_v, mult_hist[ft - 1])] if ft > 0 else
                           ([(c_v, mult_hist[3])] if g > 0 else [(c_v, hfm_done)]))
                    def upmm(e, bi=bi, ft=ft):
                        for j in range(8):
                            ins = e.matmul(pb[5][:, 0:256],
                                           bfs(gup_off[bi] + j * 1024 + 512 + ft * 128, 128),
                                           bfs(BO_HFM + j * 256, 256),
                                           start=(j == 0), stop=(j == 7))
                        return ins
                    pr.emit("tensor", upmm, waits=wup, inc=(c_t, 1))
                    umm = c_t.n
                    if ft > 0:
                        emit_down(bi, g, ft - 1, mult_hist[ft - 1])
                    pr.emit("scalar", lambda e, gbank=gbank: e.activation(
                                f32s(FO_SG, 256), gbank[:, 0:256], AF.Silu),
                            waits=[(c_t, gmm)])
                    sgd = c_s.n
                    pr.emit("vector", lambda e, ft=ft: e.tensor_tensor(
                                bfs(BO_FACT + ft * 256, 256), f32s(FO_SG, 256), pb[5][:, 0:256], ALU.mult),
                            waits=[(c_t, umm), (c_s, sgd), (c_t, max(nonlocal_dn))])
                    mult_hist[ft] = c_v.n
                emit_down(bi, g, 3, mult_hist[3])
                gu_done[bi] = c_t.n
            ydn = max(nonlocal_dn)
            for tt in range(2):
                for nh in range(2):
                    w = [(c_t, ydn)]
                    if tt == 0 and nh == 0 and ystore_st:
                        w.append((s_st, ystore_st))
                    pr.emit("vector", lambda e, tt=tt, nh=nh: e.tensor_copy(
                                f32s(FO_YSB + tt * 1024 + nh * 512, 512), pb[tt * 2 + nh][:, :]),
                            waits=w)
            yev = c_v.n
            pr.emit("sync", lambda e, e_i=e_i: e.dma_start(
                        out=y_dram[e_i * CAP:(e_i + 1) * CAP, :].rearrange("(tt p) d -> p tt d", p=128),
                        in_=f32s(FO_YSB, 2048).rearrange("p (tt d) -> p tt d", tt=2)),
                    waits=[(c_v, yev)], inc=(s_st, 16))
            ystore_st = s_st.n
        if debug:
            pr.wait("sync", s_st, s_st.n)
            pr.emit("sync", lambda e: e.dma_start(out=dbg["d_ydram"], in_=y_dram), inc=(s_st, 16))

        y_all_st = s_st.n
        for i in range(NT):
            pr.emit("gpsimd", lambda e, i=i: e.indirect_dma_start(
                        out=f32s(FO_YTOK, D), out_offset=None,
                        in_=y_dram, in_offset=IndirectOffsetOnAxis(slotg[:, i:i + 1], 0)),
                    waits=[(s_st, y_all_st)], inc=(s_g, 16))
            gy = s_g.n
            pr.emit("vector", lambda e, i=i: e.tensor_add(
                        f32s(FO_OT, D), acc[:, i * D:(i + 1) * D], f32s(FO_YTOK, D)),
                    waits=[(s_g, gy)])
            ot_done = c_v.n
            pr.emit("sync", lambda e, i=i: e.dma_start(
                        out=out[i * 128:(i + 1) * 128, :], in_=f32s(FO_OT, D)),
                    waits=[(c_v, ot_done)], inc=(s_st, 16))
            pr.wait("gpsimd", s_st)
            pr.wait("vector", s_st)

        if debug:
            pr.wait("sync", c_v, c_v.n)
            pr.wait("sync", s_g, s_g.n)
            pr.emit("sync", lambda e: e.dma_start(out=dbg["d_xnfm"], in_=xn_fm[:, :]), inc=(s_st, 16))
            pr.emit("sync", lambda e: e.dma_start(out=dbg["d_logits"], in_=logit_all[:, :]), inc=(s_st, 16))
            pr.emit("sync", lambda e: e.dma_start(out=dbg["d_sval"], in_=sval[:, :]), inc=(s_st, 16))
            pr.emit("sync", lambda e: e.dma_start(out=dbg["d_carry16"], in_=carry16[:, :]), inc=(s_st, 16))
            pr.emit("sync", lambda e: e.dma_start(out=dbg["d_hdram"], in_=h_dram), inc=(s_st, 16))
            sg32 = sb("sg32", [128, NT])
            pr.emit("vector", lambda e: e.tensor_copy(sg32[:, :], slotg[:, :]))
            pr.wait("sync", c_v, c_v.n)
            pr.emit("sync", lambda e: e.dma_start(out=dbg["d_slots"], in_=sg32[:, :]), inc=(s_st, 16))

        pr.wait("sync", s_st, s_st.n)

        blk = es.enter_context(nc.Block())

        @blk.sync
        def _(eng):
            pr.replay("sync", eng)

        @blk.gpsimd
        def _(eng):
            pr.replay("gpsimd", eng)

        @blk.tensor
        def _(eng):
            pr.replay("tensor", eng)

        @blk.vector
        def _(eng):
            pr.replay("vector", eng)

        @blk.scalar
        def _(eng):
            pr.replay("scalar", eng)

    return nc



def stage_inputs(L0_attn_x, L0_attn_Wk, L0_attn_bk, L0_attn_Wv, L0_attn_bv,
                 L0_attn_ln_g, L0_attn_ln_b, L0_attn_Wo, L0_attn_bo,
                 L0_ffn_w_router, L0_ffn_gate_up_proj, L0_ffn_down_proj,
                 L0_ffn_w_gate, L0_ffn_w_up, L0_ffn_w_down):
    f32 = np.float32
    x = np.ascontiguousarray(np.asarray(L0_attn_x, dtype=f32).reshape(2, 8192, D))
    shared = {
        "wk_t": np.ascontiguousarray(np.asarray(L0_attn_Wk, f32).T),
        "wv_t": np.ascontiguousarray(np.asarray(L0_attn_Wv, f32).T),
        "wo_t": np.ascontiguousarray(np.asarray(L0_attn_Wo, f32).T),
        "wg_t": np.ascontiguousarray(np.asarray(L0_ffn_w_gate, f32).T),
        "wu_t": np.ascontiguousarray(np.asarray(L0_ffn_w_up, f32).T),
        "wd_t": np.ascontiguousarray(np.asarray(L0_ffn_w_down, f32).T),
        "wr_t": np.ascontiguousarray(np.asarray(L0_ffn_w_router, f32).T),
        "gup": np.ascontiguousarray(np.asarray(L0_ffn_gate_up_proj, f32)),
        "dwn": np.ascontiguousarray(np.asarray(L0_ffn_down_proj, f32)),
        "bk_r": np.asarray(L0_attn_bk, f32).reshape(1, D),
        "bv_r": np.asarray(L0_attn_bv, f32).reshape(1, D),
        "bo_r": np.asarray(L0_attn_bo, f32).reshape(1, D),
        "lnb_fm": np.ascontiguousarray(np.asarray(L0_attn_ln_b, f32).reshape(8, 128).T),
        "lng_fm": np.ascontiguousarray(np.asarray(L0_attn_ln_g, f32).reshape(8, 128).T),
        "ident": np.eye(128, dtype=f32),
        "u_incl": np.triu(np.ones((128, 128), f32)),
        "u_strict": np.triu(np.ones((128, 128), f32), 1),
        "ones_col": np.ones((1, 128), f32),
        "ones128": np.ones((128, 1), f32),
        "iota_big": np.tile(np.arange(NE, dtype=f32)[None, :] + 1e4, (128, 1)),
        "iota_pln": np.tile(np.arange(NE, dtype=f32)[None, :], (128, 1)),
    }
    in_maps = []
    for c in range(8):
        b, q = c // 4, c % 4
        own = np.ascontiguousarray(x[b, q * T_OWN:(q + 1) * T_OWN])
        pre = np.zeros((T_PRE, D), f32)
        pre[: q * T_OWN] = x[b, : q * T_OWN]
        mask = np.zeros((128, NPT), f32)
        mask[:, : q * NT] = 1.0
        gpos = (q * T_OWN + np.arange(T_OWN, dtype=np.float64) + 1.0)
        rsq = (1.0 / np.sqrt(gpos)).astype(f32).reshape(NT, 128).T
        garbv = (float(ZROW) + np.arange(T_OWN, dtype=f32)).reshape(NT, 128).T
        m = dict(shared)
        m.update({
            "x_own": own,
            "x_pre": pre,
            "mask_pre": np.ascontiguousarray(mask),
            "rsq_pos": np.ascontiguousarray(rsq),
            "garb": np.ascontiguousarray(garbv),
        })
        in_maps.append(m)
    return in_maps


_NC_CACHE = {}


def run(debug=0, **inputs):
    in_maps = stage_inputs(**inputs)
    if debug not in _NC_CACHE:
        _NC_CACHE[debug] = build_program(debug=debug)
    return run_bass_kernel_spmd(_NC_CACHE[debug], in_maps, core_ids=list(range(8)))


def kernel(**inputs):
    res = run(debug=0, **inputs)
    outs = [np.asarray(r["out"]) for r in res.results]
    full = np.empty((2, 8192, D), np.float32)
    for c in range(8):
        b, q = c // 4, c % 4
        full[b, q * T_OWN:(q + 1) * T_OWN] = outs[c]
    return full


# version 21
# speedup vs baseline: 1.2306x; 1.0710x over previous
import numpy as np
from contextlib import ExitStack

import concourse.bass as bass
import concourse.mybir as mybir
from concourse.bass_core_rust import IndirectOffsetOnAxis
from concourse.bass_utils import run_bass_kernel_spmd

F32 = mybir.dt.float32
BF16 = mybir.dt.bfloat16
U32 = mybir.dt.uint32
AF = mybir.ActivationFunctionType
ALU = mybir.AluOpType
AX = mybir.AxisListType

D = 1024
T_OWN = 2048
NT = T_OWN // 128
T_PRE = 6144
NPT = T_PRE // 128
CAP = 256
NE = 16
FF = 2048
H_ROWS = NE * CAP + T_OWN
Y_ROWS = NE * CAP + 1
ZROW = NE * CAP
LN_EPS = 1e-5

PBF_N = 29696
PF_N = 7168

QPERM = np.array([4 * q + c for c in range(4) for q in range(D // 4)])


class Sem:
    def __init__(self, h):
        self.h = h
        self.n = 0
    def inc(self, k):
        self.n += k
        return self.n


class Prog:
    def __init__(self):
        self.ops = []
        self.strict = {}

    def emit(self, engine, fn, waits=(), inc=None, relaxed=False):
        strict = self.strict.get(engine)
        w = list(waits)
        if strict is not None and not relaxed:
            w.append((strict, strict.n))
            if inc is None:
                inc = (strict, 1)
        w = tuple(w)
        if inc is not None:
            inc[0].inc(inc[1])
        def run(eng, w=w, fn=fn, inc=inc):
            for s, v in w:
                eng.wait_ge(s.h, v)
            ins = fn(eng)
            if inc is not None:
                ins.then_inc(inc[0].h, inc[1])
            return ins
        self.ops.append((engine, run))

    def wait(self, engine, sem, val=None):
        v = sem.n if val is None else val
        self.ops.append((engine, lambda eng, s=sem, v=v: eng.wait_ge(s.h, v)))

    def replay(self, engine, eng):
        for e, fn in self.ops:
            if e == engine:
                fn(eng)


def build_program(debug=0):
    nc = bass.Bass("TRN2", target_bir_lowering=False, debug=False)

    def di(name, shape, dt=F32):
        return nc.dram_tensor(name, shape, dt, kind="ExternalInput").ap()

    x_own = di("x_own", [T_OWN, D])
    x_pre = di("x_pre", [T_PRE, D])
    wk_t = di("wk_t", [D, D])
    wv_t = di("wv_t", [D, D])
    wo_t = di("wo_t", [D, D])
    wg_t = di("wg_t", [D, FF])
    wu_t = di("wu_t", [D, FF])
    wd_t = di("wd_t", [FF, D])
    wr_t = di("wr_t", [D, NE])
    gup = di("gup", [NE, D, 2 * FF])
    dwn = di("dwn", [NE, FF, D])
    bk_r = di("bk_r", [1, D])
    bv_r = di("bv_r", [1, D])
    bo_r = di("bo_r", [1, D])
    lnb_fm = di("lnb_fm", [128, 8])
    lng_fm = di("lng_fm", [128, 8])
    ident = di("ident", [128, 128])
    u_incl = di("u_incl", [128, 128])
    u_strict = di("u_strict", [128, 128])
    ones_col = di("ones_col", [1, 128])
    ones128 = di("ones128", [128, 1])
    iota_big = di("iota_big", [128, NE])
    iota_pln = di("iota_pln", [128, NE])
    rsq_pos = di("rsq_pos", [128, NT])
    mask_pre = di("mask_pre", [128, NPT])
    garb = di("garb", [128, NT])

    out = nc.dram_tensor("out", [T_OWN, D], F32, kind="ExternalOutput").ap()

    dbg = {}
    if debug:
        def do(name, shape, dt=F32):
            dbg[name] = nc.dram_tensor(name, shape, dt, kind="ExternalOutput").ap()
        do("d_xnfm", [128, 8 * T_OWN], BF16)
        do("d_logits", [128, NE * NT])
        do("d_slots", [128, NT])
        do("d_sval", [128, NT])
        do("d_carry16", [1, NE])
        do("d_carryD", [1, D])
        do("d_carryE", [1, D])
        do("d_hdram", [H_ROWS, D], BF16)
        do("d_ydram", [Y_ROWS, D])
        do("d_K", [128, D])
        do("d_kn", [128, D])
        do("d_bound", [128, D])
        do("d_mem", [128, D])
        do("d_retr", [128, D])
        do("d_accC", [128, NT * D])
        do("d_accE", [128, NT * D])
        do("d_rowsC", [128, D])
        do("d_row", [1, D])
        do("d_hfm", [128, 2048], BF16)
        do("d_fact", [128, 1024], BF16)
        do("d_gate", [128, 256])

    h_dram = nc.dram_tensor("h_dram", [H_ROWS, D], BF16, kind="Internal").ap()
    y_dram = nc.dram_tensor("y_dram", [Y_ROWS, D], F32, kind="Internal").ap()

    es = ExitStack()
    sb = lambda name, shape, dt=F32: es.enter_context(nc.sbuf_tensor(name, shape, dt))
    ps = lambda name, shape, dt=F32: es.enter_context(nc.psum_tensor(name, shape, dt))

    pr = Prog()
    with es:
        s_x = Sem(es.enter_context(nc.semaphore("s_x")))
        s_g = Sem(es.enter_context(nc.semaphore("s_g")))
        s_st = Sem(es.enter_context(nc.semaphore("s_st")))
        c_v = Sem(es.enter_context(nc.semaphore("c_v")))
        c_s = Sem(es.enter_context(nc.semaphore("c_s")))
        c_t = Sem(es.enter_context(nc.semaphore("c_t")))
        c_p = Sem(es.enter_context(nc.semaphore("c_p")))
        pr.strict["vector"] = c_v
        pr.strict["scalar"] = c_s
        pr.strict["gpsimd"] = c_p

        acc = sb("acc", [128, NT * D])
        xn_fm = sb("xn_fm", [128, 8 * T_OWN], BF16)
        logit_all = sb("logit_all", [128, NE * NT])
        sval = sb("sval", [128, NT])
        slotg = sb("slotg", [128, NT], U32)
        carry16 = sb("carry16", [1, NE])
        rowsA = sb("rowsA", [128, D])
        rowsB = sb("rowsB", [128, D])
        carry = rowsA[0:1, :]
        attn_row = rowsB[0:1, :]
        k_ident = sb("k_ident", [128, 128])
        k_ident_bf = sb("k_ident_bf", [128, 128], BF16)
        k_ui = sb("k_ui", [128, 128])
        k_us = sb("k_us", [128, 128])
        k_onesc3 = sb("k_onesc3", [128, 128])
        k_onesb3 = sb("k_onesb3", [128, 128], BF16)
        rowsC = sb("rowsC", [128, D], BF16)
        k_onesc = k_onesc3[0:1, :]
        k_ones128 = sb("k_ones128", [128, 1])
        k_iob = sb("k_iob", [128, NE])
        k_iop = sb("k_iop", [128, NE])
        k_rsq = sb("k_rsq", [128, NT])
        k_mask = sb("k_mask", [128, NPT])
        k_garb = sb("k_garb", [128, NT])
        k_lnb = sb("k_lnb", [128, 8])
        k_lng = sb("k_lng", [128, 8])
        wr_sb = sb("wr_sb", [128, 8 * NE])
        pbf = sb("pbf_pool", [128, PBF_N], BF16)
        pf = sb("pf_pool", [128, PF_N])
        def bfs(off, n):
            return pbf[:, off:off + n]
        def f32s(off, n):
            return pf[:, off:off + n]
        s1c = sb("s1c", [128, 1]); nmc = sb("nmc", [128, 1])
        vsc = sb("vsc", [128, 1]); sdc = sb("sdc", [128, 1]); rrc = sb("rrc", [128, 1])
        mxc = sb("mxc", [128, 1]); idxc = sb("idxc", [128, 1])
        seqc = sb("seqc", [128, NE]); t16c = sb("t16c", [128, NE]); ohc = sb("ohc", [128, NE])
        poshc = sb("poshc", [128, NE]); ptc = sb("ptc", [128, 1]); gec = sb("gec", [128, 1])
        slf = sb("slf", [128, 1]); slsf = sb("slsf", [128, 1]); slgf = sb("slgf", [128, 1])
        slz = sb("slz", [128, 1]); s_u32 = sb("s_u32", [128, 1], U32)
        qv0 = sb("qv0", [128, 256]); qv1 = sb("qv1", [128, 256])
        qp0 = sb("qp0", [128, 256]); qp1 = sb("qp1", [128, 256])
        ssq, snr, src_ = qv0, qv1, qp0
        sq2w = sb("sq2w", [128, 512])
        mqc = sb("mqc", [128, 1]); vsc2 = sb("vsc2", [128, 1])
        k_eps24 = sb("k_eps24", [128, 1]); k_epsln = sb("k_epsln", [128, 1])
        k_maskn = sb("k_maskn", [128, NPT])

        pb = [ps(f"pb{k}", [128, 512], F32) for k in range(7)]
        pbank_bf = es.enter_context(nc.psum_tensor("pbank_bf", [128, 1024], BF16))

        def ld(dst, src):
            pr.emit("sync", lambda e, d=dst, s=src: e.dma_start(out=d, in_=s), inc=(s_x, 16))
        for dst, src in [(k_ident[:, :], ident), (k_ui[:, :], u_incl), (k_us[:, :], u_strict),
                         (k_onesc3[0:1, :], ones_col), (k_onesc3[32:33, :], ones_col),
                         (k_onesc3[64:65, :], ones_col), (k_ones128[:, :], ones128),
                         (k_iob[:, :], iota_big), (k_iop[:, :], iota_pln), (k_rsq[:, :], rsq_pos),
                         (k_mask[:, :], mask_pre), (k_garb[:, :], garb),
                         (rowsA[32:33, :], bk_r), (rowsA[64:65, :], bv_r), (rowsB[32:33, :], bo_r),
                         (k_lnb[:, :], lnb_fm), (k_lng[:, :], lng_fm)]:
            ld(dst, src)
        ld(wr_sb[:, :].rearrange("p (c e) -> p c e", c=8), wr_t.rearrange("(c p) e -> p c e", p=128))
        pr.emit("gpsimd", lambda e: e.dma_start(out=k_ident_bf[:, :], in_=ident), inc=(s_g, 16))
        pr.emit("gpsimd", lambda e: e.dma_start(out=k_onesb3[0:1, :], in_=ones_col), inc=(s_g, 16))
        pr.emit("gpsimd", lambda e: e.dma_start(out=k_onesb3[32:33, :], in_=ones_col), inc=(s_g, 16))
        pr.emit("gpsimd", lambda e: e.dma_start(out=k_onesb3[64:65, :], in_=ones_col), inc=(s_g, 16))
        pr.emit("gpsimd", lambda e: e.dma_start(out=rowsC[32:33, :], in_=bk_r), inc=(s_g, 16))
        pr.emit("gpsimd", lambda e: e.dma_start(out=rowsC[64:65, :], in_=bv_r), inc=(s_g, 16))
        const_x = s_x.n
        const_g = s_g.n

        pr.emit("vector", lambda e: e.memset(rowsA[0:1, :], 0.0))
        pr.emit("vector", lambda e: e.memset(carry16[:, :], 0.0))
        pr.emit("vector", lambda e: e.memset(rowsB[64:65, :], 0.0))
        pr.emit("vector", lambda e: e.memset(k_eps24[:, :], 1e-24))
        pr.emit("vector", lambda e: e.memset(k_epsln[:, :], LN_EPS))
        pr.emit("vector", lambda e: e.tensor_scalar(k_maskn[:, :], k_mask[:, :], -1.0, None, op0=ALU.mult),
                waits=[(s_x, const_x)])
        v_zero = c_v.n
        pr.emit("sync", lambda e: e.dma_start(out=y_dram[ZROW:ZROW + 1, :], in_=rowsB[64:65, :]),
                waits=[(c_v, v_zero)], inc=(s_st, 16))

        FO_XS0, FO_XS1, FO_XN32, FO_LNSCR, FO_FM32 = 0, 1024, 2048, 3072, 4096
        FO_KS, FO_VS, FO_KN, FO_BD, FO_MS = 4096, 5120, 6144, 1024, 5120
        FO_XP0, FO_XP1, FO_XNP = 0, 1024, 2048
        FO_RT, FO_ZSC, FO_Z = 1024, 0, 2048
        FO_SG = 3072
        FO_YSB = 4096
        FO_YTOK, FO_OT = 6144, 2048
        BO_XNTOK, BO_HS = 0, NT * D
        BO_WG0, BO_WG1, BO_WU0, BO_WU1 = 0, 4096, 8192, 12288
        BO_WD0, BO_WD1, BO_ACT = 16384, 20480, 24576
        BO_WK, BO_WV, BO_WOG = 0, 8192, 16384
        BO_XNPFM, BO_ZB, BO_ZFM = 24576, 24576, 24576
        BO_GUP0, BO_GUP1, BO_DWN0, BO_DWN1 = 0, 8192, 16384, 20480
        BO_HTOK, BO_HFM, BO_FACT = 24576, 26624, 28672

        def emit_ln(xin, xout, xw):
            pr.emit("vector", lambda e: e.reduce_sum(s1c[:, :], xin, axis=AX.X),
                    waits=([(s_x, xw)] if xw else []))
            pr.emit("vector", lambda e: e.tensor_scalar(nmc[:, :], s1c[:, :], -1.0 / D, None, op0=ALU.mult))
            nm_done = c_v.n
            pr.emit("scalar", lambda e: e.activation(f32s(FO_LNSCR, D), xin, AF.Square,
                                                     bias=nmc[:, :], scale=1.0,
                                                     accum_out=vsc[:, :]),
                    waits=[(c_v, nm_done)])
            vs_done = c_s.n
            pr.emit("vector", lambda e: e.tensor_scalar(vsc[:, :], vsc[:, :], 1.0 / D, LN_EPS,
                                                        op0=ALU.mult, op1=ALU.add),
                    waits=[(c_s, vs_done)])
            ve_done = c_v.n
            pr.emit("scalar", lambda e: e.activation(sdc[:, :], vsc[:, :], AF.Sqrt),
                    waits=[(c_v, ve_done)])
            sd_done = c_s.n
            pr.emit("vector", lambda e: e.reciprocal(rrc[:, :], sdc[:, :]),
                    waits=[(c_s, sd_done)])
            pr.emit("vector", lambda e: e.tensor_scalar(xout, xin, nmc[:, :], rrc[:, :],
                                                        op0=ALU.add, op1=ALU.mult))
            return c_v.n

        for i in range(NT):
            xst = f32s(FO_XS0 if i % 2 == 0 else FO_XS1, D)
            if i >= 2:
                pr.wait("sync", c_v)
                pr.wait("sync", c_s)
            pr.emit("sync", lambda e, d=xst, s=x_own[i * 128:(i + 1) * 128, :]:
                    e.dma_start(out=d, in_=s), inc=(s_x, 16))
            xw = s_x.n
            emit_ln(xst, f32s(FO_XN32, D), xw)
            pr.emit("vector", lambda e, i=i, xst=xst: e.tensor_tensor(
                        acc[:, i * D:(i + 1) * D], f32s(FO_XN32, D), xst, ALU.add))
            pr.emit("vector", lambda e, i=i: e.tensor_copy(
                        bfs(BO_XNTOK + i * D, D), f32s(FO_XN32, D)))
            xn_done = c_v.n
            pr.wait("tensor", c_v, xn_done)
            def tp_all(e, i=i):
                for j in range(8):
                    e.transpose(pbank_bf[:, j * 128:(j + 1) * 128],
                                bfs(BO_XNTOK + i * D + j * 128, 128),
                                k_ident_bf[:, :])
                for j in range(8):
                    ins = e.transpose(pb[4 + j // 4][:, (j % 4) * 128:(j % 4 + 1) * 128],
                                      f32s(FO_XN32 + j * 128, 128), k_ident[:, :])
                return ins
            pr.emit("tensor", tp_all, waits=[(s_g, const_g)], inc=(c_t, 1))
            tp_done = c_t.n
            pr.emit("vector", lambda e, i=i: e.tensor_copy(
                        xn_fm[:, :].rearrange("p (c t) -> p c t", c=8)[:, :, i * 128:(i + 1) * 128],
                        pbank_bf[:, :].rearrange("p (c t) -> p c t", c=8)[:, :, :]),
                    waits=[(c_t, tp_done)])
            pr.emit("vector", lambda e: e.tensor_copy(f32s(FO_FM32, 512), pb[4][:, :]))
            pr.emit("vector", lambda e: e.tensor_copy(f32s(FO_FM32 + 512, 512), pb[5][:, :]))
            ev_done = c_v.n
            def router(e, i=i):
                for j in range(8):
                    ins = e.matmul(pb[6][:, 0:NE], f32s(FO_FM32 + j * 128, 128),
                                   wr_sb[:, j * NE:(j + 1) * NE],
                                   start=(j == 0), stop=(j == 7))
                return ins
            pr.emit("tensor", router, waits=[(c_v, ev_done), (s_x, const_x)], inc=(c_t, 1))
            lg_done = c_t.n
            pr.emit("vector", lambda e, i=i: e.tensor_copy(
                        logit_all[:, i * NE:(i + 1) * NE], pb[6][:, 0:NE]),
                    waits=[(c_t, lg_done)])

        pr.wait("vector", s_x, const_x)
        for i in range(NT):
            lg = logit_all[:, i * NE:(i + 1) * NE]
            pr.emit("vector", lambda e, lg=lg: e.reduce_max(mxc[:, :], lg, axis=AX.X))
            pr.emit("vector", lambda e, lg=lg: e.tensor_single_scalar(seqc[:, :], lg, mxc[:, :], ALU.is_equal))
            pr.emit("vector", lambda e: e.scalar_tensor_tensor(t16c[:, :], seqc[:, :], -1e4,
                                                               k_iob[:, :], op0=ALU.mult, op1=ALU.add))
            pr.emit("vector", lambda e: e.tensor_reduce(idxc[:, :], t16c[:, :], axis=AX.X, op=ALU.min))
            pr.emit("vector", lambda e: e.tensor_single_scalar(ohc[:, :], k_iop[:, :], idxc[:, :], ALU.is_equal))
            oh_done = c_v.n
            mx_done = c_v.n
            pr.emit("scalar", lambda e, i=i: e.activation(sval[:, i:i + 1], mxc[:, :], AF.Sigmoid),
                    waits=[(c_v, mx_done)])
            s_done = c_s.n
            def posmm(e, i=i):
                e.matmul(pb[0][:, 0:NE], k_us[:, :], ohc[:, :], start=True, stop=False)
                return e.matmul(pb[0][:, 0:NE], k_onesc3[0:1, :], carry16[:, :], start=False, stop=True)
            pr.emit("tensor", posmm, waits=[(c_v, oh_done)], inc=(c_t, 1))
            pos_done = c_t.n
            def c16mm(e, i=i):
                return e.matmul(pb[1][0:1, 0:NE], k_ones128[:, :], ohc[:, :], start=True, stop=True)
            pr.emit("tensor", c16mm, inc=(c_t, 1))
            c16_done = c_t.n
            pr.emit("vector", lambda e: e.tensor_add(carry16[:, :], carry16[:, :], pb[1][0:1, 0:NE]),
                    waits=[(c_t, c16_done)])
            pr.emit("vector", lambda e: e.tensor_tensor(poshc[:, :], pb[0][:, 0:NE], ohc[:, :], ALU.mult),
                    waits=[(c_t, pos_done)])
            pr.emit("vector", lambda e: e.reduce_sum(ptc[:, :], poshc[:, :], axis=AX.X))
            pr.emit("vector", lambda e: e.scalar_tensor_tensor(
                        slf[:, :], idxc[:, :], float(CAP), ptc[:, :], op0=ALU.mult, op1=ALU.add))
            pr.emit("vector", lambda e: e.tensor_scalar(gec[:, :], ptc[:, :], float(CAP), None, op0=ALU.is_ge))
            pr.emit("vector", lambda e, i=i: e.tensor_sub(slz[:, :], k_garb[:, i:i + 1], slf[:, :]))
            pr.emit("vector", lambda e: e.tensor_mul(slz[:, :], slz[:, :], gec[:, :]))
            pr.emit("vector", lambda e: e.tensor_add(slsf[:, :], slf[:, :], slz[:, :]))
            pr.emit("vector", lambda e: e.tensor_scalar(slz[:, :], slf[:, :], -1.0, float(ZROW),
                                                        op0=ALU.mult, op1=ALU.add))
            pr.emit("vector", lambda e: e.tensor_mul(slz[:, :], slz[:, :], gec[:, :]))
            pr.emit("vector", lambda e: e.tensor_add(slgf[:, :], slf[:, :], slz[:, :]))
            pr.emit("vector", lambda e, i=i: e.tensor_copy(slotg[:, i:i + 1], slgf[:, :]))
            pr.emit("vector", lambda e: e.tensor_copy(s_u32[:, :], slsf[:, :]))
            pr.emit("vector", lambda e, i=i: e.tensor_single_scalar(
                        bfs(BO_HS, D), bfs(BO_XNTOK + i * D, D), sval[:, i:i + 1], ALU.mult),
                    waits=[(c_s, s_done)])
            hs_done = c_v.n
            pr.emit("gpsimd", lambda e: e.indirect_dma_start(
                        out=h_dram, out_offset=IndirectOffsetOnAxis(s_u32[:, 0:1], 0),
                        in_=bfs(BO_HS, D), in_offset=None),
                    waits=[(c_v, hs_done)], inc=(s_g, 16))
            pr.wait("vector", s_g)
        b_done_v = c_v.n

        wg_off = [BO_WG0, BO_WG1]
        wu_off = [BO_WU0, BO_WU1]
        wd_off = [BO_WD0, BO_WD1]
        grp_done_t = {}
        for g in range(4):
            bi = g % 2
            if bi in grp_done_t:
                pr.wait("gpsimd", c_t, grp_done_t[bi])
            pr.emit("gpsimd", lambda e, g=g, bi=bi: e.dma_start(
                        out=bfs(wg_off[bi], 4096).rearrange("p (c f) -> p c f", c=8),
                        in_=wg_t[:, g * 512:(g + 1) * 512].rearrange("(c p) f -> p c f", p=128)),
                    inc=(s_g, 16))
            pr.emit("gpsimd", lambda e, g=g, bi=bi: e.dma_start(
                        out=bfs(wu_off[bi], 4096).rearrange("p (c f) -> p c f", c=8),
                        in_=wu_t[:, g * 512:(g + 1) * 512].rearrange("(c p) f -> p c f", p=128)),
                    inc=(s_g, 16))
            pr.emit("gpsimd", lambda e, g=g, bi=bi: e.dma_start(
                        out=bfs(wd_off[bi], 4096).rearrange("p (c h) -> p c h", c=4),
                        in_=wd_t[g * 512:(g + 1) * 512, :].rearrange("(c p) h -> p c h", p=128)),
                    inc=(s_g, 16))
            gw = s_g.n
            for tb in range(8):
                for ft in range(4):
                    def gatemm(e, bi=bi, tb=tb, ft=ft):
                        for j in range(8):
                            ins = e.matmul(pb[4][:, 0:256],
                                           bfs(wg_off[bi] + j * 512 + ft * 128, 128),
                                           xn_fm[:, j * T_OWN + tb * 256: j * T_OWN + (tb + 1) * 256],
                                           start=(j == 0), stop=(j == 7))
                        return ins
                    w = [(s_g, gw)]
                    if g == 0 and tb == 0 and ft == 0:
                        w.append((c_v, b_done_v))
                    pr.emit("tensor", gatemm, waits=w, inc=(c_t, 1))
                    gmm = c_t.n
                    def upmm(e, bi=bi, tb=tb, ft=ft):
                        for j in range(8):
                            ins = e.matmul(pb[5][:, 0:256],
                                           bfs(wu_off[bi] + j * 512 + ft * 128, 128),
                                           xn_fm[:, j * T_OWN + tb * 256: j * T_OWN + (tb + 1) * 256],
                                           start=(j == 0), stop=(j == 7))
                        return ins
                    pr.emit("tensor", upmm, inc=(c_t, 1))
                    umm = c_t.n
                    pr.emit("scalar", lambda e: e.activation(f32s(FO_SG, 256), pb[4][:, 0:256], AF.Silu),
                            waits=[(c_t, gmm)])
                    sgd = c_s.n
                    pr.emit("vector", lambda e, ft=ft: e.tensor_tensor(
                                bfs(BO_ACT + ft * 256, 256), f32s(FO_SG, 256), pb[5][:, 0:256], ALU.mult),
                            waits=[(c_t, umm), (c_s, sgd)])
                    act_done = c_v.n
                    def downmm(e, bi=bi, ft=ft):
                        for tt in range(2):
                            for nh in range(2):
                                ins = e.matmul(pb[tt * 2 + nh][:, :],
                                               bfs(BO_ACT + ft * 256 + tt * 128, 128),
                                               bfs(wd_off[bi] + ft * 1024 + nh * 512, 512),
                                               start=(ft == 0), stop=(ft == 3))
                        return ins
                    pr.emit("tensor", downmm, waits=[(c_v, act_done)], inc=(c_t, 1))
                dn = c_t.n
                grp_done_t[bi] = dn
                for tt in range(2):
                    ti = tb * 2 + tt
                    for nh in range(2):
                        pr.emit("vector", lambda e, ti=ti, tt=tt, nh=nh: e.tensor_add(
                                    acc[:, ti * D + nh * 512: ti * D + (nh + 1) * 512],
                                    acc[:, ti * D + nh * 512: ti * D + (nh + 1) * 512],
                                    pb[tt * 2 + nh][:, :]),
                                waits=[(c_t, dn)])
                yadd = c_v.n
                pr.wait("tensor", c_v, yadd)

        if debug:
            pr.wait("sync", c_v, c_v.n)
            pr.emit("sync", lambda e: e.dma_start(out=dbg["d_accC"], in_=acc[:, :]), inc=(s_st, 16))

        pr.wait("gpsimd", c_t, c_t.n)
        pr.emit("gpsimd", lambda e: e.dma_start(
                    out=bfs(BO_WK, 8192).rearrange("p (c n) -> p c n", c=8),
                    in_=wk_t.rearrange("(c p) n -> p c n", p=128)), inc=(s_g, 16))
        pr.emit("gpsimd", lambda e: e.dma_start(
                    out=bfs(BO_WV, 8192).rearrange("p (c n) -> p c n", c=8),
                    in_=wv_t.rearrange("(c p) n -> p c n", p=128)), inc=(s_g, 16))
        kv_w = s_g.n
        pr.wait("sync", c_v, c_v.n)
        for j in range(8):
            xo = FO_XP0 if j % 2 == 0 else FO_XP1
            if j >= 2:
                pr.wait("sync", c_t)
            pr.emit("sync", lambda e, j=j, xo=xo: e.dma_start(
                        out=f32s(xo, D), in_=wo_t[j * 128:(j + 1) * 128, :]), inc=(s_x, 16))
            ww = s_x.n
            pr.emit("vector", lambda e, j=j, xo=xo: e.tensor_single_scalar(
                        bfs(BO_WOG + j * 1024, D), f32s(xo, D), k_lng[:, j:j + 1], ALU.mult),
                    waits=[(s_x, ww)])
            wsc = c_v.n
            def rowmm2(e, j=j, xo=xo):
                e.matmul(pb[6][0:1, :], k_lnb[:, j:j + 1], f32s(xo, 512),
                         start=(j == 0), stop=(j == 7))
                return e.matmul(pb[2][0:1, :], k_lnb[:, j:j + 1], f32s(xo + 512, 512),
                                start=(j == 0), stop=(j == 7))
            pr.emit("tensor", rowmm2, waits=[(c_v, wsc)], inc=(c_t, 1))
        row_done = c_t.n
        pr.emit("vector", lambda e: e.tensor_add(rowsB[0:1, 0:512], rowsB[32:33, 0:512], pb[6][0:1, :]),
                waits=[(c_t, row_done)])
        pr.emit("vector", lambda e: e.tensor_add(rowsB[0:1, 512:1024], rowsB[32:33, 512:1024], pb[2][0:1, :]))
        pr.emit("vector", lambda e: e.tensor_copy(rowsC[0:1, :], rowsB[0:1, :]))
        if debug:
            pr.wait("sync", c_v, c_v.n)
            pr.emit("sync", lambda e: e.dma_start(out=dbg["d_row"], in_=rowsB[0:1, :]), inc=(s_st, 16))

        QTAB = {
            0: [(1, 0, 0), (-1, 1, 1), (-1, 2, 2), (-1, 3, 3)],
            1: [(1, 0, 1), (1, 1, 0), (1, 2, 3), (-1, 3, 2)],
            2: [(1, 0, 2), (-1, 1, 3), (1, 2, 0), (1, 3, 1)],
            3: [(1, 0, 3), (1, 1, 2), (-1, 2, 1), (1, 3, 0)],
        }

        def quat_emit(out_off, a_off, b_off, conj_b, pool_comps, waits_v=(), waits_p=()):
            wv = list(waits_v)
            wp = list(waits_p)
            NQ = D // 4
            for comp in range(4):
                eng = "gpsimd" if comp in pool_comps else "vector"
                t0, t1 = (qp0, qp1) if eng == "gpsimd" else (qv0, qv1)
                terms = QTAB[comp]
                if conj_b:
                    terms = [(s * (-1 if b != 0 else 1), a, b) for (s, a, b) in terms]
                terms = sorted(terms, key=lambda t: -t[0])
                ov = pf[:, out_off + comp * NQ:out_off + (comp + 1) * NQ]
                w = wp if eng == "gpsimd" else wv
                for n_, (s, a, b) in enumerate(terms):
                    av = pf[:, a_off + a * NQ:a_off + (a + 1) * NQ]
                    bv = pf[:, b_off + b * NQ:b_off + (b + 1) * NQ]
                    if n_ == 0:
                        pr.emit(eng, lambda e, av=av, bv=bv, t0=t0: e.tensor_mul(t0[:, :], av, bv),
                                waits=tuple(w))
                        if eng == "gpsimd":
                            wp = []
                        else:
                            wv = []
                    else:
                        pr.emit(eng, lambda e, av=av, bv=bv, t1=t1: e.tensor_mul(t1[:, :], av, bv))
                        dst = ov if n_ == 3 else t0[:, :]
                        op = ALU.add if s > 0 else ALU.subtract
                        pr.emit(eng, lambda e, dst=dst, t0=t0, t1=t1, op=op:
                                e.tensor_tensor(dst, t0[:, :], t1[:, :], op))
            return c_v.n, c_p.n

        FO_LN2 = 3072

        st_state = {"cons_v": 0, "cons_p": 0, "evac_s": c_s.n, "tot_add_v": 0,
                    "pbank_v": c_v.n, "dump_st": 0, "prod_mm": 0}

        def emit_kv_mm(fm_fn, extra_wait):
            def kmm(e):
                for j in range(8):
                    for nh in range(2):
                        e.matmul(pb[nh][:, :], fm_fn(j),
                                 bfs(BO_WK + j * 1024 + nh * 512, 512),
                                 start=(j == 0), stop=False)
                for nh in range(2):
                    ins = e.matmul(pb[nh][:, :], k_onesb3[32:33, :],
                                   rowsC[32:33, nh * 512:(nh + 1) * 512],
                                   start=False, stop=True)
                return ins
            pr.emit("tensor", kmm,
                    waits=[(s_g, kv_w), (c_s, st_state["evac_s"])] + list(extra_wait),
                    inc=(c_t, 1))
            def vmm(e):
                for j in range(8):
                    for nh in range(2):
                        e.matmul(pb[2 + nh][:, :], fm_fn(j),
                                 bfs(BO_WV + j * 1024 + nh * 512, 512),
                                 start=(j == 0), stop=False)
                for nh in range(2):
                    ins = e.matmul(pb[2 + nh][:, :], k_onesb3[64:65, :],
                                   rowsC[64:65, nh * 512:(nh + 1) * 512],
                                   start=False, stop=True)
                return ins
            pr.emit("tensor", vmm, inc=(c_t, 1))
            return c_t.n

        def emit_kv_evac(v_done):
            w = [(c_t, v_done), (c_v, st_state["cons_v"]), (c_p, st_state["cons_p"])]
            if debug and st_state["dump_st"]:
                w.append((s_st, st_state["dump_st"]))
                st_state["dump_st"] = 0
            pr.emit("scalar", lambda e: e.copy(f32s(FO_KS, 512), pb[0][:, :]), waits=w)
            pr.emit("scalar", lambda e: e.copy(f32s(FO_KS + 512, 512), pb[1][:, :]))
            pr.emit("scalar", lambda e: e.copy(f32s(FO_VS, 512), pb[2][:, :]))
            pr.emit("scalar", lambda e: e.copy(f32s(FO_VS + 512, 512), pb[3][:, :]))
            st_state["evac_s"] = c_s.n

        NQ = D // 4

        def emit_knorm():
            pr.emit("scalar", lambda e: e.activation(f32s(FO_LN2, D), f32s(FO_KS, D), AF.Square))
            sq_done = c_s.n
            pr.emit("vector", lambda e: e.tensor_tensor(sq2w[:, :], f32s(FO_LN2, 512),
                                                        f32s(FO_LN2 + 512, 512), ALU.add),
                    waits=[(c_s, sq_done)])
            pr.emit("vector", lambda e: e.tensor_tensor(ssq[:, :], sq2w[:, 0:256],
                                                        sq2w[:, 256:512], ALU.add))
            ss_done = c_v.n
            pr.emit("scalar", lambda e: e.activation(snr[:, :], ssq[:, :], AF.Sqrt,
                                                     bias=k_eps24[:, :], scale=1.0),
                    waits=[(c_v, ss_done)])
            nrm_done = c_s.n
            pr.emit("vector", lambda e: e.reciprocal(src_[:, :], snr[:, :]),
                    waits=[(c_s, nrm_done)])
            rs_done = c_v.n
            for c in range(4):
                eng = "vector" if c < 2 else "gpsimd"
                w = [(c_v, rs_done)] if c == 2 else []
                pr.emit(eng, lambda e, c=c: e.tensor_tensor(
                            pf[:, FO_KN + c * NQ:FO_KN + (c + 1) * NQ],
                            pf[:, FO_KS + c * NQ:FO_KS + (c + 1) * NQ],
                            src_[:, :], ALU.mult), waits=w)
            return c_v.n, c_p.n

        def emit_knorm_bound(parity):
            kn_v, kn_p = emit_knorm()
            pool_comps = (1, 2, 3) if parity == 0 else (2, 3)
            return quat_emit(FO_BD, FO_VS, FO_KN, False, pool_comps,
                             waits_v=[(c_v, kn_v), (c_p, kn_p)],
                             waits_p=[(c_v, kn_v), (c_p, kn_p)])

        def emit_tot_carry(lhs_col):
            for nh in range(2):
                def totmm(e, nh=nh, lhs_col=lhs_col):
                    return e.matmul(pb[6][0:1, :], lhs_col,
                                    f32s(FO_BD + nh * 512, 512), start=True, stop=True)
                w = [(c_v, st_state["cons_v"]), (c_p, st_state["cons_p"]),
                     (c_v, st_state["tot_add_v"])] if nh == 0 else [(c_v, c_v.n)]
                pr.emit("tensor", totmm, waits=w, inc=(c_t, 1))
                td = c_t.n
                pr.emit("vector", lambda e, nh=nh: e.tensor_add(
                            rowsA[0:1, nh * 512:(nh + 1) * 512],
                            rowsA[0:1, nh * 512:(nh + 1) * 512], pb[6][0:1, :]),
                        waits=[(c_t, td)])
            st_state["tot_add_v"] = c_v.n

        PSLOT = 1024

        def emit_tail_D(j8):
            kn_v, kn_p = emit_knorm()
            mcol = {1: k_mask[:, j8:j8 + 1], -1: k_maskn[:, j8:j8 + 1]}
            for half, (bank, comps) in enumerate([(pb[6], (0, 1)), (pb[5], (2, 3))]):
                first_v = first_p = True
                for ci, comp in enumerate(comps):
                    for n_, (s, a, b) in enumerate(QTAB[comp]):
                        slot = ci * 4 + n_
                        eng = "vector" if slot % 2 == 0 else "gpsimd"
                        w = []
                        if eng == "vector" and first_v:
                            w = [(c_v, kn_v), (c_p, kn_p), (c_t, st_state["prod_mm"])]
                            first_v = False
                        elif eng == "gpsimd" and first_p:
                            w = [(c_v, kn_v), (c_p, kn_p), (c_t, st_state["prod_mm"])]
                            first_p = False
                        pr.emit(eng, lambda e, a=a, b=b, slot=slot: e.tensor_tensor(
                                    f32s(PSLOT + slot * 256, 256),
                                    pf[:, FO_VS + a * NQ:FO_VS + (a + 1) * NQ],
                                    pf[:, FO_KN + b * NQ:FO_KN + (b + 1) * NQ],
                                    ALU.mult), waits=w)
                pv, pp = c_v.n, c_p.n

                def prodmm(e, bank=bank, comps=comps):
                    for ci, comp in enumerate(comps):
                        for n_, (s, a, b) in enumerate(QTAB[comp]):
                            slot = ci * 4 + n_
                            ins = e.matmul(bank[0:1, ci * 256:(ci + 1) * 256],
                                           mcol[1 if s > 0 else -1],
                                           f32s(PSLOT + slot * 256, 256),
                                           start=(n_ == 0), stop=(n_ == 3))
                    return ins
                pr.emit("tensor", prodmm,
                        waits=[(c_v, pv), (c_p, pp), (c_v, st_state["tot_add_v"])],
                        inc=(c_t, 1))
                st_state["prod_mm"] = c_t.n
            st_state["cons_v"], st_state["cons_p"] = pv, pp
            mm_done = st_state["prod_mm"]
            pr.emit("vector", lambda e: e.tensor_add(
                        rowsA[0:1, 0:512], rowsA[0:1, 0:512], pb[6][0:1, :]),
                    waits=[(c_t, mm_done)])
            pr.emit("vector", lambda e: e.tensor_add(
                        rowsA[0:1, 512:1024], rowsA[0:1, 512:1024], pb[5][0:1, :]))
            st_state["tot_add_v"] = c_v.n

        def emit_tail_E(i):
            bd_v, bd_p = emit_knorm_bound(i & 1)
            st_state["cons_v"], st_state["cons_p"] = bd_v, bd_p
            def memmm(e, i=i):
                for nh in range(2):
                    e.matmul(pb[4 + nh][:, :], k_ui[:, :], f32s(FO_BD + nh * 512, 512),
                             start=True, stop=False)
                    ins = e.matmul(pb[4 + nh][:, :], k_onesc3[0:1, :],
                                   rowsA[0:1, nh * 512:(nh + 1) * 512],
                                   start=False, stop=True)
                return ins
            pr.emit("tensor", memmm, waits=[(c_v, bd_v), (c_p, bd_p)], inc=(c_t, 1))
            mem_done = c_t.n
            pr.emit("vector", lambda e, i=i: e.tensor_single_scalar(
                        f32s(FO_MS, 512), pb[4][:, :], k_rsq[:, i:i + 1], ALU.mult),
                    waits=[(c_t, mem_done)])
            pr.emit("vector", lambda e, i=i: e.tensor_single_scalar(
                        f32s(FO_MS + 512, 512), pb[5][:, :], k_rsq[:, i:i + 1], ALU.mult))
            ms_done = c_v.n
            emit_tot_carry(k_ones128[:, :])
            if debug and i == 0:
                pr.wait("sync", c_v, c_v.n)
                pr.wait("sync", c_p, c_p.n)
                pr.emit("sync", lambda e: e.dma_start(out=dbg["d_K"], in_=f32s(FO_KS, D)), inc=(s_st, 16))
                pr.emit("sync", lambda e: e.dma_start(out=dbg["d_kn"], in_=f32s(FO_KN, D)), inc=(s_st, 16))
                pr.emit("sync", lambda e: e.dma_start(out=dbg["d_bound"], in_=f32s(FO_BD, D)), inc=(s_st, 16))
                pr.emit("sync", lambda e: e.dma_start(out=dbg["d_mem"], in_=f32s(FO_MS, D)), inc=(s_st, 16))
                st_state["dump_st"] = s_st.n
            rt_v, rt_p = quat_emit(FO_ZSC, FO_MS, FO_KN, True, (1, 2, 3) if (i & 1) else (2, 3),
                                   waits_v=[(c_v, ms_done)], waits_p=[(c_v, ms_done)])
            pr.wait("vector", c_p, rt_p)
            pr.wait("vector", c_v, rt_v)
            if debug and i == 0:
                pr.wait("sync", c_v, c_v.n)
                pr.emit("sync", lambda e: e.dma_start(out=dbg["d_retr"], in_=f32s(FO_ZSC, D)), inc=(s_st, 16))
                st_state["dump_st"] = s_st.n
            emit_ln(f32s(FO_ZSC, D), bfs(BO_ZB, D), 0)
            zb_done = c_v.n
            pr.wait("tensor", c_v, zb_done)
            def tpz(e):
                for j in range(8):
                    ins = e.transpose(pbank_bf[:, j * 128:(j + 1) * 128],
                                      bfs(BO_ZB + j * 128, 128), k_ident_bf[:, :])
                return ins
            pr.emit("tensor", tpz, inc=(c_t, 1))
            tzd = c_t.n
            pr.emit("vector", lambda e: e.tensor_copy(bfs(BO_ZFM, D), pbank_bf[:, :]),
                    waits=[(c_t, tzd)])
            zfm_done = c_v.n
            st_state["pbank_v"] = zfm_done
            def womm(e, i=i):
                for j in range(8):
                    for nh in range(2):
                        e.matmul(pb[4 + nh][:, :], bfs(BO_ZFM + j * 128, 128),
                                 bfs(BO_WOG + j * 1024 + nh * 512, 512),
                                 start=(j == 0), stop=False)
                for nh in range(2):
                    ins = e.matmul(pb[4 + nh][:, :], k_onesb3[0:1, :],
                                   rowsC[0:1, nh * 512:(nh + 1) * 512],
                                   start=False, stop=True)
                return ins
            pr.emit("tensor", womm, waits=[(c_v, zfm_done)], inc=(c_t, 1))
            wo_done = c_t.n
            for nh in range(2):
                pr.emit("vector", lambda e, i=i, nh=nh: e.tensor_add(
                            acc[:, i * D + nh * 512: i * D + (nh + 1) * 512],
                            acc[:, i * D + nh * 512: i * D + (nh + 1) * 512],
                            pb[4 + nh][:, :]),
                        waits=[(c_t, wo_done)])

        ln_done_hist = {}
        pending = None
        for step in range(NPT + NT):
            if step < NPT:
                j8 = step
                xo = FO_XP0
                if j8 >= 1:
                    pr.wait("sync", c_v, ln_done_hist[j8 - 1])
                else:
                    pr.wait("sync", c_t, row_done)
                pr.emit("sync", lambda e, j8=j8, xo=xo: e.dma_start(
                            out=f32s(xo, D), in_=x_pre[j8 * 128:(j8 + 1) * 128, :]), inc=(s_x, 16))
                xw = s_x.n
                ln_done_hist[j8] = emit_ln(f32s(xo, D), bfs(BO_XNPFM, D), xw)
                xnb_done = c_v.n
                pr.wait("tensor", c_v, max(xnb_done, st_state["pbank_v"]))
                def tpp(e):
                    for j in range(8):
                        ins = e.transpose(pbank_bf[:, j * 128:(j + 1) * 128],
                                          bfs(BO_XNPFM + j * 128, 128), k_ident_bf[:, :])
                    return ins
                pr.emit("tensor", tpp, inc=(c_t, 1))
                tpd = c_t.n
                pr.emit("vector", lambda e: e.tensor_copy(bfs(BO_XNPFM, D), pbank_bf[:, :]),
                        waits=[(c_t, tpd)])
                fme = c_v.n
                st_state["pbank_v"] = fme
                v_done = emit_kv_mm(lambda j: bfs(BO_XNPFM + j * 128, 128), [(c_v, fme)])
            else:
                i = step - NPT
                v_done = emit_kv_mm(
                    lambda j, i=i: xn_fm[:, j * T_OWN + i * 128: j * T_OWN + (i + 1) * 128], [])
            if pending is not None:
                (emit_tail_D if pending[0] == "D" else emit_tail_E)(pending[1])
                if debug and step == NPT:
                    pr.wait("sync", c_v, c_v.n)
                    pr.emit("sync", lambda e: e.dma_start(out=dbg["d_carryD"], in_=rowsA[0:1, :]), inc=(s_st, 16))
                    st_state["dump_st"] = s_st.n
            emit_kv_evac(v_done)
            pending = ("D", j8) if step < NPT else ("E", step - NPT)
        (emit_tail_D if pending[0] == "D" else emit_tail_E)(pending[1])
        if debug:
            pr.wait("sync", c_v, c_v.n)
            pr.emit("sync", lambda e: e.dma_start(out=dbg["d_carryE"], in_=rowsA[0:1, :]), inc=(s_st, 16))
            pr.emit("sync", lambda e: e.dma_start(out=dbg["d_accE"], in_=acc[:, :]), inc=(s_st, 16))
            prc = sb("prc", [128, D])
            pr.emit("vector", lambda e: e.tensor_copy(prc[:, :], rowsC[:, :]))
            pr.wait("sync", c_v, c_v.n)
            pr.emit("sync", lambda e: e.dma_start(out=dbg["d_rowsC"], in_=prc[:, :]), inc=(s_st, 16))

        nonlocal_dn = [0, 0]
        dn_sg = [0, 0]
        dwn_off = [BO_DWN0, BO_DWN1]
        def emit_down(bi, g, ft, mult_v):
            def downmm(e, bi=bi, g=g, ft=ft):
                for tt in range(2):
                    for nh in range(2):
                        ins = e.matmul(pb[tt * 2 + nh][:, :],
                                       bfs(BO_FACT + ft * 256 + tt * 128, 128),
                                       bfs(dwn_off[bi] + ft * 1024 + nh * 512, 512),
                                       start=(g == 0 and ft == 0), stop=(g == 3 and ft == 3))
                return ins
            pr.emit("tensor", downmm, waits=[(c_v, mult_v), (s_g, dn_sg[bi])], inc=(c_t, 1))
            nonlocal_dn[bi] = c_t.n

        gup_off = [BO_GUP0, BO_GUP1]
        pr.wait("gpsimd", c_t, c_t.n)
        gu_done = {}
        ystore_st = 0
        hload_w = None
        for e_i in range(NE):
            if e_i == 0:
                pr.emit("sync", lambda e: e.dma_start(
                            out=bfs(BO_HTOK, 2048).rearrange("p (tt d) -> p tt d", tt=2),
                            in_=h_dram[0:CAP, :].rearrange("(tt p) d -> p tt d", p=128)),
                        waits=[(s_g, s_g.n), (c_t, c_t.n)], inc=(s_x, 16))
                hload_w = s_x.n
            hw = hload_w
            for tt in range(2):
                def tph(e, tt=tt):
                    for j in range(8):
                        ins = e.transpose(pbank_bf[:, j * 128:(j + 1) * 128],
                                          bfs(BO_HTOK + tt * 1024 + j * 128, 128), k_ident_bf[:, :])
                    return ins
                pr.emit("tensor", tph, waits=[(s_x, hw), (c_v, c_v.n)], inc=(c_t, 1))
                tphd = c_t.n
                pr.emit("vector", lambda e, tt=tt: e.tensor_copy(
                            bfs(BO_HFM, 2048).rearrange("p (c t) -> p c t", c=8)[:, :, tt * 128:(tt + 1) * 128],
                            pbank_bf[:, :].rearrange("p (c t) -> p c t", c=8)[:, :, :]),
                        waits=[(c_t, tphd)])
            hfm_done = c_v.n
            st_state["pbank_v"] = hfm_done
            if e_i + 1 < NE:
                pr.emit("sync", lambda e, e_i=e_i: e.dma_start(
                            out=bfs(BO_HTOK, 2048).rearrange("p (tt d) -> p tt d", tt=2),
                            in_=h_dram[(e_i + 1) * CAP:(e_i + 2) * CAP, :].rearrange("(tt p) d -> p tt d", p=128)),
                        waits=[(c_t, tphd)], inc=(s_x, 16))
                hload_w = s_x.n
            mult_hist = {}
            for g in range(4):
                bi = g % 2
                wgu = [(c_t, gu_done[bi])] if bi in gu_done else []
                pr.emit("gpsimd", lambda e, e_i=e_i, g=g, bi=bi: e.dma_start(
                            out=bfs(gup_off[bi], 8192).rearrange("p (c f) -> p c f", c=8)[:, :, 0:512],
                            in_=gup[e_i, :, g * 512:(g + 1) * 512].rearrange("(c p) f -> p c f", p=128)),
                        waits=wgu, inc=(s_g, 16))
                pr.emit("gpsimd", lambda e, e_i=e_i, g=g, bi=bi: e.dma_start(
                            out=bfs(gup_off[bi], 8192).rearrange("p (c f) -> p c f", c=8)[:, :, 512:1024],
                            in_=gup[e_i, :, FF + g * 512: FF + (g + 1) * 512].rearrange("(c p) f -> p c f", p=128)),
                        inc=(s_g, 16))
                pr.emit("gpsimd", lambda e, e_i=e_i, g=g, bi=bi: e.dma_start(
                            out=bfs(dwn_off[bi], 4096).rearrange("p (c h) -> p c h", c=4),
                            in_=dwn[e_i, g * 512:(g + 1) * 512, :].rearrange("(c p) h -> p c h", p=128)),
                        waits=[(c_t, nonlocal_dn[bi])], inc=(s_g, 16))
                dn_sg[bi] = s_g.n
                gw = s_g.n
                for ft in range(4):
                    gbank = pb[4] if ft % 2 == 0 else pb[6]
                    def gatemm(e, bi=bi, ft=ft, gbank=gbank):
                        for j in range(8):
                            ins = e.matmul(gbank[:, 0:256],
                                           bfs(gup_off[bi] + j * 1024 + ft * 128, 128),
                                           bfs(BO_HFM + j * 256, 256),
                                           start=(j == 0), stop=(j == 7))
                        return ins
                    w = [(s_g, gw)]
                    if g == 0 and ft == 0:
                        w.append((c_v, hfm_done))
                    pr.emit("tensor", gatemm, waits=w, inc=(c_t, 1))
                    gmm = c_t.n
                    wup = ([(c_v, mult_hist[ft - 1])] if ft > 0 else
                           ([(c_v, mult_hist[3])] if g > 0 else [(c_v, hfm_done)]))
                    def upmm(e, bi=bi, ft=ft):
                        for j in range(8):
                            ins = e.matmul(pb[5][:, 0:256],
                                           bfs(gup_off[bi] + j * 1024 + 512 + ft * 128, 128),
                                           bfs(BO_HFM + j * 256, 256),
                                           start=(j == 0), stop=(j == 7))
                        return ins
                    pr.emit("tensor", upmm, waits=wup, inc=(c_t, 1))
                    umm = c_t.n
                    if ft > 0:
                        emit_down(bi, g, ft - 1, mult_hist[ft - 1])
                    pr.emit("scalar", lambda e, gbank=gbank: e.activation(
                                f32s(FO_SG, 256), gbank[:, 0:256], AF.Silu),
                            waits=[(c_t, gmm)])
                    sgd = c_s.n
                    pr.emit("vector", lambda e, ft=ft: e.tensor_tensor(
                                bfs(BO_FACT + ft * 256, 256), f32s(FO_SG, 256), pb[5][:, 0:256], ALU.mult),
                            waits=[(c_t, umm), (c_s, sgd), (c_t, max(nonlocal_dn))])
                    mult_hist[ft] = c_v.n
                emit_down(bi, g, 3, mult_hist[3])
                gu_done[bi] = c_t.n
            ydn = max(nonlocal_dn)
            for tt in range(2):
                for nh in range(2):
                    w = [(c_t, ydn)]
                    if tt == 0 and nh == 0 and ystore_st:
                        w.append((s_st, ystore_st))
                    pr.emit("vector", lambda e, tt=tt, nh=nh: e.tensor_copy(
                                f32s(FO_YSB + tt * 1024 + nh * 512, 512), pb[tt * 2 + nh][:, :]),
                            waits=w)
            yev = c_v.n
            pr.emit("sync", lambda e, e_i=e_i: e.dma_start(
                        out=y_dram[e_i * CAP:(e_i + 1) * CAP, :].rearrange("(tt p) d -> p tt d", p=128),
                        in_=f32s(FO_YSB, 2048).rearrange("p (tt d) -> p tt d", tt=2)),
                    waits=[(c_v, yev)], inc=(s_st, 16))
            ystore_st = s_st.n
        if debug:
            pr.wait("sync", s_st, s_st.n)
            pr.emit("sync", lambda e: e.dma_start(out=dbg["d_ydram"], in_=y_dram), inc=(s_st, 16))

        y_all_st = s_st.n
        for i in range(NT):
            pr.emit("gpsimd", lambda e, i=i: e.indirect_dma_start(
                        out=f32s(FO_YTOK, D), out_offset=None,
                        in_=y_dram, in_offset=IndirectOffsetOnAxis(slotg[:, i:i + 1], 0)),
                    waits=[(s_st, y_all_st)], inc=(s_g, 16))
            gy = s_g.n
            pr.emit("vector", lambda e, i=i: e.tensor_add(
                        f32s(FO_OT, D), acc[:, i * D:(i + 1) * D], f32s(FO_YTOK, D)),
                    waits=[(s_g, gy)])
            ot_done = c_v.n
            pr.emit("sync", lambda e, i=i: e.dma_start(
                        out=out[i * 128:(i + 1) * 128, :], in_=f32s(FO_OT, D)),
                    waits=[(c_v, ot_done)], inc=(s_st, 16))
            pr.wait("gpsimd", s_st)
            pr.wait("vector", s_st)

        if debug:
            pr.wait("sync", c_v, c_v.n)
            pr.wait("sync", s_g, s_g.n)
            pr.emit("sync", lambda e: e.dma_start(out=dbg["d_xnfm"], in_=xn_fm[:, :]), inc=(s_st, 16))
            pr.emit("sync", lambda e: e.dma_start(out=dbg["d_logits"], in_=logit_all[:, :]), inc=(s_st, 16))
            pr.emit("sync", lambda e: e.dma_start(out=dbg["d_sval"], in_=sval[:, :]), inc=(s_st, 16))
            pr.emit("sync", lambda e: e.dma_start(out=dbg["d_carry16"], in_=carry16[:, :]), inc=(s_st, 16))
            pr.emit("sync", lambda e: e.dma_start(out=dbg["d_hdram"], in_=h_dram), inc=(s_st, 16))
            sg32 = sb("sg32", [128, NT])
            pr.emit("vector", lambda e: e.tensor_copy(sg32[:, :], slotg[:, :]))
            pr.wait("sync", c_v, c_v.n)
            pr.emit("sync", lambda e: e.dma_start(out=dbg["d_slots"], in_=sg32[:, :]), inc=(s_st, 16))

        pr.wait("sync", s_st, s_st.n)

        blk = es.enter_context(nc.Block())

        @blk.sync
        def _(eng):
            pr.replay("sync", eng)

        @blk.gpsimd
        def _(eng):
            pr.replay("gpsimd", eng)

        @blk.tensor
        def _(eng):
            pr.replay("tensor", eng)

        @blk.vector
        def _(eng):
            pr.replay("vector", eng)

        @blk.scalar
        def _(eng):
            pr.replay("scalar", eng)

    return nc



def stage_inputs(L0_attn_x, L0_attn_Wk, L0_attn_bk, L0_attn_Wv, L0_attn_bv,
                 L0_attn_ln_g, L0_attn_ln_b, L0_attn_Wo, L0_attn_bo,
                 L0_ffn_w_router, L0_ffn_gate_up_proj, L0_ffn_down_proj,
                 L0_ffn_w_gate, L0_ffn_w_up, L0_ffn_w_down):
    f32 = np.float32
    x = np.ascontiguousarray(np.asarray(L0_attn_x, dtype=f32).reshape(2, 8192, D))
    shared = {
        "wk_t": np.ascontiguousarray(np.asarray(L0_attn_Wk, f32).T[:, QPERM]),
        "wv_t": np.ascontiguousarray(np.asarray(L0_attn_Wv, f32).T[:, QPERM]),
        "wo_t": np.ascontiguousarray(np.asarray(L0_attn_Wo, f32).T[QPERM, :]),
        "wg_t": np.ascontiguousarray(np.asarray(L0_ffn_w_gate, f32).T),
        "wu_t": np.ascontiguousarray(np.asarray(L0_ffn_w_up, f32).T),
        "wd_t": np.ascontiguousarray(np.asarray(L0_ffn_w_down, f32).T),
        "wr_t": np.ascontiguousarray(np.asarray(L0_ffn_w_router, f32).T),
        "gup": np.ascontiguousarray(np.asarray(L0_ffn_gate_up_proj, f32)),
        "dwn": np.ascontiguousarray(np.asarray(L0_ffn_down_proj, f32)),
        "bk_r": np.ascontiguousarray(np.asarray(L0_attn_bk, f32)[QPERM].reshape(1, D)),
        "bv_r": np.ascontiguousarray(np.asarray(L0_attn_bv, f32)[QPERM].reshape(1, D)),
        "bo_r": np.asarray(L0_attn_bo, f32).reshape(1, D),
        "lnb_fm": np.ascontiguousarray(np.asarray(L0_attn_ln_b, f32)[QPERM].reshape(8, 128).T),
        "lng_fm": np.ascontiguousarray(np.asarray(L0_attn_ln_g, f32)[QPERM].reshape(8, 128).T),
        "ident": np.eye(128, dtype=f32),
        "u_incl": np.triu(np.ones((128, 128), f32)),
        "u_strict": np.triu(np.ones((128, 128), f32), 1),
        "ones_col": np.ones((1, 128), f32),
        "ones128": np.ones((128, 1), f32),
        "iota_big": np.tile(np.arange(NE, dtype=f32)[None, :] + 1e4, (128, 1)),
        "iota_pln": np.tile(np.arange(NE, dtype=f32)[None, :], (128, 1)),
    }
    in_maps = []
    for c in range(8):
        b, q = c // 4, c % 4
        own = np.ascontiguousarray(x[b, q * T_OWN:(q + 1) * T_OWN])
        pre = np.zeros((T_PRE, D), f32)
        pre[: q * T_OWN] = x[b, : q * T_OWN]
        mask = np.zeros((128, NPT), f32)
        mask[:, : q * NT] = 1.0
        gpos = (q * T_OWN + np.arange(T_OWN, dtype=np.float64) + 1.0)
        rsq = (1.0 / np.sqrt(gpos)).astype(f32).reshape(NT, 128).T
        garbv = (float(ZROW) + np.arange(T_OWN, dtype=f32)).reshape(NT, 128).T
        m = dict(shared)
        m.update({
            "x_own": own,
            "x_pre": pre,
            "mask_pre": np.ascontiguousarray(mask),
            "rsq_pos": np.ascontiguousarray(rsq),
            "garb": np.ascontiguousarray(garbv),
        })
        in_maps.append(m)
    return in_maps


_NC_CACHE = {}


def run(debug=0, **inputs):
    in_maps = stage_inputs(**inputs)
    if debug not in _NC_CACHE:
        _NC_CACHE[debug] = build_program(debug=debug)
    return run_bass_kernel_spmd(_NC_CACHE[debug], in_maps, core_ids=list(range(8)))


def kernel(**inputs):
    res = run(debug=0, **inputs)
    outs = [np.asarray(r["out"]) for r in res.results]
    full = np.empty((2, 8192, D), np.float32)
    for c in range(8):
        b, q = c // 4, c % 4
        full[b, q * T_OWN:(q + 1) * T_OWN] = outs[c]
    return full


# version 24 (final)
# speedup vs baseline: 1.3142x; 1.0680x over previous
import numpy as np
from contextlib import ExitStack

import concourse.bass as bass
import concourse.mybir as mybir
from concourse.bass_core_rust import IndirectOffsetOnAxis
from concourse.bass_utils import run_bass_kernel_spmd

F32 = mybir.dt.float32
BF16 = mybir.dt.bfloat16
U32 = mybir.dt.uint32
AF = mybir.ActivationFunctionType
ALU = mybir.AluOpType
AX = mybir.AxisListType

D = 1024
T_OWN = 2048
NT = T_OWN // 128
T_PRE = 6144
NPT = T_PRE // 128
CAP = 256
NE = 16
FF = 2048
H_ROWS = NE * CAP + T_OWN
Y_ROWS = NE * CAP + 1
ZROW = NE * CAP
LN_EPS = 1e-5

PBF_N = 29696
PF_N = 7168

QPERM = np.array([4 * q + c for c in range(4) for q in range(D // 4)])


class Sem:
    def __init__(self, h):
        self.h = h
        self.n = 0
    def inc(self, k):
        self.n += k
        return self.n


class Prog:
    def __init__(self):
        self.ops = []
        self.strict = {}

    def emit(self, engine, fn, waits=(), inc=None, relaxed=False):
        strict = self.strict.get(engine)
        w = list(waits)
        if strict is not None and not relaxed:
            w.append((strict, strict.n))
            if inc is None:
                inc = (strict, 1)
        w = tuple(w)
        if inc is not None:
            inc[0].inc(inc[1])
        def run(eng, w=w, fn=fn, inc=inc):
            for s, v in w:
                eng.wait_ge(s.h, v)
            ins = fn(eng)
            if inc is not None:
                ins.then_inc(inc[0].h, inc[1])
            return ins
        self.ops.append((engine, run))

    def wait(self, engine, sem, val=None):
        v = sem.n if val is None else val
        self.ops.append((engine, lambda eng, s=sem, v=v: eng.wait_ge(s.h, v)))

    def replay(self, engine, eng):
        for e, fn in self.ops:
            if e == engine:
                fn(eng)


def build_program(debug=0):
    nc = bass.Bass("TRN2", target_bir_lowering=False, debug=False)

    def di(name, shape, dt=F32):
        return nc.dram_tensor(name, shape, dt, kind="ExternalInput").ap()

    x_own = di("x_own", [T_OWN, D])
    x_pre = di("x_pre", [T_PRE, D])
    wk_t = di("wk_t", [D, D])
    wv_t = di("wv_t", [D, D])
    wo_t = di("wo_t", [D, D])
    wg_t = di("wg_t", [D, FF])
    wu_t = di("wu_t", [D, FF])
    wd_t = di("wd_t", [FF, D])
    wr_t = di("wr_t", [D, NE])
    gup = di("gup", [NE, D, 2 * FF])
    dwn = di("dwn", [NE, FF, D])
    bk_r = di("bk_r", [1, D])
    bv_r = di("bv_r", [1, D])
    bo_r = di("bo_r", [1, D])
    lnb_fm = di("lnb_fm", [128, 8])
    lng_fm = di("lng_fm", [128, 8])
    ident = di("ident", [128, 128])
    u_incl = di("u_incl", [128, 128])
    u_strict = di("u_strict", [128, 128])
    ones_col = di("ones_col", [1, 128])
    ones128 = di("ones128", [128, 1])
    iota_big = di("iota_big", [128, NE])
    iota_pln = di("iota_pln", [128, NE])
    rsq_pos = di("rsq_pos", [128, NT])
    mask_pre = di("mask_pre", [128, NPT])
    garb = di("garb", [128, NT])

    out = nc.dram_tensor("out", [T_OWN, D], F32, kind="ExternalOutput").ap()

    dbg = {}
    if debug:
        def do(name, shape, dt=F32):
            dbg[name] = nc.dram_tensor(name, shape, dt, kind="ExternalOutput").ap()
        do("d_xnfm", [128, 8 * T_OWN], BF16)
        do("d_logits", [128, NE * NT])
        do("d_slots", [128, NT])
        do("d_sval", [128, NT])
        do("d_carry16", [1, NE])
        do("d_carryD", [1, D])
        do("d_carryE", [1, D])
        do("d_hdram", [H_ROWS, D], BF16)
        do("d_ydram", [Y_ROWS, D])
        do("d_K", [128, D])
        do("d_kn", [128, D])
        do("d_bound", [128, D])
        do("d_mem", [128, D])
        do("d_retr", [128, D])
        do("d_accC", [128, NT * D])
        do("d_accE", [128, NT * D])
        do("d_rowsC", [128, D])
        do("d_row", [1, D])
        do("d_hfm", [128, 2048], BF16)
        do("d_fact", [128, 1024], BF16)
        do("d_gate", [128, 256])

    h_dram = nc.dram_tensor("h_dram", [H_ROWS, D], BF16, kind="Internal").ap()
    y_dram = nc.dram_tensor("y_dram", [Y_ROWS, D], F32, kind="Internal").ap()

    es = ExitStack()
    sb = lambda name, shape, dt=F32: es.enter_context(nc.sbuf_tensor(name, shape, dt))
    ps = lambda name, shape, dt=F32: es.enter_context(nc.psum_tensor(name, shape, dt))

    pr = Prog()
    with es:
        s_x = Sem(es.enter_context(nc.semaphore("s_x")))
        s_g = Sem(es.enter_context(nc.semaphore("s_g")))
        s_st = Sem(es.enter_context(nc.semaphore("s_st")))
        c_v = Sem(es.enter_context(nc.semaphore("c_v")))
        c_s = Sem(es.enter_context(nc.semaphore("c_s")))
        c_t = Sem(es.enter_context(nc.semaphore("c_t")))
        c_p = Sem(es.enter_context(nc.semaphore("c_p")))
        pr.strict["vector"] = c_v
        pr.strict["scalar"] = c_s
        pr.strict["gpsimd"] = c_p

        acc = sb("acc", [128, NT * D])
        xn_fm = sb("xn_fm", [128, 8 * T_OWN], BF16)
        logit_all = sb("logit_all", [128, NE * NT])
        sval = sb("sval", [128, NT])
        slotg = sb("slotg", [128, NT], U32)
        carry16 = sb("carry16", [1, NE])
        rowsA = sb("rowsA", [128, D])
        rowsB = sb("rowsB", [128, D])
        carry = rowsA[0:1, :]
        attn_row = rowsB[0:1, :]
        k_ident = sb("k_ident", [128, 128])
        k_ident_bf = sb("k_ident_bf", [128, 128], BF16)
        k_ui = sb("k_ui", [128, 128])
        k_us = sb("k_us", [128, 128])
        k_onesc3 = sb("k_onesc3", [128, 128])
        k_onesb3 = sb("k_onesb3", [128, 128], BF16)
        rowsC = sb("rowsC", [128, D], BF16)
        k_onesc = k_onesc3[0:1, :]
        k_ones128 = sb("k_ones128", [128, 1])
        k_iob = sb("k_iob", [128, NE])
        k_iop = sb("k_iop", [128, NE])
        k_rsq = sb("k_rsq", [128, NT])
        k_mask = sb("k_mask", [128, NPT])
        k_garb = sb("k_garb", [128, NT])
        k_lnb = sb("k_lnb", [128, 8])
        k_lng = sb("k_lng", [128, 8])
        wr_sb = sb("wr_sb", [128, 8 * NE])
        pbf = sb("pbf_pool", [128, PBF_N], BF16)
        pf = sb("pf_pool", [128, PF_N])
        def bfs(off, n):
            return pbf[:, off:off + n]
        def f32s(off, n):
            return pf[:, off:off + n]
        s1c = sb("s1c", [128, 1]); nmc = sb("nmc", [128, 1])
        vsc = sb("vsc", [128, 1]); sdc = sb("sdc", [128, 1]); rrc = sb("rrc", [128, 1])
        mxc = sb("mxc", [128, 1]); idxc = sb("idxc", [128, 1])
        seqc = sb("seqc", [128, NE]); t16c = sb("t16c", [128, NE]); ohc = sb("ohc", [128, NE])
        poshc = sb("poshc", [128, NE]); ptc = sb("ptc", [128, 1]); gec = sb("gec", [128, 1])
        slf = sb("slf", [128, 1]); slsf = sb("slsf", [128, 1]); slgf = sb("slgf", [128, 1])
        slz = sb("slz", [128, 1]); s_u32 = sb("s_u32", [128, 1], U32)
        qv0 = sb("qv0", [128, 256]); qv1 = sb("qv1", [128, 256])
        qp0 = sb("qp0", [128, 256]); qp1 = sb("qp1", [128, 256])
        ssq, snr, src_ = qv0, qv1, qp0
        sq2w = sb("sq2w", [128, 512])
        mqc = sb("mqc", [128, 1]); vsc2 = sb("vsc2", [128, 1])
        k_eps24 = sb("k_eps24", [128, 1]); k_epsln = sb("k_epsln", [128, 1])
        k_maskn = sb("k_maskn", [128, NPT])

        pb = [ps(f"pb{k}", [128, 512], F32) for k in range(7)]
        pbank_bf = es.enter_context(nc.psum_tensor("pbank_bf", [128, 1024], BF16))

        def ld(dst, src):
            pr.emit("sync", lambda e, d=dst, s=src: e.dma_start(out=d, in_=s), inc=(s_x, 16))
        for dst, src in [(k_ident[:, :], ident), (k_ui[:, :], u_incl), (k_us[:, :], u_strict),
                         (k_onesc3[0:1, :], ones_col), (k_onesc3[32:33, :], ones_col),
                         (k_onesc3[64:65, :], ones_col), (k_ones128[:, :], ones128),
                         (k_iob[:, :], iota_big), (k_iop[:, :], iota_pln), (k_rsq[:, :], rsq_pos),
                         (k_mask[:, :], mask_pre), (k_garb[:, :], garb),
                         (rowsA[32:33, :], bk_r), (rowsA[64:65, :], bv_r), (rowsB[32:33, :], bo_r),
                         (k_lnb[:, :], lnb_fm), (k_lng[:, :], lng_fm)]:
            ld(dst, src)
        ld(wr_sb[:, :].rearrange("p (c e) -> p c e", c=8), wr_t.rearrange("(c p) e -> p c e", p=128))
        pr.emit("gpsimd", lambda e: e.dma_start(out=k_ident_bf[:, :], in_=ident), inc=(s_g, 16))
        pr.emit("gpsimd", lambda e: e.dma_start(out=k_onesb3[0:1, :], in_=ones_col), inc=(s_g, 16))
        pr.emit("gpsimd", lambda e: e.dma_start(out=k_onesb3[32:33, :], in_=ones_col), inc=(s_g, 16))
        pr.emit("gpsimd", lambda e: e.dma_start(out=k_onesb3[64:65, :], in_=ones_col), inc=(s_g, 16))
        pr.emit("gpsimd", lambda e: e.dma_start(out=rowsC[32:33, :], in_=bk_r), inc=(s_g, 16))
        pr.emit("gpsimd", lambda e: e.dma_start(out=rowsC[64:65, :], in_=bv_r), inc=(s_g, 16))
        const_x = s_x.n
        const_g = s_g.n

        pr.emit("vector", lambda e: e.memset(rowsA[0:1, :], 0.0))
        pr.emit("vector", lambda e: e.memset(carry16[:, :], 0.0))
        pr.emit("vector", lambda e: e.memset(rowsB[64:65, :], 0.0))
        pr.emit("vector", lambda e: e.memset(k_eps24[:, :], 1e-24))
        pr.emit("vector", lambda e: e.memset(k_epsln[:, :], LN_EPS))
        pr.emit("vector", lambda e: e.tensor_scalar(k_maskn[:, :], k_mask[:, :], -1.0, None, op0=ALU.mult),
                waits=[(s_x, const_x)])
        v_zero = c_v.n
        pr.emit("sync", lambda e: e.dma_start(out=y_dram[ZROW:ZROW + 1, :], in_=rowsB[64:65, :]),
                waits=[(c_v, v_zero)], inc=(s_st, 16))

        FO_XS0, FO_XS1, FO_XN32, FO_LNSCR, FO_FM32 = 0, 1024, 2048, 3072, 4096
        FO_KS, FO_VS, FO_KN, FO_BD, FO_MS = 4096, 5120, 6144, 1024, 5120
        FO_XP0, FO_XP1, FO_XNP = 0, 1024, 2048
        FO_RT, FO_ZSC, FO_Z = 1024, 0, 2048
        FO_SG = 3072
        FO_YSB = 4096
        FO_YTOK, FO_OT = 6144, 2048
        BO_XNTOK, BO_HS = 0, NT * D
        BO_WG0, BO_WG1, BO_WU0, BO_WU1 = 0, 4096, 8192, 12288
        BO_WD0, BO_WD1, BO_ACT = 16384, 20480, 24576
        BO_WK, BO_WV, BO_WOG = 0, 8192, 16384
        BO_XNPFM, BO_ZB, BO_ZFM = 24576, 24576, 24576
        BO_GUP0, BO_GUP1, BO_DWN0, BO_DWN1 = 0, 8192, 16384, 20480
        BO_HTOK, BO_HFM, BO_FACT = 24576, 26624, 28672

        def emit_ln(xin, xout, xw):
            pr.emit("vector", lambda e: e.reduce_sum(s1c[:, :], xin, axis=AX.X),
                    waits=([(s_x, xw)] if xw else []))
            pr.emit("vector", lambda e: e.tensor_scalar(nmc[:, :], s1c[:, :], -1.0 / D, None, op0=ALU.mult))
            nm_done = c_v.n
            pr.emit("scalar", lambda e: e.activation(f32s(FO_LNSCR, D), xin, AF.Square,
                                                     bias=nmc[:, :], scale=1.0,
                                                     accum_out=vsc[:, :]),
                    waits=[(c_v, nm_done)])
            vs_done = c_s.n
            pr.emit("vector", lambda e: e.tensor_scalar(vsc[:, :], vsc[:, :], 1.0 / D, LN_EPS,
                                                        op0=ALU.mult, op1=ALU.add),
                    waits=[(c_s, vs_done)])
            ve_done = c_v.n
            pr.emit("scalar", lambda e: e.activation(sdc[:, :], vsc[:, :], AF.Sqrt),
                    waits=[(c_v, ve_done)])
            sd_done = c_s.n
            pr.emit("vector", lambda e: e.reciprocal(rrc[:, :], sdc[:, :]),
                    waits=[(c_s, sd_done)])
            pr.emit("vector", lambda e: e.tensor_scalar(xout, xin, nmc[:, :], rrc[:, :],
                                                        op0=ALU.add, op1=ALU.mult))
            return c_v.n

        for i in range(NT):
            xst = f32s(FO_XS0 if i % 2 == 0 else FO_XS1, D)
            if i >= 2:
                pr.wait("sync", c_v)
                pr.wait("sync", c_s)
            pr.emit("sync", lambda e, d=xst, s=x_own[i * 128:(i + 1) * 128, :]:
                    e.dma_start(out=d, in_=s), inc=(s_x, 16))
            xw = s_x.n
            emit_ln(xst, f32s(FO_XN32, D), xw)
            pr.emit("vector", lambda e, i=i, xst=xst: e.tensor_tensor(
                        acc[:, i * D:(i + 1) * D], f32s(FO_XN32, D), xst, ALU.add))
            pr.emit("vector", lambda e, i=i: e.tensor_copy(
                        bfs(BO_XNTOK + i * D, D), f32s(FO_XN32, D)))
            xn_done = c_v.n
            pr.wait("tensor", c_v, xn_done)
            def tp_all(e, i=i):
                for j in range(8):
                    e.transpose(pbank_bf[:, j * 128:(j + 1) * 128],
                                bfs(BO_XNTOK + i * D + j * 128, 128),
                                k_ident_bf[:, :])
                for j in range(8):
                    ins = e.transpose(pb[4 + j // 4][:, (j % 4) * 128:(j % 4 + 1) * 128],
                                      f32s(FO_XN32 + j * 128, 128), k_ident[:, :])
                return ins
            pr.emit("tensor", tp_all, waits=[(s_g, const_g)], inc=(c_t, 1))
            tp_done = c_t.n
            pr.emit("vector", lambda e, i=i: e.tensor_copy(
                        xn_fm[:, :].rearrange("p (c t) -> p c t", c=8)[:, :, i * 128:(i + 1) * 128],
                        pbank_bf[:, :].rearrange("p (c t) -> p c t", c=8)[:, :, :]),
                    waits=[(c_t, tp_done)])
            pr.emit("vector", lambda e: e.tensor_copy(f32s(FO_FM32, 512), pb[4][:, :]))
            pr.emit("vector", lambda e: e.tensor_copy(f32s(FO_FM32 + 512, 512), pb[5][:, :]))
            ev_done = c_v.n
            def router(e, i=i):
                for j in range(8):
                    ins = e.matmul(pb[6][:, 0:NE], f32s(FO_FM32 + j * 128, 128),
                                   wr_sb[:, j * NE:(j + 1) * NE],
                                   start=(j == 0), stop=(j == 7))
                return ins
            pr.emit("tensor", router, waits=[(c_v, ev_done), (s_x, const_x)], inc=(c_t, 1))
            lg_done = c_t.n
            pr.emit("vector", lambda e, i=i: e.tensor_copy(
                        logit_all[:, i * NE:(i + 1) * NE], pb[6][:, 0:NE]),
                    waits=[(c_t, lg_done)])

        pr.wait("vector", s_x, const_x)
        for i in range(NT):
            lg = logit_all[:, i * NE:(i + 1) * NE]
            pr.emit("vector", lambda e, lg=lg: e.reduce_max(mxc[:, :], lg, axis=AX.X))
            pr.emit("vector", lambda e, lg=lg: e.tensor_single_scalar(seqc[:, :], lg, mxc[:, :], ALU.is_equal))
            pr.emit("vector", lambda e: e.scalar_tensor_tensor(t16c[:, :], seqc[:, :], -1e4,
                                                               k_iob[:, :], op0=ALU.mult, op1=ALU.add))
            pr.emit("vector", lambda e: e.tensor_reduce(idxc[:, :], t16c[:, :], axis=AX.X, op=ALU.min))
            pr.emit("vector", lambda e: e.tensor_single_scalar(ohc[:, :], k_iop[:, :], idxc[:, :], ALU.is_equal))
            oh_done = c_v.n
            mx_done = c_v.n
            pr.emit("scalar", lambda e, i=i: e.activation(sval[:, i:i + 1], mxc[:, :], AF.Sigmoid),
                    waits=[(c_v, mx_done)])
            s_done = c_s.n
            def posmm(e, i=i):
                e.matmul(pb[0][:, 0:NE], k_us[:, :], ohc[:, :], start=True, stop=False)
                return e.matmul(pb[0][:, 0:NE], k_onesc3[0:1, :], carry16[:, :], start=False, stop=True)
            pr.emit("tensor", posmm, waits=[(c_v, oh_done)], inc=(c_t, 1))
            pos_done = c_t.n
            def c16mm(e, i=i):
                return e.matmul(pb[1][0:1, 0:NE], k_ones128[:, :], ohc[:, :], start=True, stop=True)
            pr.emit("tensor", c16mm, inc=(c_t, 1))
            c16_done = c_t.n
            pr.emit("vector", lambda e: e.tensor_add(carry16[:, :], carry16[:, :], pb[1][0:1, 0:NE]),
                    waits=[(c_t, c16_done)])
            pr.emit("vector", lambda e: e.tensor_tensor(poshc[:, :], pb[0][:, 0:NE], ohc[:, :], ALU.mult),
                    waits=[(c_t, pos_done)])
            pr.emit("vector", lambda e: e.reduce_sum(ptc[:, :], poshc[:, :], axis=AX.X))
            pr.emit("vector", lambda e: e.scalar_tensor_tensor(
                        slf[:, :], idxc[:, :], float(CAP), ptc[:, :], op0=ALU.mult, op1=ALU.add))
            pr.emit("vector", lambda e: e.tensor_scalar(gec[:, :], ptc[:, :], float(CAP), None, op0=ALU.is_ge))
            pr.emit("vector", lambda e, i=i: e.tensor_sub(slz[:, :], k_garb[:, i:i + 1], slf[:, :]))
            pr.emit("vector", lambda e: e.tensor_mul(slz[:, :], slz[:, :], gec[:, :]))
            pr.emit("vector", lambda e: e.tensor_add(slsf[:, :], slf[:, :], slz[:, :]))
            pr.emit("vector", lambda e: e.tensor_scalar(slz[:, :], slf[:, :], -1.0, float(ZROW),
                                                        op0=ALU.mult, op1=ALU.add))
            pr.emit("vector", lambda e: e.tensor_mul(slz[:, :], slz[:, :], gec[:, :]))
            pr.emit("vector", lambda e: e.tensor_add(slgf[:, :], slf[:, :], slz[:, :]))
            pr.emit("vector", lambda e, i=i: e.tensor_copy(slotg[:, i:i + 1], slgf[:, :]))
            pr.emit("vector", lambda e: e.tensor_copy(s_u32[:, :], slsf[:, :]))
            pr.emit("vector", lambda e, i=i: e.tensor_single_scalar(
                        bfs(BO_HS, D), bfs(BO_XNTOK + i * D, D), sval[:, i:i + 1], ALU.mult),
                    waits=[(c_s, s_done)])
            hs_done = c_v.n
            pr.emit("gpsimd", lambda e: e.indirect_dma_start(
                        out=h_dram, out_offset=IndirectOffsetOnAxis(s_u32[:, 0:1], 0),
                        in_=bfs(BO_HS, D), in_offset=None),
                    waits=[(c_v, hs_done)], inc=(s_g, 16))
            pr.wait("vector", s_g)
        b_done_v = c_v.n

        wg_off = [BO_WG0, BO_WG1]
        wu_off = [BO_WU0, BO_WU1]
        wd_off = [BO_WD0, BO_WD1]
        grp_done_t = {}
        for g in range(4):
            bi = g % 2
            if bi in grp_done_t:
                pr.wait("gpsimd", c_t, grp_done_t[bi])
            pr.emit("gpsimd", lambda e, g=g, bi=bi: e.dma_start(
                        out=bfs(wg_off[bi], 4096).rearrange("p (c f) -> p c f", c=8),
                        in_=wg_t[:, g * 512:(g + 1) * 512].rearrange("(c p) f -> p c f", p=128)),
                    inc=(s_g, 16))
            pr.emit("gpsimd", lambda e, g=g, bi=bi: e.dma_start(
                        out=bfs(wu_off[bi], 4096).rearrange("p (c f) -> p c f", c=8),
                        in_=wu_t[:, g * 512:(g + 1) * 512].rearrange("(c p) f -> p c f", p=128)),
                    inc=(s_g, 16))
            pr.emit("gpsimd", lambda e, g=g, bi=bi: e.dma_start(
                        out=bfs(wd_off[bi], 4096).rearrange("p (c h) -> p c h", c=4),
                        in_=wd_t[g * 512:(g + 1) * 512, :].rearrange("(c p) h -> p c h", p=128)),
                    inc=(s_g, 16))
            gw = s_g.n
            last_sgd = 0
            last_mult = 0
            for tb in range(4):
                for ft in range(4):
                    def gatemm(e, bi=bi, tb=tb, ft=ft):
                        for j in range(8):
                            ins = e.matmul(pb[4][:, 0:512],
                                           bfs(wg_off[bi] + j * 512 + ft * 128, 128),
                                           xn_fm[:, j * T_OWN + tb * 512: j * T_OWN + (tb + 1) * 512],
                                           start=(j == 0), stop=(j == 7))
                        return ins
                    w = [(s_g, gw), (c_s, last_sgd)]
                    if g == 0 and tb == 0 and ft == 0:
                        w.append((c_v, b_done_v))
                    pr.emit("tensor", gatemm, waits=w, inc=(c_t, 1))
                    gmm = c_t.n
                    def upmm(e, bi=bi, tb=tb, ft=ft):
                        for j in range(8):
                            ins = e.matmul(pb[5][:, 0:512],
                                           bfs(wu_off[bi] + j * 512 + ft * 128, 128),
                                           xn_fm[:, j * T_OWN + tb * 512: j * T_OWN + (tb + 1) * 512],
                                           start=(j == 0), stop=(j == 7))
                        return ins
                    pr.emit("tensor", upmm, waits=[(c_v, last_mult)], inc=(c_t, 1))
                    umm = c_t.n
                    pr.emit("scalar", lambda e: e.activation(f32s(FO_SG, 512), pb[4][:, 0:512], AF.Silu),
                            waits=[(c_t, gmm)])
                    sgd = c_s.n
                    last_sgd = sgd
                    pr.emit("vector", lambda e, ft=ft: e.tensor_tensor(
                                bfs(BO_ACT + ft * 512, 512), f32s(FO_SG, 512), pb[5][:, 0:512], ALU.mult),
                            waits=[(c_t, umm), (c_s, sgd)])
                    last_mult = c_v.n
                acts_done = c_v.n
                dn_hist = []
                for hb in range(2):
                    def downmm(e, bi=bi, hb=hb):
                        for ft in range(4):
                            for tt2 in range(2):
                                for nh in range(2):
                                    ins = e.matmul(pb[tt2 * 2 + nh][:, :],
                                                   bfs(BO_ACT + ft * 512 + (hb * 2 + tt2) * 128, 128),
                                                   bfs(wd_off[bi] + ft * 1024 + nh * 512, 512),
                                                   start=(ft == 0), stop=(ft == 3))
                        return ins
                    w = [(c_v, acts_done)] if hb == 0 else [(c_v, c_v.n)]
                    pr.emit("tensor", downmm, waits=w, inc=(c_t, 1))
                    dn = c_t.n
                    dn_hist.append(dn)
                    for tt2 in range(2):
                        ti = tb * 4 + hb * 2 + tt2
                        for nh in range(2):
                            pr.emit("vector", lambda e, ti=ti, tt2=tt2, nh=nh: e.tensor_add(
                                        acc[:, ti * D + nh * 512: ti * D + (nh + 1) * 512],
                                        acc[:, ti * D + nh * 512: ti * D + (nh + 1) * 512],
                                        pb[tt2 * 2 + nh][:, :]),
                                    waits=[(c_t, dn)])
                grp_done_t[bi] = dn_hist[-1]
                yadd = c_v.n
                pr.wait("tensor", c_v, yadd)

        if debug:
            pr.wait("sync", c_v, c_v.n)
            pr.emit("sync", lambda e: e.dma_start(out=dbg["d_accC"], in_=acc[:, :]), inc=(s_st, 16))

        pr.wait("gpsimd", c_t, c_t.n)
        pr.emit("gpsimd", lambda e: e.dma_start(
                    out=bfs(BO_WK, 8192).rearrange("p (c n) -> p c n", c=8),
                    in_=wk_t.rearrange("(c p) n -> p c n", p=128)), inc=(s_g, 16))
        pr.emit("gpsimd", lambda e: e.dma_start(
                    out=bfs(BO_WV, 8192).rearrange("p (c n) -> p c n", c=8),
                    in_=wv_t.rearrange("(c p) n -> p c n", p=128)), inc=(s_g, 16))
        kv_w = s_g.n
        pr.wait("sync", c_v, c_v.n)
        for j in range(8):
            xo = FO_XP0 if j % 2 == 0 else FO_XP1
            if j >= 2:
                pr.wait("sync", c_t)
            pr.emit("sync", lambda e, j=j, xo=xo: e.dma_start(
                        out=f32s(xo, D), in_=wo_t[j * 128:(j + 1) * 128, :]), inc=(s_x, 16))
            ww = s_x.n
            pr.emit("vector", lambda e, j=j, xo=xo: e.tensor_single_scalar(
                        bfs(BO_WOG + j * 1024, D), f32s(xo, D), k_lng[:, j:j + 1], ALU.mult),
                    waits=[(s_x, ww)])
            wsc = c_v.n
            def rowmm2(e, j=j, xo=xo):
                e.matmul(pb[6][0:1, :], k_lnb[:, j:j + 1], f32s(xo, 512),
                         start=(j == 0), stop=(j == 7))
                return e.matmul(pb[2][0:1, :], k_lnb[:, j:j + 1], f32s(xo + 512, 512),
                                start=(j == 0), stop=(j == 7))
            pr.emit("tensor", rowmm2, waits=[(c_v, wsc)], inc=(c_t, 1))
        row_done = c_t.n
        pr.emit("vector", lambda e: e.tensor_add(rowsB[0:1, 0:512], rowsB[32:33, 0:512], pb[6][0:1, :]),
                waits=[(c_t, row_done)])
        pr.emit("vector", lambda e: e.tensor_add(rowsB[0:1, 512:1024], rowsB[32:33, 512:1024], pb[2][0:1, :]))
        pr.emit("vector", lambda e: e.tensor_copy(rowsC[0:1, :], rowsB[0:1, :]))
        if debug:
            pr.wait("sync", c_v, c_v.n)
            pr.emit("sync", lambda e: e.dma_start(out=dbg["d_row"], in_=rowsB[0:1, :]), inc=(s_st, 16))

        QTAB = {
            0: [(1, 0, 0), (-1, 1, 1), (-1, 2, 2), (-1, 3, 3)],
            1: [(1, 0, 1), (1, 1, 0), (1, 2, 3), (-1, 3, 2)],
            2: [(1, 0, 2), (-1, 1, 3), (1, 2, 0), (1, 3, 1)],
            3: [(1, 0, 3), (1, 1, 2), (-1, 2, 1), (1, 3, 0)],
        }

        def quat_emit(out_off, a_off, b_off, conj_b, pool_comps, waits_v=(), waits_p=()):
            wv = list(waits_v)
            wp = list(waits_p)
            NQ = D // 4
            for comp in range(4):
                eng = "gpsimd" if comp in pool_comps else "vector"
                t0, t1 = (qp0, qp1) if eng == "gpsimd" else (qv0, qv1)
                terms = QTAB[comp]
                if conj_b:
                    terms = [(s * (-1 if b != 0 else 1), a, b) for (s, a, b) in terms]
                terms = sorted(terms, key=lambda t: -t[0])
                ov = pf[:, out_off + comp * NQ:out_off + (comp + 1) * NQ]
                w = wp if eng == "gpsimd" else wv
                for n_, (s, a, b) in enumerate(terms):
                    av = pf[:, a_off + a * NQ:a_off + (a + 1) * NQ]
                    bv = pf[:, b_off + b * NQ:b_off + (b + 1) * NQ]
                    if n_ == 0:
                        pr.emit(eng, lambda e, av=av, bv=bv, t0=t0: e.tensor_mul(t0[:, :], av, bv),
                                waits=tuple(w))
                        if eng == "gpsimd":
                            wp = []
                        else:
                            wv = []
                    else:
                        pr.emit(eng, lambda e, av=av, bv=bv, t1=t1: e.tensor_mul(t1[:, :], av, bv))
                        dst = ov if n_ == 3 else t0[:, :]
                        op = ALU.add if s > 0 else ALU.subtract
                        pr.emit(eng, lambda e, dst=dst, t0=t0, t1=t1, op=op:
                                e.tensor_tensor(dst, t0[:, :], t1[:, :], op))
            return c_v.n, c_p.n

        FO_LN2 = 3072

        st_state = {"cons_v": 0, "cons_p": 0, "evac_s": c_s.n, "tot_add_v": 0,
                    "pbank_v": c_v.n, "dump_st": 0, "prod_mm": 0}

        def emit_kv_mm(fm_fn, extra_wait):
            def kmm(e):
                for j in range(8):
                    for nh in range(2):
                        e.matmul(pb[nh][:, :], fm_fn(j),
                                 bfs(BO_WK + j * 1024 + nh * 512, 512),
                                 start=(j == 0), stop=False)
                for nh in range(2):
                    ins = e.matmul(pb[nh][:, :], k_onesb3[32:33, :],
                                   rowsC[32:33, nh * 512:(nh + 1) * 512],
                                   start=False, stop=True)
                return ins
            pr.emit("tensor", kmm,
                    waits=[(s_g, kv_w), (c_s, st_state["evac_s"])] + list(extra_wait),
                    inc=(c_t, 1))
            def vmm(e):
                for j in range(8):
                    for nh in range(2):
                        e.matmul(pb[2 + nh][:, :], fm_fn(j),
                                 bfs(BO_WV + j * 1024 + nh * 512, 512),
                                 start=(j == 0), stop=False)
                for nh in range(2):
                    ins = e.matmul(pb[2 + nh][:, :], k_onesb3[64:65, :],
                                   rowsC[64:65, nh * 512:(nh + 1) * 512],
                                   start=False, stop=True)
                return ins
            pr.emit("tensor", vmm, inc=(c_t, 1))
            return c_t.n

        def emit_kv_evac(v_done):
            w = [(c_t, v_done), (c_v, st_state["cons_v"]), (c_p, st_state["cons_p"])]
            if debug and st_state["dump_st"]:
                w.append((s_st, st_state["dump_st"]))
                st_state["dump_st"] = 0
            pr.emit("scalar", lambda e: e.copy(f32s(FO_KS, 512), pb[0][:, :]), waits=w)
            pr.emit("scalar", lambda e: e.copy(f32s(FO_KS + 512, 512), pb[1][:, :]))
            pr.emit("scalar", lambda e: e.copy(f32s(FO_VS, 512), pb[2][:, :]))
            pr.emit("scalar", lambda e: e.copy(f32s(FO_VS + 512, 512), pb[3][:, :]))
            st_state["evac_s"] = c_s.n

        NQ = D // 4

        def emit_knorm():
            pr.emit("scalar", lambda e: e.activation(f32s(FO_LN2, D), f32s(FO_KS, D), AF.Square))
            sq_done = c_s.n
            pr.emit("vector", lambda e: e.tensor_tensor(sq2w[:, :], f32s(FO_LN2, 512),
                                                        f32s(FO_LN2 + 512, 512), ALU.add),
                    waits=[(c_s, sq_done)])
            pr.emit("vector", lambda e: e.tensor_tensor(ssq[:, :], sq2w[:, 0:256],
                                                        sq2w[:, 256:512], ALU.add))
            ss_done = c_v.n
            pr.emit("scalar", lambda e: e.activation(snr[:, :], ssq[:, :], AF.Sqrt,
                                                     bias=k_eps24[:, :], scale=1.0),
                    waits=[(c_v, ss_done)])
            nrm_done = c_s.n
            pr.emit("vector", lambda e: e.reciprocal(src_[:, :], snr[:, :]),
                    waits=[(c_s, nrm_done)])
            rs_done = c_v.n
            for c in range(4):
                eng = "vector" if c < 2 else "gpsimd"
                w = [(c_v, rs_done)] if c == 2 else []
                pr.emit(eng, lambda e, c=c: e.tensor_tensor(
                            pf[:, FO_KN + c * NQ:FO_KN + (c + 1) * NQ],
                            pf[:, FO_KS + c * NQ:FO_KS + (c + 1) * NQ],
                            src_[:, :], ALU.mult), waits=w)
            return c_v.n, c_p.n

        def emit_knorm_bound(parity):
            kn_v, kn_p = emit_knorm()
            pool_comps = (1, 2, 3) if parity == 0 else (2, 3)
            return quat_emit(FO_BD, FO_VS, FO_KN, False, pool_comps,
                             waits_v=[(c_v, kn_v), (c_p, kn_p)],
                             waits_p=[(c_v, kn_v), (c_p, kn_p)])

        def emit_tot_carry(lhs_col):
            for nh in range(2):
                def totmm(e, nh=nh, lhs_col=lhs_col):
                    return e.matmul(pb[6][0:1, :], lhs_col,
                                    f32s(FO_BD + nh * 512, 512), start=True, stop=True)
                w = [(c_v, st_state["cons_v"]), (c_p, st_state["cons_p"]),
                     (c_v, st_state["tot_add_v"])] if nh == 0 else [(c_v, c_v.n)]
                pr.emit("tensor", totmm, waits=w, inc=(c_t, 1))
                td = c_t.n
                pr.emit("vector", lambda e, nh=nh: e.tensor_add(
                            rowsA[0:1, nh * 512:(nh + 1) * 512],
                            rowsA[0:1, nh * 512:(nh + 1) * 512], pb[6][0:1, :]),
                        waits=[(c_t, td)])
            st_state["tot_add_v"] = c_v.n

        PSLOT = 1024

        def emit_tail_D(j8):
            kn_v, kn_p = emit_knorm()
            mcol = {1: k_mask[:, j8:j8 + 1], -1: k_maskn[:, j8:j8 + 1]}
            for half, (bank, comps) in enumerate([(pb[6], (0, 1)), (pb[5], (2, 3))]):
                first_v = first_p = True
                for ci, comp in enumerate(comps):
                    for n_, (s, a, b) in enumerate(QTAB[comp]):
                        slot = ci * 4 + n_
                        eng = "vector" if slot % 2 == 0 else "gpsimd"
                        w = []
                        if eng == "vector" and first_v:
                            w = [(c_v, kn_v), (c_p, kn_p), (c_t, st_state["prod_mm"])]
                            first_v = False
                        elif eng == "gpsimd" and first_p:
                            w = [(c_v, kn_v), (c_p, kn_p), (c_t, st_state["prod_mm"])]
                            first_p = False
                        pr.emit(eng, lambda e, a=a, b=b, slot=slot: e.tensor_tensor(
                                    f32s(PSLOT + slot * 256, 256),
                                    pf[:, FO_VS + a * NQ:FO_VS + (a + 1) * NQ],
                                    pf[:, FO_KN + b * NQ:FO_KN + (b + 1) * NQ],
                                    ALU.mult), waits=w)
                pv, pp = c_v.n, c_p.n

                def prodmm(e, bank=bank, comps=comps):
                    for ci, comp in enumerate(comps):
                        for n_, (s, a, b) in enumerate(QTAB[comp]):
                            slot = ci * 4 + n_
                            ins = e.matmul(bank[0:1, ci * 256:(ci + 1) * 256],
                                           mcol[1 if s > 0 else -1],
                                           f32s(PSLOT + slot * 256, 256),
                                           start=(n_ == 0), stop=(n_ == 3))
                    return ins
                pr.emit("tensor", prodmm,
                        waits=[(c_v, pv), (c_p, pp), (c_v, st_state["tot_add_v"])],
                        inc=(c_t, 1))
                st_state["prod_mm"] = c_t.n
            st_state["cons_v"], st_state["cons_p"] = pv, pp
            mm_done = st_state["prod_mm"]
            pr.emit("vector", lambda e: e.tensor_add(
                        rowsA[0:1, 0:512], rowsA[0:1, 0:512], pb[6][0:1, :]),
                    waits=[(c_t, mm_done)])
            pr.emit("vector", lambda e: e.tensor_add(
                        rowsA[0:1, 512:1024], rowsA[0:1, 512:1024], pb[5][0:1, :]))
            st_state["tot_add_v"] = c_v.n

        def emit_tail_E(i):
            bd_v, bd_p = emit_knorm_bound(i & 1)
            st_state["cons_v"], st_state["cons_p"] = bd_v, bd_p
            def memmm(e, i=i):
                for nh in range(2):
                    e.matmul(pb[4 + nh][:, :], k_ui[:, :], f32s(FO_BD + nh * 512, 512),
                             start=True, stop=False)
                    ins = e.matmul(pb[4 + nh][:, :], k_onesc3[0:1, :],
                                   rowsA[0:1, nh * 512:(nh + 1) * 512],
                                   start=False, stop=True)
                return ins
            pr.emit("tensor", memmm, waits=[(c_v, bd_v), (c_p, bd_p)], inc=(c_t, 1))
            mem_done = c_t.n
            pr.emit("vector", lambda e, i=i: e.tensor_single_scalar(
                        f32s(FO_MS, 512), pb[4][:, :], k_rsq[:, i:i + 1], ALU.mult),
                    waits=[(c_t, mem_done)])
            pr.emit("vector", lambda e, i=i: e.tensor_single_scalar(
                        f32s(FO_MS + 512, 512), pb[5][:, :], k_rsq[:, i:i + 1], ALU.mult))
            ms_done = c_v.n
            emit_tot_carry(k_ones128[:, :])
            if debug and i == 0:
                pr.wait("sync", c_v, c_v.n)
                pr.wait("sync", c_p, c_p.n)
                pr.emit("sync", lambda e: e.dma_start(out=dbg["d_K"], in_=f32s(FO_KS, D)), inc=(s_st, 16))
                pr.emit("sync", lambda e: e.dma_start(out=dbg["d_kn"], in_=f32s(FO_KN, D)), inc=(s_st, 16))
                pr.emit("sync", lambda e: e.dma_start(out=dbg["d_bound"], in_=f32s(FO_BD, D)), inc=(s_st, 16))
                pr.emit("sync", lambda e: e.dma_start(out=dbg["d_mem"], in_=f32s(FO_MS, D)), inc=(s_st, 16))
                st_state["dump_st"] = s_st.n
            rt_v, rt_p = quat_emit(FO_ZSC, FO_MS, FO_KN, True, (1, 2, 3) if (i & 1) else (2, 3),
                                   waits_v=[(c_v, ms_done)], waits_p=[(c_v, ms_done)])
            pr.wait("vector", c_p, rt_p)
            pr.wait("vector", c_v, rt_v)
            if debug and i == 0:
                pr.wait("sync", c_v, c_v.n)
                pr.emit("sync", lambda e: e.dma_start(out=dbg["d_retr"], in_=f32s(FO_ZSC, D)), inc=(s_st, 16))
                st_state["dump_st"] = s_st.n
            emit_ln(f32s(FO_ZSC, D), bfs(BO_ZB, D), 0)
            zb_done = c_v.n
            pr.wait("tensor", c_v, zb_done)
            def tpz(e):
                for j in range(8):
                    ins = e.transpose(pbank_bf[:, j * 128:(j + 1) * 128],
                                      bfs(BO_ZB + j * 128, 128), k_ident_bf[:, :])
                return ins
            pr.emit("tensor", tpz, inc=(c_t, 1))
            tzd = c_t.n
            pr.emit("vector", lambda e: e.tensor_copy(bfs(BO_ZFM, D), pbank_bf[:, :]),
                    waits=[(c_t, tzd)])
            zfm_done = c_v.n
            st_state["pbank_v"] = zfm_done
            def womm(e, i=i):
                for j in range(8):
                    for nh in range(2):
                        e.matmul(pb[4 + nh][:, :], bfs(BO_ZFM + j * 128, 128),
                                 bfs(BO_WOG + j * 1024 + nh * 512, 512),
                                 start=(j == 0), stop=False)
                for nh in range(2):
                    ins = e.matmul(pb[4 + nh][:, :], k_onesb3[0:1, :],
                                   rowsC[0:1, nh * 512:(nh + 1) * 512],
                                   start=False, stop=True)
                return ins
            pr.emit("tensor", womm, waits=[(c_v, zfm_done)], inc=(c_t, 1))
            wo_done = c_t.n
            for nh in range(2):
                pr.emit("vector", lambda e, i=i, nh=nh: e.tensor_add(
                            acc[:, i * D + nh * 512: i * D + (nh + 1) * 512],
                            acc[:, i * D + nh * 512: i * D + (nh + 1) * 512],
                            pb[4 + nh][:, :]),
                        waits=[(c_t, wo_done)])

        ln_done_hist = {}
        pending = None
        for step in range(NPT + NT):
            if step < NPT:
                j8 = step
                xo = FO_XP0
                if j8 >= 1:
                    pr.wait("sync", c_v, ln_done_hist[j8 - 1])
                else:
                    pr.wait("sync", c_t, row_done)
                pr.emit("sync", lambda e, j8=j8, xo=xo: e.dma_start(
                            out=f32s(xo, D), in_=x_pre[j8 * 128:(j8 + 1) * 128, :]), inc=(s_x, 16))
                xw = s_x.n
                ln_done_hist[j8] = emit_ln(f32s(xo, D), bfs(BO_XNPFM, D), xw)
                xnb_done = c_v.n
                pr.wait("tensor", c_v, max(xnb_done, st_state["pbank_v"]))
                def tpp(e):
                    for j in range(8):
                        ins = e.transpose(pbank_bf[:, j * 128:(j + 1) * 128],
                                          bfs(BO_XNPFM + j * 128, 128), k_ident_bf[:, :])
                    return ins
                pr.emit("tensor", tpp, inc=(c_t, 1))
                tpd = c_t.n
                pr.emit("vector", lambda e: e.tensor_copy(bfs(BO_XNPFM, D), pbank_bf[:, :]),
                        waits=[(c_t, tpd)])
                fme = c_v.n
                st_state["pbank_v"] = fme
                v_done = emit_kv_mm(lambda j: bfs(BO_XNPFM + j * 128, 128), [(c_v, fme)])
            else:
                i = step - NPT
                v_done = emit_kv_mm(
                    lambda j, i=i: xn_fm[:, j * T_OWN + i * 128: j * T_OWN + (i + 1) * 128], [])
            if pending is not None:
                (emit_tail_D if pending[0] == "D" else emit_tail_E)(pending[1])
                if debug and step == NPT:
                    pr.wait("sync", c_v, c_v.n)
                    pr.emit("sync", lambda e: e.dma_start(out=dbg["d_carryD"], in_=rowsA[0:1, :]), inc=(s_st, 16))
                    st_state["dump_st"] = s_st.n
            emit_kv_evac(v_done)
            pending = ("D", j8) if step < NPT else ("E", step - NPT)
        (emit_tail_D if pending[0] == "D" else emit_tail_E)(pending[1])
        if debug:
            pr.wait("sync", c_v, c_v.n)
            pr.emit("sync", lambda e: e.dma_start(out=dbg["d_carryE"], in_=rowsA[0:1, :]), inc=(s_st, 16))
            pr.emit("sync", lambda e: e.dma_start(out=dbg["d_accE"], in_=acc[:, :]), inc=(s_st, 16))
            prc = sb("prc", [128, D])
            pr.emit("vector", lambda e: e.tensor_copy(prc[:, :], rowsC[:, :]))
            pr.wait("sync", c_v, c_v.n)
            pr.emit("sync", lambda e: e.dma_start(out=dbg["d_rowsC"], in_=prc[:, :]), inc=(s_st, 16))

        nonlocal_dn = [0, 0]
        dn_sg = [0, 0]
        dwn_off = [BO_DWN0, BO_DWN1]
        def emit_down(bi, g, ft, mult_v):
            def downmm(e, bi=bi, g=g, ft=ft):
                for tt in range(2):
                    for nh in range(2):
                        ins = e.matmul(pb[tt * 2 + nh][:, :],
                                       bfs(BO_FACT + ft * 256 + tt * 128, 128),
                                       bfs(dwn_off[bi] + ft * 1024 + nh * 512, 512),
                                       start=(g == 0 and ft == 0), stop=(g == 3 and ft == 3))
                return ins
            pr.emit("tensor", downmm, waits=[(c_v, mult_v), (s_g, dn_sg[bi])], inc=(c_t, 1))
            nonlocal_dn[bi] = c_t.n

        gup_off = [BO_GUP0, BO_GUP1]
        pr.wait("gpsimd", c_t, c_t.n)
        gu_done = {}
        ystore_st = 0
        hload_w = None
        for e_i in range(NE):
            if e_i == 0:
                pr.emit("sync", lambda e: e.dma_start(
                            out=bfs(BO_HTOK, 2048).rearrange("p (tt d) -> p tt d", tt=2),
                            in_=h_dram[0:CAP, :].rearrange("(tt p) d -> p tt d", p=128)),
                        waits=[(s_g, s_g.n), (c_t, c_t.n)], inc=(s_x, 16))
                hload_w = s_x.n
            hw = hload_w
            for tt in range(2):
                def tph(e, tt=tt):
                    for j in range(8):
                        ins = e.transpose(pbank_bf[:, j * 128:(j + 1) * 128],
                                          bfs(BO_HTOK + tt * 1024 + j * 128, 128), k_ident_bf[:, :])
                    return ins
                pr.emit("tensor", tph, waits=[(s_x, hw), (c_v, c_v.n)], inc=(c_t, 1))
                tphd = c_t.n
                pr.emit("vector", lambda e, tt=tt: e.tensor_copy(
                            bfs(BO_HFM, 2048).rearrange("p (c t) -> p c t", c=8)[:, :, tt * 128:(tt + 1) * 128],
                            pbank_bf[:, :].rearrange("p (c t) -> p c t", c=8)[:, :, :]),
                        waits=[(c_t, tphd)])
            hfm_done = c_v.n
            st_state["pbank_v"] = hfm_done
            if e_i + 1 < NE:
                pr.emit("sync", lambda e, e_i=e_i: e.dma_start(
                            out=bfs(BO_HTOK, 2048).rearrange("p (tt d) -> p tt d", tt=2),
                            in_=h_dram[(e_i + 1) * CAP:(e_i + 2) * CAP, :].rearrange("(tt p) d -> p tt d", p=128)),
                        waits=[(c_t, tphd)], inc=(s_x, 16))
                hload_w = s_x.n
            mult_hist = {}
            for g in range(4):
                bi = g % 2
                wgu = [(c_t, gu_done[bi])] if bi in gu_done else []
                pr.emit("gpsimd", lambda e, e_i=e_i, g=g, bi=bi: e.dma_start(
                            out=bfs(gup_off[bi], 8192).rearrange("p (c f) -> p c f", c=8)[:, :, 0:512],
                            in_=gup[e_i, :, g * 512:(g + 1) * 512].rearrange("(c p) f -> p c f", p=128)),
                        waits=wgu, inc=(s_g, 16))
                pr.emit("gpsimd", lambda e, e_i=e_i, g=g, bi=bi: e.dma_start(
                            out=bfs(gup_off[bi], 8192).rearrange("p (c f) -> p c f", c=8)[:, :, 512:1024],
                            in_=gup[e_i, :, FF + g * 512: FF + (g + 1) * 512].rearrange("(c p) f -> p c f", p=128)),
                        inc=(s_g, 16))
                pr.emit("gpsimd", lambda e, e_i=e_i, g=g, bi=bi: e.dma_start(
                            out=bfs(dwn_off[bi], 4096).rearrange("p (c h) -> p c h", c=4),
                            in_=dwn[e_i, g * 512:(g + 1) * 512, :].rearrange("(c p) h -> p c h", p=128)),
                        waits=[(c_t, nonlocal_dn[bi])], inc=(s_g, 16))
                dn_sg[bi] = s_g.n
                gw = s_g.n
                for ft in range(4):
                    gbank = pb[4] if ft % 2 == 0 else pb[6]
                    def gatemm(e, bi=bi, ft=ft, gbank=gbank):
                        for j in range(8):
                            ins = e.matmul(gbank[:, 0:256],
                                           bfs(gup_off[bi] + j * 1024 + ft * 128, 128),
                                           bfs(BO_HFM + j * 256, 256),
                                           start=(j == 0), stop=(j == 7))
                        return ins
                    w = [(s_g, gw)]
                    if g == 0 and ft == 0:
                        w.append((c_v, hfm_done))
                    pr.emit("tensor", gatemm, waits=w, inc=(c_t, 1))
                    gmm = c_t.n
                    wup = ([(c_v, mult_hist[ft - 1])] if ft > 0 else
                           ([(c_v, mult_hist[3])] if g > 0 else [(c_v, hfm_done)]))
                    def upmm(e, bi=bi, ft=ft):
                        for j in range(8):
                            ins = e.matmul(pb[5][:, 0:256],
                                           bfs(gup_off[bi] + j * 1024 + 512 + ft * 128, 128),
                                           bfs(BO_HFM + j * 256, 256),
                                           start=(j == 0), stop=(j == 7))
                        return ins
                    pr.emit("tensor", upmm, waits=wup, inc=(c_t, 1))
                    umm = c_t.n
                    if ft > 0:
                        emit_down(bi, g, ft - 1, mult_hist[ft - 1])
                    pr.emit("scalar", lambda e, gbank=gbank: e.activation(
                                f32s(FO_SG, 256), gbank[:, 0:256], AF.Silu),
                            waits=[(c_t, gmm)])
                    sgd = c_s.n
                    pr.emit("vector", lambda e, ft=ft: e.tensor_tensor(
                                bfs(BO_FACT + ft * 256, 256), f32s(FO_SG, 256), pb[5][:, 0:256], ALU.mult),
                            waits=[(c_t, umm), (c_s, sgd), (c_t, max(nonlocal_dn))])
                    mult_hist[ft] = c_v.n
                emit_down(bi, g, 3, mult_hist[3])
                gu_done[bi] = c_t.n
            ydn = max(nonlocal_dn)
            for tt in range(2):
                for nh in range(2):
                    w = [(c_t, ydn)]
                    if tt == 0 and nh == 0 and ystore_st:
                        w.append((s_st, ystore_st))
                    pr.emit("vector", lambda e, tt=tt, nh=nh: e.tensor_copy(
                                f32s(FO_YSB + tt * 1024 + nh * 512, 512), pb[tt * 2 + nh][:, :]),
                            waits=w)
            yev = c_v.n
            pr.emit("sync", lambda e, e_i=e_i: e.dma_start(
                        out=y_dram[e_i * CAP:(e_i + 1) * CAP, :].rearrange("(tt p) d -> p tt d", p=128),
                        in_=f32s(FO_YSB, 2048).rearrange("p (tt d) -> p tt d", tt=2)),
                    waits=[(c_v, yev)], inc=(s_st, 16))
            ystore_st = s_st.n
        if debug:
            pr.wait("sync", s_st, s_st.n)
            pr.emit("sync", lambda e: e.dma_start(out=dbg["d_ydram"], in_=y_dram), inc=(s_st, 16))

        y_all_st = s_st.n
        for i in range(NT):
            pr.emit("gpsimd", lambda e, i=i: e.indirect_dma_start(
                        out=f32s(FO_YTOK, D), out_offset=None,
                        in_=y_dram, in_offset=IndirectOffsetOnAxis(slotg[:, i:i + 1], 0)),
                    waits=[(s_st, y_all_st)], inc=(s_g, 16))
            gy = s_g.n
            pr.emit("vector", lambda e, i=i: e.tensor_add(
                        f32s(FO_OT, D), acc[:, i * D:(i + 1) * D], f32s(FO_YTOK, D)),
                    waits=[(s_g, gy)])
            ot_done = c_v.n
            pr.emit("sync", lambda e, i=i: e.dma_start(
                        out=out[i * 128:(i + 1) * 128, :], in_=f32s(FO_OT, D)),
                    waits=[(c_v, ot_done)], inc=(s_st, 16))
            pr.wait("gpsimd", s_st)
            pr.wait("vector", s_st)

        if debug:
            pr.wait("sync", c_v, c_v.n)
            pr.wait("sync", s_g, s_g.n)
            pr.emit("sync", lambda e: e.dma_start(out=dbg["d_xnfm"], in_=xn_fm[:, :]), inc=(s_st, 16))
            pr.emit("sync", lambda e: e.dma_start(out=dbg["d_logits"], in_=logit_all[:, :]), inc=(s_st, 16))
            pr.emit("sync", lambda e: e.dma_start(out=dbg["d_sval"], in_=sval[:, :]), inc=(s_st, 16))
            pr.emit("sync", lambda e: e.dma_start(out=dbg["d_carry16"], in_=carry16[:, :]), inc=(s_st, 16))
            pr.emit("sync", lambda e: e.dma_start(out=dbg["d_hdram"], in_=h_dram), inc=(s_st, 16))
            sg32 = sb("sg32", [128, NT])
            pr.emit("vector", lambda e: e.tensor_copy(sg32[:, :], slotg[:, :]))
            pr.wait("sync", c_v, c_v.n)
            pr.emit("sync", lambda e: e.dma_start(out=dbg["d_slots"], in_=sg32[:, :]), inc=(s_st, 16))

        pr.wait("sync", s_st, s_st.n)

        blk = es.enter_context(nc.Block())

        @blk.sync
        def _(eng):
            pr.replay("sync", eng)

        @blk.gpsimd
        def _(eng):
            pr.replay("gpsimd", eng)

        @blk.tensor
        def _(eng):
            pr.replay("tensor", eng)

        @blk.vector
        def _(eng):
            pr.replay("vector", eng)

        @blk.scalar
        def _(eng):
            pr.replay("scalar", eng)

    return nc



def stage_inputs(L0_attn_x, L0_attn_Wk, L0_attn_bk, L0_attn_Wv, L0_attn_bv,
                 L0_attn_ln_g, L0_attn_ln_b, L0_attn_Wo, L0_attn_bo,
                 L0_ffn_w_router, L0_ffn_gate_up_proj, L0_ffn_down_proj,
                 L0_ffn_w_gate, L0_ffn_w_up, L0_ffn_w_down):
    f32 = np.float32
    x = np.ascontiguousarray(np.asarray(L0_attn_x, dtype=f32).reshape(2, 8192, D))
    shared = {
        "wk_t": np.ascontiguousarray(np.asarray(L0_attn_Wk, f32).T[:, QPERM]),
        "wv_t": np.ascontiguousarray(np.asarray(L0_attn_Wv, f32).T[:, QPERM]),
        "wo_t": np.ascontiguousarray(np.asarray(L0_attn_Wo, f32).T[QPERM, :]),
        "wg_t": np.ascontiguousarray(np.asarray(L0_ffn_w_gate, f32).T),
        "wu_t": np.ascontiguousarray(np.asarray(L0_ffn_w_up, f32).T),
        "wd_t": np.ascontiguousarray(np.asarray(L0_ffn_w_down, f32).T),
        "wr_t": np.ascontiguousarray(np.asarray(L0_ffn_w_router, f32).T),
        "gup": np.ascontiguousarray(np.asarray(L0_ffn_gate_up_proj, f32)),
        "dwn": np.ascontiguousarray(np.asarray(L0_ffn_down_proj, f32)),
        "bk_r": np.ascontiguousarray(np.asarray(L0_attn_bk, f32)[QPERM].reshape(1, D)),
        "bv_r": np.ascontiguousarray(np.asarray(L0_attn_bv, f32)[QPERM].reshape(1, D)),
        "bo_r": np.asarray(L0_attn_bo, f32).reshape(1, D),
        "lnb_fm": np.ascontiguousarray(np.asarray(L0_attn_ln_b, f32)[QPERM].reshape(8, 128).T),
        "lng_fm": np.ascontiguousarray(np.asarray(L0_attn_ln_g, f32)[QPERM].reshape(8, 128).T),
        "ident": np.eye(128, dtype=f32),
        "u_incl": np.triu(np.ones((128, 128), f32)),
        "u_strict": np.triu(np.ones((128, 128), f32), 1),
        "ones_col": np.ones((1, 128), f32),
        "ones128": np.ones((128, 1), f32),
        "iota_big": np.tile(np.arange(NE, dtype=f32)[None, :] + 1e4, (128, 1)),
        "iota_pln": np.tile(np.arange(NE, dtype=f32)[None, :], (128, 1)),
    }
    in_maps = []
    for c in range(8):
        b, q = c // 4, c % 4
        own = np.ascontiguousarray(x[b, q * T_OWN:(q + 1) * T_OWN])
        pre = np.zeros((T_PRE, D), f32)
        pre[: q * T_OWN] = x[b, : q * T_OWN]
        mask = np.zeros((128, NPT), f32)
        mask[:, : q * NT] = 1.0
        gpos = (q * T_OWN + np.arange(T_OWN, dtype=np.float64) + 1.0)
        rsq = (1.0 / np.sqrt(gpos)).astype(f32).reshape(NT, 128).T
        garbv = (float(ZROW) + np.arange(T_OWN, dtype=f32)).reshape(NT, 128).T
        m = dict(shared)
        m.update({
            "x_own": own,
            "x_pre": pre,
            "mask_pre": np.ascontiguousarray(mask),
            "rsq_pos": np.ascontiguousarray(rsq),
            "garb": np.ascontiguousarray(garbv),
        })
        in_maps.append(m)
    return in_maps


_NC_CACHE = {}


def run(debug=0, **inputs):
    in_maps = stage_inputs(**inputs)
    if debug not in _NC_CACHE:
        _NC_CACHE[debug] = build_program(debug=debug)
    return run_bass_kernel_spmd(_NC_CACHE[debug], in_maps, core_ids=list(range(8)))


def kernel(**inputs):
    res = run(debug=0, **inputs)
    outs = [np.asarray(r["out"]) for r in res.results]
    full = np.empty((2, 8192, D), np.float32)
    for c in range(8):
        b, q = c // 4, c % 4
        full[b, q * T_OWN:(q + 1) * T_OWN] = outs[c]
    return full
